# version 18
# baseline (speedup 1.0000x reference)
import numpy as np
import ml_dtypes
from contextlib import ExitStack

import concourse.bass as bass
import concourse.mybir as mybir
from concourse.bass_utils import run_bass_kernel_spmd

F32 = mybir.dt.float32
F32R = mybir.dt.float32r
BF16 = mybir.dt.bfloat16
AF = mybir.ActivationFunctionType
ALU = mybir.AluOpType
AX = mybir.AxisListType
BFD = ml_dtypes.bfloat16

NCORE = 8
B, L, D = 2, 8192, 1024
TPC = 2048
NH, DH = 16, 64
DFF = 4096
DIN = 2048
NSTATE = 8
DTR = 64
EPS = 1e-5

ENGS = ("sync", "gpsimd", "scalar", "vector", "tensor")


class P:

    def __init__(self, npsum=8):
        self.nc = bass.Bass("TRN2", target_bir_lowering=False)
        self.ctx = ExitStack()
        self.ops = {e: [] for e in ENGS}
        self.sems = {}
        self.cnt = {}
        self.chain_cnt = {e: 0 for e in ENGS}
        self.pp = [self.ps(f"pp{i}", [128, 512], F32) for i in range(npsum)]

    def sem(self, name):
        if name not in self.sems:
            self.sems[name] = self.ctx.enter_context(self.nc.semaphore(name))
            self.cnt[name] = 0
        return self.sems[name]

    def sb(self, name, shape, dt):
        return self.ctx.enter_context(self.nc.sbuf_tensor(name, shape, dt))

    def ps(self, name, shape, dt=F32):
        return self.ctx.enter_context(self.nc.psum_tensor(name, shape, dt))

    def din(self, name, shape, dt):
        return self.nc.dram_tensor(name, shape, dt, kind="ExternalInput").ap()

    def dout(self, name, shape, dt):
        return self.nc.dram_tensor(name, shape, dt, kind="ExternalOutput").ap()

    def on(self, eng, fn):
        self.ops[eng].append(fn)

    def wait(self, eng, name, val):
        if val <= 0:
            return
        s = self.sem(name)
        self.ops[eng].append(lambda e: e.wait_ge(s, val))

    def _emit(self, eng, build):
        cs = self.sem("chain_" + eng)
        prior = self.chain_cnt[eng]
        self.chain_cnt[eng] += 1

        def fn(e, prior=prior):
            e.wait_ge(cs, prior)
            build(e).then_inc(cs, 1)
        self.ops[eng].append(fn)
        return self.chain_cnt[eng]

    def v(self, build):
        return self._emit("vector", build)

    def s(self, build):
        return self._emit("scalar", build)

    def s_nc(self, build):
        cs = self.sem("chain_scalar")
        self.chain_cnt["scalar"] += 1

        def fn(e):
            build(e).then_inc(cs, 1)
        self.ops["scalar"].append(fn)
        return self.chain_cnt["scalar"]

    def g(self, build):
        return self._emit("gpsimd", build)

    def chain_mark(self, eng):
        return self.chain_cnt[eng]

    def wait_chain(self, eng, waiter_eng, val):
        if val <= 0:
            return
        cs = self.sem("chain_" + eng)
        self.ops[waiter_eng].append(lambda e: e.wait_ge(cs, val))

    def t(self, build):
        s = self.sem("mm")
        self.cnt["mm"] += 1
        val = self.cnt["mm"]

        def fn(e):
            last = build(e)
            (last if last is not None else e.nop()).then_inc(s, 1)
        self.ops["tensor"].append(fn)
        return val

    def dma(self, q, out, in_, sem_name=None, **kw):
        name = sem_name or ("dma_" + q)
        s = self.sem(name)
        self.cnt[name] += 16
        val = self.cnt[name]

        def fn(e):
            e.dma_start(out=out, in_=in_, **kw).then_inc(s, 16)
        self.ops[q].append(fn)
        return val

    def dma_mark(self, q, sem_name=None):
        name = sem_name or ("dma_" + q)
        self.sem(name)
        return self.cnt[name]

    def finalize(self):
        blk = self.ctx.enter_context(self.nc.Block())
        for eng in ENGS:
            fns = self.ops[eng]
            getattr(blk, eng)(lambda e, fns=fns: [f(e) for f in fns])
        self.ctx.close()
        return self.nc


class Banks:

    def __init__(self, p, tiles):
        self.p = p
        self.tiles = tiles
        self.guard = [None] * len(tiles)
        self.i = 0

    def next(self):
        i = self.i % len(self.tiles)
        self.i += 1
        if self.guard[i]:
            eng, val = self.guard[i]
            self.p.wait_chain(eng, "tensor", val)
        return i, self.tiles[i]

    def set_guard(self, i, eng, val):
        self.guard[i] = (eng, val)


class OutStream:

    def __init__(self, p, name, tile_shape, dt, nbuf=4, q="gpsimd"):
        self.p = p
        self.q = q
        self.nbuf = nbuf
        self.name = name
        self.bufs = [p.sb(f"{name}_st{i}", tile_shape, dt) for i in range(nbuf)]
        self.uses = [0] * nbuf
        self.k = 0

    def put(self, evac_build, dram_ap, part=None):
        p = self.p
        i = self.k % self.nbuf
        buf = self.bufs[i]
        sname = f"dst_{self.name}_{i}"
        p.wait("scalar", sname, self.uses[i] * 16)
        mark = p.s(lambda e, buf=buf: evac_build(e, buf))
        p.wait_chain("scalar", self.q, mark)
        src = buf[:] if part is None else buf[:, part]
        p.dma(self.q, dram_ap, src, sem_name=sname)
        self.uses[i] += 1
        self.k += 1

    def flush_wait(self, eng):
        for i in range(self.nbuf):
            self.p.wait(eng, f"dst_{self.name}_{i}", self.uses[i] * 16)


class InStream:

    def __init__(self, p, name, tile_shape, dt, nbuf=2, q="sync"):
        self.p = p
        self.q = q
        self.nbuf = nbuf
        self.name = name
        self.bufs = [p.sb(f"{name}_ld{i}", tile_shape, dt) for i in range(nbuf)]
        self.k = 0
        self.guard = [None] * nbuf

    def _sem(self, i):
        return f"lds_{self.name}_{i}"

    def load(self, src_ap, part=None):
        p = self.p
        i = self.k % self.nbuf
        if self.guard[i]:
            sname, val = self.guard[i]
            p.wait(self.q, sname, val)
        dst = self.bufs[i][:] if part is None else self.bufs[i][:, part]
        tok = p.dma(self.q, dst, src_ap, sem_name=self._sem(i))
        self.k += 1
        return i, self.bufs[i], (self._sem(i), tok)

    def extra(self, i, src_ap, part):
        tok = self.p.dma(self.q, self.bufs[i][:, part], src_ap,
                         sem_name=self._sem(i))
        return (self._sem(i), tok)

    def release(self, i, sem_name, val):
        self.guard[i] = (sem_name, val)


def _pm_feat(sl):
    T, Dp = sl.shape
    return np.ascontiguousarray(
        np.asarray(sl, dtype=np.float32).T.reshape(Dp // 128, 128, T).transpose(1, 0, 2))


def _wblkT(wT):
    I, O = wT.shape
    wT = np.asarray(wT, dtype=np.float32)
    return np.ascontiguousarray(wT.reshape(I // 128, 128, O // 128, 128).transpose(1, 2, 0, 3))


def _wblk(w):
    return _wblkT(np.asarray(w, dtype=np.float32).T)


def _wplainT(wT):
    I, O = wT.shape
    wT = np.asarray(wT, dtype=np.float32)
    return np.ascontiguousarray(wT.reshape(I // 128, 128, O).transpose(1, 0, 2))


def _wplain(w):
    return _wplainT(np.asarray(w, dtype=np.float32).T)


def _pcol(vec):
    v = np.asarray(vec, dtype=np.float32)
    C = v.shape[0] // 128
    return np.ascontiguousarray(v.reshape(C, 128).T)


def _prow(vec):
    return np.ascontiguousarray(np.asarray(vec, dtype=np.float32).reshape(1, -1))


def flat(ap):
    if ap.ndim == 2:
        return ap
    if ap.ndim == 3:
        return ap.rearrange("p a b -> p (a b)")
    if ap.ndim == 4:
        return ap.rearrange("p a b c -> p (a b c)")
    raise ValueError(ap.shape)


def build_l0attn_pre():
    p = P()
    x_in = p.din("x_in", [128, 8, TPC], F32)
    wq = p.din("wq", [128, 8, 8, 128], F32)
    wk = p.din("wk", [128, 8, 8, 128], F32)
    wkp = p.din("wkp", [128, 8, D], F32)
    wvp = p.din("wvp", [128, 8, D], F32)
    bq = p.din("bq", [128, 8], F32)
    bk = p.din("bk", [128, 8], F32)
    bk_row = p.din("bk_row", [1, D], F32)
    bv_row = p.din("bv_row", [1, D], F32)

    o_qf = p.dout("o_qf", [128, 16, 8, 128], BF16)
    o_kf = p.dout("o_kf", [128, 16, 8, 128], BF16)
    o_kt = p.dout("o_kt", [128, 16, D], BF16)
    o_vt = p.dout("o_vt", [128, 16, D], BF16)
    o_st = p.dout("o_st", [64, NH, DH], F32)

    x_bf = p.sb("x_bf", [128, 8 * TPC], BF16)
    wq_sb = p.sb("wq_sb", [128, 64 * 128], BF16)
    wk_sb = p.sb("wk_sb", [128, 64 * 128], BF16)
    wkp_sb = p.sb("wkp_sb", [128, 8 * D], BF16)
    wvp_sb = p.sb("wvp_sb", [128, 8 * D], BF16)
    bq_sb = p.sb("bq_sb", [128, 8], F32)
    bk_sb = p.sb("bk_sb", [128, 8], F32)
    bkr_sb = p.sb("bkr_sb", [1, D], BF16)
    bvr_sb = p.sb("bvr_sb", [1, D], BF16)
    ones_sb = p.sb("ones_sb", [1, 128], BF16)
    kt_sb = p.sb("kt_sb", [128, 16 * D], BF16)
    vt_sb = p.sb("vt_sb", [128, 16 * D], BF16)
    st_sb = p.sb("st_sb", [64, NH * DH], F32)

    lv = 0
    for dst, src in ((x_bf[:], x_in), (wq_sb[:], wq), (wk_sb[:], wk),
                     (wkp_sb[:], wkp), (wvp_sb[:], wvp),
                     (bkr_sb[:], bk_row), (bvr_sb[:], bv_row)):
        lv = p.dma("gpsimd", dst, flat(src))
    ls = p.dma("sync", bq_sb[:], bq, sem_name="lds")
    ls = p.dma("sync", bk_sb[:], bk, sem_name="lds")
    p.v(lambda e: e.memset(ones_sb[:], 1.0))
    ones_mark = p.chain_mark("vector")

    p.wait("tensor", "dma_gpsimd", lv)
    p.wait("tensor", "lds", ls)
    p.wait_chain("vector", "tensor", ones_mark)
    p.wait("scalar", "lds", ls)

    banks = Banks(p, [pb[:, 0:512] for pb in p.pp[0:4]])
    qf_out = OutStream(p, "qf", [128, 512], BF16)
    kf_out = OutStream(p, "kf", [128, 512], BF16)

    for w_sb, bias_col, ostream, oname in ((wq_sb, bq_sb, qf_out, o_qf),
                                           (wk_sb, bk_sb, kf_out, o_kf)):
        for oc in range(8):
            for tt in range(4):
                bi, pb = banks.next()

                def mm(e, oc=oc, tt=tt, w_sb=w_sb, pb=pb):
                    last = None
                    for ic in range(8):
                        last = e.matmul(pb, w_sb[:, (oc * 8 + ic) * 128:(oc * 8 + ic + 1) * 128],
                                 x_bf[:, ic * TPC + tt * 512: ic * TPC + (tt + 1) * 512],
                                 start=(ic == 0), stop=(ic == 7))
                    return last
                tok = p.t(mm)
                p.wait("scalar", "mm", tok)

                def ev(e, buf, pb=pb, bias_col=bias_col, oc=oc):
                    return e.activation(out=buf[:], in_=pb, func=AF.Identity,
                                        bias=bias_col[:, oc:oc + 1])
                ostream.put(ev, oname[:, tt * 4:(tt + 1) * 4, oc, :])
                banks.set_guard(bi, "scalar", p.chain_mark("scalar"))

    for w_sb, out_sb, brow, oname in ((wkp_sb, kt_sb, bkr_sb, o_kt),
                                      (wvp_sb, vt_sb, bvr_sb, o_vt)):
        for tc in range(16):
            for oh in range(2):
                bi, pb = banks.next()

                def mm(e, tc=tc, oh=oh, w_sb=w_sb, brow=brow, pb=pb):
                    for ic in range(8):
                        e.matmul(pb, x_bf[:, ic * TPC + tc * 128: ic * TPC + (tc + 1) * 128],
                                 w_sb[:, ic * D + oh * 512: ic * D + (oh + 1) * 512],
                                 start=(ic == 0), stop=False)
                    return e.matmul(pb, ones_sb[:], brow[:, oh * 512:(oh + 1) * 512],
                             start=False, stop=True)
                tok = p.t(mm)
                p.wait("scalar", "mm", tok)
                mark = p.s(lambda e, tc=tc, oh=oh, out_sb=out_sb, pb=pb:
                           e.copy(out=out_sb[:, tc * D + oh * 512: tc * D + (oh + 1) * 512],
                                  in_=pb))
                banks.set_guard(bi, "scalar", mark)
        mark = p.chain_mark("scalar")
        p.wait_chain("scalar", "gpsimd", mark)
        p.dma("gpsimd", flat(oname), out_sb[:], sem_name="dst_kv")

    kv_mark = p.chain_mark("scalar")
    p.wait_chain("scalar", "tensor", kv_mark)
    pS = [p.pp[4][0:64, 0:512], p.pp[5][0:64, 0:512]]

    def smm(e):
        last = None
        for h in range(NH):
            pb = pS[h // 8]
            off = (h % 8) * DH
            for c in range(16):
                last = e.matmul(pb[:, off:off + DH],
                         kt_sb[:, c * D + h * DH: c * D + (h + 1) * DH],
                         vt_sb[:, c * D + h * DH: c * D + (h + 1) * DH],
                         start=(c == 0), stop=(c == 15))
        return last
    tok = p.t(smm)
    p.wait("scalar", "mm", tok)
    p.s(lambda e: e.copy(out=st_sb[:, 0:512], in_=pS[0]))
    p.s(lambda e: e.copy(out=st_sb[:, 512:1024], in_=pS[1]))
    mark = p.chain_mark("scalar")
    p.wait_chain("scalar", "gpsimd", mark)
    last = p.dma("gpsimd", flat(o_st), st_sb[:], sem_name="dst_kv")
    qf_out.flush_wait("gpsimd")
    kf_out.flush_wait("gpsimd")
    p.wait("gpsimd", "dst_kv", last)
    return p.finalize()


def build_l0attn_main():
    p = P()
    x_in = p.din("x_in", [128, 8, TPC], F32)
    qf = p.din("qf", [128, 16, 8, 128], BF16)
    kf = p.din("kf", [128, 16, 8, 128], BF16)
    kt = p.din("kt", [128, 16, D], BF16)
    vt = p.din("vt", [128, 16, D], BF16)
    s_pred = p.din("s_pred", [128, 3, NH, DH], F32)
    wo = p.din("wo", [128, 8, 8, 128], F32)
    bo = p.din("bo", [128, 8], F32)
    trimask = p.din("trimask", [128, 128], F32)
    ident = p.din("ident", [64, 64], F32)
    ones2 = p.din("ones2", [128, 128], F32)

    o_x1 = p.dout("o_x1", [128, 8, TPC], F32)
    o_x1bf = p.dout("o_x1bf", [128, 8, TPC], BF16)

    spred_sb = p.sb("spred_sb", [128, 3 * NH * DH], F32)
    wo_sb = p.sb("wo_sb", [128, 64 * 128], BF16)
    bo_sb = p.sb("bo_sb", [128, 8], F32)
    tri_sb = p.sb("tri_sb", [128, 128], F32)
    id_sb = p.sb("id_sb", [64, 64], BF16)
    sacc = [p.sb(f"sacc{i}", [128, NH * DH], F32) for i in range(2)]
    scur_sb = p.sb("scur_sb", [128, NH * DH], BF16)
    masked_sb = p.sb("masked_sb", [128, 2 * 128], BF16)
    attn_sb = p.sb("attn_sb", [128, 8 * TPC], BF16)
    r1_sb = p.sb("r1_sb", [128, 8 * TPC], F32)
    xs = InStream(p, "xs", [128, 512], F32, nbuf=2, q="sync")
    qks = InStream(p, "qks", [128, 2 * 8 * 128], BF16, nbuf=2, q="sync")
    kvs = InStream(p, "kvs", [128, 2 * D], BF16, nbuf=2, q="gpsimd")

    ln = dict(
        ones_col_r=p.sb("onescol", [128, 1], F32R),
        ones_row_r=p.sb("onesrow", [1, 128], F32R),
        ones_col_f=p.sb("onescolf", [128, 1], F32),
        sq_r=p.sb("sq_r", [128, 2 * 512], F32R),
        rows=p.sb("rows", [1, 4 * 512], F32),
        rrow=p.sb("rrow", [1, 2 * 512], F32R),
        tmp32=p.sb("tmp32", [128, 512], F32),
    )
    x1bf_out = OutStream(p, "x1bf", [128, 512], BF16)
    x1_out = OutStream(p, "x1f", [128, 512], F32)

    lv = 0
    for dst, src in ((spred_sb[:], s_pred), (wo_sb[:], wo)):
        lv = p.dma("gpsimd", dst, flat(src))
    lv = p.dma("gpsimd", ln["ones_col_r"][:], ones2[:, 0:1],
               allow_slow_non_contiguous=True)
    lv = p.dma("gpsimd", ln["ones_row_r"][:], ones2[0:1, :])
    lv = p.dma("gpsimd", id_sb[:], ident)
    ls = p.dma("sync", bo_sb[:], bo, sem_name="lds")
    ls = p.dma("sync", tri_sb[:], trimask, sem_name="lds")
    ls = p.dma("sync", ln["ones_col_f"][:], ones2[:, 0:1], sem_name="lds",
               allow_slow_non_contiguous=True)
    for eng in ("tensor", "vector", "scalar"):
        p.wait(eng, "dma_gpsimd", lv)
        p.wait(eng, "lds", ls)

    p.v(lambda e: e.tensor_add(sacc[0][:], spred_sb[:, 0:NH * DH],
                               spred_sb[:, NH * DH:2 * NH * DH]))
    p.v(lambda e: e.tensor_add(sacc[1][:], sacc[0][:],
                               spred_sb[:, 2 * NH * DH:3 * NH * DH]))
    vmark = p.chain_mark("vector")
    p.wait_chain("vector", "scalar", vmark)
    p.s(lambda e: e.copy(out=scur_sb[:], in_=sacc[1][:]))
    smark = p.chain_mark("scalar")
    p.wait_chain("scalar", "tensor", smark)

    pSc = [p.pp[0][:, 0:128], p.pp[1][:, 0:128]]
    pOut = [p.pp[2][:, 0:128], p.pp[3][:, 0:128]]
    pS = [p.pp[4][:, 0:512], p.pp[5][:, 0:512]]
    S_sb = sacc[1]

    sc_guard = [0, 0]
    out_guard = [0, 0]
    for c in range(16):
        qi, qkbuf, _ = qks.load(
            qf[:, c:c + 1, :, :].rearrange("p a f t -> p (a f t)"),
            part=slice(0, 8 * 128))
        qsem, qktok = qks.extra(qi, kf[:, c:c + 1, :, :].rearrange("p a f t -> p (a f t)"),
                                part=slice(8 * 128, 2 * 8 * 128))
        ki, kvbuf, _ = kvs.load(kt[:, c:c + 1, :].rearrange("p a f -> p (a f)"),
                                part=slice(0, D))
        ksem, kvtok = kvs.extra(ki, vt[:, c:c + 1, :].rearrange("p a f -> p (a f)"),
                                part=slice(D, 2 * D))
        p.wait("tensor", qsem, qktok)
        p.wait("tensor", ksem, kvtok)

        for hp in range(8):
            fc = hp
            for j in range(2):
                p.wait_chain("vector", "tensor", sc_guard[j])

            def mm_pair(e, c=c, fc=fc, qkbuf=qkbuf):
                ksl = qkbuf[:, 8 * 128 + fc * 128:8 * 128 + (fc + 1) * 128]
                qsl = qkbuf[:, fc * 128:(fc + 1) * 128]
                last = None
                for j in range(2):
                    last = e.matmul(pSc[j], ksl[j * 64:j * 64 + 64, :], qsl[j * 64:j * 64 + 64, :],
                             start=True, stop=True)
                return last
            tok = p.t(mm_pair)
            p.wait("vector", "mm", tok)
            for j in range(2):
                p.v(lambda e, j=j: e.tensor_tensor(
                    out=masked_sb[:, j * 128:(j + 1) * 128], in0=pSc[j],
                    in1=tri_sb[:], op=ALU.mult))
            vmark = p.chain_mark("vector")
            sc_guard = [vmark, vmark]
            p.wait_chain("vector", "tensor", vmark)
            p.wait_chain("scalar", "tensor", out_guard[c % 2])

            def mm_out(e, c=c, hp=hp, fc=fc, qkbuf=qkbuf, kvbuf=kvbuf):
                pb = pOut[c % 2]
                qsl = qkbuf[:, fc * 128:(fc + 1) * 128]
                last = None
                for j in range(2):
                    h = 2 * hp + j
                    e.matmul(pb[j * 64:j * 64 + 64, :],
                             kvbuf[:, D + h * DH:D + (h + 1) * DH],
                             masked_sb[:, j * 128:(j + 1) * 128],
                             start=True, stop=False, tile_position=(0, j * 64))
                    last = e.matmul(pb[j * 64:j * 64 + 64, :],
                             scur_sb[j * 64:j * 64 + 64, h * DH:(h + 1) * DH],
                             qsl[j * 64:j * 64 + 64, :],
                             start=False, stop=True, tile_position=(j * 64, j * 64))
                return last
            tok = p.t(mm_out)
            p.wait("scalar", "mm", tok)
            mark = p.s(lambda e, c=c, fc=fc: e.copy(
                out=attn_sb[:, fc * TPC + c * 128: fc * TPC + (c + 1) * 128],
                in_=pOut[c % 2]))
            out_guard[c % 2] = mark

        if c < 15:
            def mm_supd(e, c=c, kvbuf=kvbuf):
                last = None
                for h in range(NH):
                    pb = pS[h // 8]
                    off = (h % 8) * DH
                    e.matmul(pb[0:64, off:off + DH],
                             kvbuf[:, h * DH:(h + 1) * DH],
                             kvbuf[:, D + h * DH:D + (h + 1) * DH],
                             start=True, stop=True, tile_position=(0, 0))
                    last = e.matmul(pb[64:128, off:off + DH],
                             kvbuf[:, h * DH:(h + 1) * DH],
                             kvbuf[:, D + h * DH:D + (h + 1) * DH],
                             start=True, stop=True, tile_position=(0, 64))
                return last
            tok = p.t(mm_supd)
            qks.release(qi, "mm", tok)
            kvs.release(ki, "mm", tok)
            p.wait("vector", "mm", tok)
            p.v(lambda e: e.tensor_add(S_sb[:, 0:512], S_sb[:, 0:512], pS[0]))
            mv = p.v(lambda e: e.tensor_add(S_sb[:, 512:1024], S_sb[:, 512:1024], pS[1]))
            p.wait_chain("vector", "scalar", mv)
            p.wait_chain("vector", "tensor", mv)
            mark = p.s(lambda e: e.copy(out=scur_sb[:], in_=S_sb[:]))
            p.wait_chain("scalar", "tensor", mark)
        else:
            mmlast = p.cnt["mm"]
            qks.release(qi, "mm", mmlast)
            kvs.release(ki, "mm", mmlast)

    attn_mark = p.chain_mark("scalar")
    p.wait_chain("scalar", "tensor", attn_mark)
    banks = Banks(p, [pb[:, 0:512] for pb in p.pp[6:8]])

    for oc in range(8):
        for tt in range(4):
            sl = slice(oc * TPC + tt * 512, oc * TPC + (tt + 1) * 512)
            xi, xbuf, (xsem, xtok) = xs.load(flat(x_in)[:, sl])
            bi, pb = banks.next()

            def mm(e, oc=oc, tt=tt, pb=pb):
                last = None
                for ic in range(8):
                    last = e.matmul(pb, wo_sb[:, (oc * 8 + ic) * 128:(oc * 8 + ic + 1) * 128],
                             attn_sb[:, ic * TPC + tt * 512: ic * TPC + (tt + 1) * 512],
                             start=(ic == 0), stop=(ic == 7))
                return last
            tok = p.t(mm)
            p.wait("scalar", "mm", tok)
            m1 = p.s(lambda e, pb=pb, oc=oc, sl=sl: e.activation(
                out=r1_sb[:, sl], in_=pb, func=AF.Identity, bias=bo_sb[:, oc:oc + 1]))
            p.wait_chain("scalar", "vector", m1)
            p.wait("vector", xsem, xtok)
            mv = p.v(lambda e, sl=sl, xbuf=xbuf: e.tensor_tensor(
                out=r1_sb[:, sl], in0=r1_sb[:, sl], in1=xbuf[:], op=ALU.add))
            xs.release(xi, "chain_vector", mv)
            banks.set_guard(bi, "scalar", m1)

    vres = p.chain_mark("vector")
    p.wait_chain("vector", "scalar", vres)
    p.wait_chain("vector", "tensor", vres)

    emit_layernorm(p, r1_sb, ln, o_x1, o_x1bf, x1_out, x1bf_out)
    x1_out.flush_wait("gpsimd")
    x1bf_out.flush_wait("gpsimd")
    return p.finalize()


def emit_layernorm(p, r_sb, ln, o_f32, o_bf16, f32_out, bf16_out, nchunk=8,
                   tpc=TPC, pbanks=(0, 1, 2, 3), out_fn=None, ptiles=None,
                   tts=None):
    ntt = tpc // 512
    nfeat = nchunk * 128
    if ptiles is None:
        ptiles = [p.pp[i][:] for i in pbanks]
    pSum = ptiles[0][0:1, 0:512]
    pSq = ptiles[1][0:1, 0:512]
    pA = ptiles[2][:, 0:512]
    pB = ptiles[3][:, 0:512]
    rows, rrow, sq_r, tmp32 = ln["rows"], ln["rrow"], ln["sq_r"], ln["tmp32"]

    for tt in (range(ntt) if tts is None else tts):
        ts = slice(tt * 512, (tt + 1) * 512)
        sq_tok = [0, 0]
        for c in range(nchunk):
            sl = slice(c * tpc + tt * 512, c * tpc + (tt + 1) * 512)
            jb = c % 2
            p.wait("scalar", "mm", sq_tok[jb])
            smark = p.s(lambda e, c=c, sl=sl, jb=jb: e.activation(
                out=sq_r[:, jb * 512:(jb + 1) * 512], in_=r_sb[:, sl], func=AF.Square))
            p.wait_chain("scalar", "tensor", smark)
            sq_tok[jb] = p.t(lambda e, c=c, jb=jb: e.matmul(
                pSq, ln["ones_col_r"][:], sq_r[:, jb * 512:(jb + 1) * 512],
                start=(c == 0), stop=(c == nchunk - 1)))

        def statmm(e, tt=tt):
            last = None
            for c in range(nchunk):
                sl = slice(c * tpc + tt * 512, c * tpc + (tt + 1) * 512)
                last = e.matmul(pSum, ln["ones_col_f"][:], r_sb[:, sl],
                         start=(c == 0), stop=(c == nchunk - 1))
            return last
        tok = p.t(statmm)
        p.wait("vector", "mm", tok)
        r0 = slice(0, 512)
        r1s = slice(512, 1024)
        r2s = slice(1024, 1536)
        a0 = slice(0, 512)
        b0 = slice(512, 1024)
        p.v(lambda e: e.tensor_scalar_mul(rows[0:1, r0], pSum, 1.0 / nfeat))
        p.v(lambda e: e.tensor_scalar_mul(rows[0:1, r1s], pSq, 1.0 / nfeat))
        p.v(lambda e: e.tensor_tensor(out=rows[0:1, r2s], in0=rows[0:1, r0],
                                      in1=rows[0:1, r0], op=ALU.mult))
        p.v(lambda e: e.tensor_tensor(out=rows[0:1, r1s], in0=rows[0:1, r1s],
                                      in1=rows[0:1, r2s], op=ALU.subtract))
        vmark = p.chain_mark("vector")
        p.wait_chain("vector", "scalar", vmark)
        p.s(lambda e: e.activation(out=rows[0:1, r2s], in_=rows[0:1, r1s],
                                   func=AF.Ln, bias=np.float32(EPS)))
        p.s(lambda e: e.activation(out=rows[0:1, r1s], in_=rows[0:1, r2s],
                                   func=AF.Exp, scale=np.float32(-0.5)))
        smark = p.chain_mark("scalar")
        p.wait_chain("scalar", "vector", smark)
        p.v(lambda e: e.tensor_tensor(out=rows[0:1, r0], in0=rows[0:1, r0],
                                      in1=rows[0:1, r1s], op=ALU.mult))
        p.v(lambda e: e.tensor_scalar_mul(rows[0:1, r0], rows[0:1, r0], -1.0))
        vmark = p.chain_mark("vector")
        p.wait_chain("vector", "scalar", vmark)
        p.s(lambda e: e.copy(out=rrow[0:1, a0], in_=rows[0:1, r1s]))
        mark = p.s(lambda e: e.copy(out=rrow[0:1, b0], in_=rows[0:1, r0]))
        p.wait_chain("scalar", "tensor", mark)

        def bc(e, a0=a0, b0=b0):
            e.matmul(pA, ln["ones_row_r"][:], rrow[0:1, a0], start=True, stop=True)
            return e.matmul(pB, ln["ones_row_r"][:], rrow[0:1, b0], start=True, stop=True)
        tok = p.t(bc)
        p.wait("vector", "mm", tok)
        for c in range(nchunk):
            sl = slice(c * tpc + tt * 512, c * tpc + (tt + 1) * 512)
            p.v(lambda e, sl=sl: e.tensor_tensor(out=tmp32[:], in0=r_sb[:, sl],
                                                 in1=pA, op=ALU.mult))
            mv = p.v(lambda e, sl=sl: e.tensor_tensor(out=r_sb[:, sl], in0=tmp32[:],
                                                      in1=pB, op=ALU.add))
            p.wait_chain("vector", "scalar", mv)
            if out_fn is None:
                f32_ap, bf16_ap = flat(o_f32)[:, sl], flat(o_bf16)[:, sl]
            else:
                f32_ap, bf16_ap = out_fn(c, tt)
            if f32_out is not None:
                f32_out.put(lambda e, buf, sl=sl: e.copy(out=buf[:], in_=r_sb[:, sl]),
                            f32_ap)
            bf16_out.put(lambda e, buf, sl=sl: e.copy(out=buf[:], in_=r_sb[:, sl]),
                         bf16_ap)
        p.wait_chain("vector", "tensor", p.chain_mark("vector"))


def build_l0ffn():
    p = P()
    x1 = p.din("x1", [128, 8, TPC], F32)
    x1bf = p.din("x1bf", [128, 8, TPC], BF16)
    w1 = p.din("w1", [128, 32, 8, 128], F32)
    w2 = p.din("w2", [128, 32, 8, 128], F32)
    h1 = p.din("h1", [128, 32, 32, 128], F32)
    h2 = p.din("h2", [128, 32, 32, 128], F32)
    w3 = p.din("w3", [128, 8, 32, 128], F32)
    s1 = p.din("s1", [128, 32], F32)
    s2 = p.din("s2", [128, 32], F32)
    h1s = p.din("h1s", [128, 32], F32)
    h2s = p.din("h2s", [128, 32], F32)
    s3 = p.din("s3", [128, 8], F32)
    ones2 = p.din("ones2", [128, 128], F32)

    o_x2 = p.dout("o_x2", [128, 8, TPC], F32)
    o_x2bf = p.dout("o_x2bf", [128, 8, TPC], BF16)

    HTOK = 1024

    xbf_sb = p.sb("xbf_sb", [128, 8 * HTOK], BF16)
    h_sb = p.sb("h_sb", [128, 32 * HTOK], BF16)
    g_sb = p.sb("g_sb", [128, 32 * HTOK], BF16)
    r2_sb = p.sb("r2_sb", [128, 8 * 512], F32)
    tmp1 = p.sb("tmp1", [128, 512], BF16)
    tmp2 = p.sb("tmp2", [128, 512], BF16)
    sc_sb = p.sb("sc_sb", [128, 32 * 4 + 8], F32)
    ws = InStream(p, "ws", [128, 32 * 128], BF16, nbuf=2, q="gpsimd")
    x1s = InStream(p, "x1s", [128, 512], F32, nbuf=2, q="sync")
    ln = dict(
        ones_col_r=p.sb("onescol", [128, 1], F32R),
        ones_row_r=p.sb("onesrow", [1, 128], F32R),
        ones_col_f=p.sb("onescolf", [128, 1], F32),
        sq_r=p.sb("sq_r", [128, 2 * 512], F32R),
        rows=p.sb("rows", [1, 4 * 512], F32),
        rrow=p.sb("rrow", [1, 2 * 512], F32R),
        tmp32=p.sb("tmp32", [128, 512], F32),
    )
    x2_out = OutStream(p, "x2f", [128, 512], F32, nbuf=2, q="scalar")
    x2bf_out = OutStream(p, "x2bf", [128, 512], BF16, nbuf=2, q="scalar")

    lv = p.dma("gpsimd", ln["ones_col_r"][:], ones2[:, 0:1],
               allow_slow_non_contiguous=True)
    lv = p.dma("gpsimd", ln["ones_row_r"][:], ones2[0:1, :])
    ls = p.dma("sync", sc_sb[:, 0:32], s1, sem_name="lds")
    ls = p.dma("sync", sc_sb[:, 32:64], s2, sem_name="lds")
    ls = p.dma("sync", sc_sb[:, 64:96], h1s, sem_name="lds")
    ls = p.dma("sync", sc_sb[:, 96:128], h2s, sem_name="lds")
    ls = p.dma("sync", sc_sb[:, 128:136], s3, sem_name="lds")
    ls = p.dma("sync", ln["ones_col_f"][:], ones2[:, 0:1], sem_name="lds",
               allow_slow_non_contiguous=True)
    for eng in ("tensor", "vector", "scalar"):
        p.wait(eng, "dma_gpsimd", lv)
        p.wait(eng, "lds", ls)

    banks = Banks(p, [pb[:, 0:512] for pb in p.pp[0:4]])
    xbf_uses = 0
    tokA_prev = 0
    for H in range(2):
        if H > 0:
            p.wait("gpsimd", "mm", tokA_prev)
        htok = slice(H * HTOK, (H + 1) * HTOK)
        lvh = p.dma("gpsimd", xbf_sb[:].rearrange("p (c t) -> p c t", c=8),
                    x1bf[:, :, htok], sem_name="ldxh")
        p.wait("tensor", "ldxh", lvh)

        for oc in range(32):
            wi1, wb1, _ = ws.load(
                w1[:, oc, :, :].rearrange("p a b -> p (a b)"), part=slice(0, 1024))
            wsem1, wtok1 = ws.extra(
                wi1, w2[:, oc, :, :].rearrange("p a b -> p (a b)"),
                part=slice(1024, 2048))
            for tt in range(2):
                bi1, pb1 = banks.next()
                p.wait("tensor", wsem1, wtok1)

                def mm1(e, oc=oc, tt=tt, pb=pb1, wb=wb1):
                    last = None
                    for ic in range(8):
                        last = e.matmul(pb, wb[:, ic * 128:(ic + 1) * 128],
                                        xbf_sb[:, ic * HTOK + tt * 512: ic * HTOK + (tt + 1) * 512],
                                        start=(ic == 0), stop=(ic == 7))
                    return last
                tok1 = p.t(mm1)
                bi2, pb2 = banks.next()

                def mm2(e, oc=oc, tt=tt, pb=pb2, wb=wb1):
                    last = None
                    for ic in range(8):
                        last = e.matmul(pb, wb[:, 1024 + ic * 128:1024 + (ic + 1) * 128],
                                        xbf_sb[:, ic * HTOK + tt * 512: ic * HTOK + (tt + 1) * 512],
                                        start=(ic == 0), stop=(ic == 7))
                    return last
                tok2 = p.t(mm2)
                p.wait("scalar", "mm", tok2)
                m1 = p.s(lambda e, pb=pb1, oc=oc: e.activation(
                    out=tmp1[:], in_=pb, func=AF.Silu, scale=sc_sb[:, oc:oc + 1]))
                m2 = p.s(lambda e, pb=pb2, oc=oc: e.activation(
                    out=tmp2[:], in_=pb, func=AF.Identity, scale=sc_sb[:, 32 + oc:33 + oc]))
                p.wait_chain("scalar", "vector", m2)
                mv = p.v(lambda e, oc=oc, tt=tt: e.tensor_mul(
                    h_sb[:, oc * HTOK + tt * 512: oc * HTOK + (tt + 1) * 512],
                    tmp1[:], tmp2[:]))
                p.wait_chain("vector", "scalar", mv)
                banks.set_guard(bi1, "scalar", m1)
                banks.set_guard(bi2, "scalar", m2)
            ws.release(wi1, "mm", tok2)
        tokA_prev = p.cnt["mm"]
        hmark = p.chain_mark("vector")
        p.wait_chain("vector", "tensor", hmark)

        for phase, (wt, ssoff, src, dst) in enumerate(
                ((h1, 64, h_sb, g_sb), (h2, 96, g_sb, h_sb))):
            for oc in range(32):
                wi, wb, (wsem, wtok) = ws.load(wt[:, oc, :, :].rearrange("p a b -> p (a b)"))
                for tt in range(2):
                    bi, pb = banks.next()
                    p.wait("tensor", wsem, wtok)

                    def mmh(e, oc=oc, tt=tt, pb=pb, wb=wb, src=src):
                        last = None
                        for ic in range(32):
                            last = e.matmul(pb, wb[:, ic * 128:(ic + 1) * 128],
                                            src[:, ic * HTOK + tt * 512: ic * HTOK + (tt + 1) * 512],
                                            start=(ic == 0), stop=(ic == 31))
                        return last
                    tok = p.t(mmh)
                    p.wait("scalar", "mm", tok)
                    mk = p.s(lambda e, pb=pb, oc=oc, tt=tt, dst=dst, ssoff=ssoff: e.activation(
                        out=dst[:, oc * HTOK + tt * 512: oc * HTOK + (tt + 1) * 512],
                        in_=pb, func=AF.Gelu, scale=sc_sb[:, ssoff + oc:ssoff + oc + 1]))
                    banks.set_guard(bi, "scalar", mk)
                ws.release(wi, "mm", tok)
            fmark = p.chain_mark("scalar")
            p.wait_chain("scalar", "tensor", fmark)

        for tt in range(2):
            gtok = H * HTOK + tt * 512
            for oc in range(8):
                wi3, wb3, (wsem3, wtok3) = ws.load(
                    w3[:, oc, :, :].rearrange("p a b -> p (a b)"))
                bi, pb = banks.next()
                p.wait("tensor", wsem3, wtok3)

                def mmo(e, oc=oc, tt=tt, pb=pb, wb=wb3):
                    last = None
                    for ic in range(32):
                        last = e.matmul(pb, wb[:, ic * 128:(ic + 1) * 128],
                                        h_sb[:, ic * HTOK + tt * 512: ic * HTOK + (tt + 1) * 512],
                                        start=(ic == 0), stop=(ic == 31))
                    return last
                tok = p.t(mmo)
                ws.release(wi3, "mm", tok)
                sl = slice(oc * 512, (oc + 1) * 512)
                xi, xbuf, (xsem, xtok) = x1s.load(
                    flat(x1)[:, oc * TPC + gtok: oc * TPC + gtok + 512])
                p.wait("scalar", "mm", tok)
                mk = p.s(lambda e, pb=pb, oc=oc, sl=sl: e.activation(
                    out=r2_sb[:, sl], in_=pb, func=AF.Identity,
                    scale=sc_sb[:, 128 + oc:129 + oc]))
                p.wait_chain("scalar", "vector", mk)
                p.wait("vector", xsem, xtok)
                mv = p.v(lambda e, sl=sl, xbuf=xbuf: e.tensor_tensor(
                    out=r2_sb[:, sl], in0=r2_sb[:, sl], in1=xbuf[:], op=ALU.add))
                x1s.release(xi, "chain_vector", mv)
                banks.set_guard(bi, "scalar", mk)
            vmk = p.chain_mark("vector")
            p.wait_chain("vector", "scalar", vmk)
            p.wait_chain("vector", "tensor", vmk)

            def out_fn(c, _tt, gtok=gtok):
                ap = flat(o_x2)[:, c * TPC + gtok: c * TPC + gtok + 512]
                apb = flat(o_x2bf)[:, c * TPC + gtok: c * TPC + gtok + 512]
                return ap, apb
            emit_layernorm(p, r2_sb, ln, None, None, x2_out, x2bf_out,
                           nchunk=8, tpc=512, pbanks=(4, 5, 6, 7), out_fn=out_fn)
    x2_out.flush_wait("gpsimd")
    x2bf_out.flush_wait("gpsimd")
    return p.finalize()


def build_l1front():
    p = P()
    x2bf = p.din("x2bf", [128, 8, TPC], BF16)
    xprev = p.din("xprev", [128, 8, 3], BF16)
    w_in = p.din("w_in", [128, 32, 8, 128], F32)
    dwk = p.din("dwk", [128, 4, 16], F32)
    pw = p.din("pw", [128, 16, 16, 128], F32)
    cbias = p.din("cbias", [128, 16], F32)
    w_x = p.din("w_x", [128, 16, 80], F32)
    w_dt = p.din("w_dt", [64, DIN], F32)
    bdt = p.din("bdt", [128, 16], F32)
    ident = p.din("ident", [128, 128], F32)

    o_z = p.dout("o_z", [128, 16, TPC], BF16)
    o_xc = p.dout("o_xc", [128, 16, TPC], BF16)
    o_delta = p.dout("o_delta", [128, 16, TPC], F32)
    o_bc = p.dout("o_bc", [16, TPC], F32)
    xin_scr = p.nc.dram_tensor("xin_scr", [128, 16, 3 + TPC], BF16, kind="Internal").ap()

    x2_sb = p.sb("x2_sb", [128, 8 * TPC], BF16)
    xp_sb = p.sb("xp_sb", [128, 8 * 3], BF16)
    conv_sb = p.sb("conv_sb", [128, 16 * TPC], BF16)
    xdbl_sb = p.sb("xdbl_sb", [80, TPC], BF16)
    wx_sb = p.sb("wx_sb", [128, 16 * 80], BF16)
    wdt_sb = p.sb("wdt_sb", [64, DIN], BF16)
    dwk_sb = p.sb("dwk_sb", [128, 4 * 16], F32)
    cb_sb = p.sb("cb_sb", [128, 16], F32)
    bdt_sb = p.sb("bdt_sb", [128, 16], F32)
    id_sb = p.sb("id_sb", [128, 128], F32)
    diag_sb = [p.sb(f"diag_sb{i}", [128, 4 * 128], BF16) for i in range(2)]
    xct = [p.sb(f"xct{i}", [128, 512], BF16) for i in range(2)]
    spa = p.sb("spa", [128, 2048], F32)
    spb = p.sb("spb", [128, 2048], F32)
    spc = p.sb("spc", [128, 2048], F32)
    ws = InStream(p, "ws", [128, 16 * 128], BF16, nbuf=2, q="gpsimd")
    xins = InStream(p, "xins", [128, 3 + TPC], BF16, nbuf=2, q="sync")
    z_out = OutStream(p, "z", [128, 512], BF16, nbuf=2, q="scalar")
    xin_out = OutStream(p, "xin", [128, 512], BF16, nbuf=2, q="scalar")
    xc_out = OutStream(p, "xc", [128, 512], BF16, nbuf=2, q="scalar")

    lv = p.dma("gpsimd", x2_sb[:], flat(x2bf))
    lv = p.dma("gpsimd", xp_sb[:], flat(xprev))
    lv = p.dma("gpsimd", wx_sb[:], flat(w_x))
    lv = p.dma("gpsimd", wdt_sb[:], w_dt)
    ls = p.dma("sync", dwk_sb[:], flat(dwk), sem_name="lds")
    ls = p.dma("sync", cb_sb[:], cbias, sem_name="lds")
    ls = p.dma("sync", bdt_sb[:], bdt, sem_name="lds")
    ls = p.dma("sync", id_sb[:], ident, sem_name="lds")
    for eng in ("tensor", "vector", "scalar"):
        p.wait(eng, "dma_gpsimd", lv)
        p.wait(eng, "lds", ls)

    banks = Banks(p, [pb[:, 0:512] for pb in p.pp[0:2]])

    for oc in range(16):
        wi, wb, (wsem, wtok) = ws.load(w_in[:, oc, :, :].rearrange("p a b -> p (a b)"),
                                       part=slice(0, 8 * 128))
        bi, pb = banks.next()
        p.wait("tensor", wsem, wtok)

        def mmh(e, oc=oc, pb=pb, wb=wb):
            last = None
            for ic in range(8):
                last = e.matmul(pb[:, 0:3], wb[:, ic * 128:(ic + 1) * 128],
                                xp_sb[:, ic * 3:(ic + 1) * 3],
                                start=(ic == 0), stop=(ic == 7))
            return last
        tok = p.t(mmh)
        p.wait("scalar", "mm", tok)

        def evh(e, buf, pb=pb):
            return e.copy(out=buf[:, 0:3], in_=pb[:, 0:3])
        xin_out.put(evh, xin_scr[:, oc, 0:3], part=slice(0, 3))
        banks.set_guard(bi, "scalar", p.chain_mark("scalar"))
        for tt in range(4):
            bi, pb = banks.next()

            def mm(e, oc=oc, tt=tt, pb=pb, wb=wb):
                last = None
                for ic in range(8):
                    last = e.matmul(pb, wb[:, ic * 128:(ic + 1) * 128],
                                    x2_sb[:, ic * TPC + tt * 512: ic * TPC + (tt + 1) * 512],
                                    start=(ic == 0), stop=(ic == 7))
                return last
            tok = p.t(mm)
            p.wait("scalar", "mm", tok)
            xin_out.put(lambda e, buf, pb=pb: e.copy(out=buf[:], in_=pb),
                        xin_scr[:, oc, 3 + tt * 512: 3 + (tt + 1) * 512])
            banks.set_guard(bi, "scalar", p.chain_mark("scalar"))
        ws.release(wi, "mm", tok)
    for oc in range(16):
        wi, wb, (wsem, wtok) = ws.load(w_in[:, 16 + oc, :, :].rearrange("p a b -> p (a b)"),
                                       part=slice(0, 8 * 128))
        for tt in range(4):
            bi, pb = banks.next()
            p.wait("tensor", wsem, wtok)

            def mm(e, oc=oc, tt=tt, pb=pb, wb=wb):
                last = None
                for ic in range(8):
                    last = e.matmul(pb, wb[:, ic * 128:(ic + 1) * 128],
                                    x2_sb[:, ic * TPC + tt * 512: ic * TPC + (tt + 1) * 512],
                                    start=(ic == 0), stop=(ic == 7))
                return last
            tok = p.t(mm)
            p.wait("scalar", "mm", tok)
            z_out.put(lambda e, buf, pb=pb: e.copy(out=buf[:], in_=pb),
                      flat(o_z)[:, oc * TPC + tt * 512: oc * TPC + (tt + 1) * 512])
            banks.set_guard(bi, "scalar", p.chain_mark("scalar"))
        ws.release(wi, "mm", tok)

    xin_out.flush_wait("sync")
    prev_conv_tok = [0, 0]
    for oc in range(16):
        par = oc % 2
        xi, xb, (xsem, xtok) = xins.load(xin_scr[:, oc, :])
        p.wait("vector", "mm", prev_conv_tok[par])
        for j in range(4):
            p.v(lambda e, j=j, oc=oc, par=par: e.tensor_single_scalar(
                out=diag_sb[par][:, j * 128:(j + 1) * 128], in_=id_sb[:],
                scalar=dwk_sb[:, j * 16 + oc: j * 16 + oc + 1], op=ALU.mult))
        dmark = p.chain_mark("vector")
        p.wait_chain("vector", "tensor", dmark)
        p.wait("tensor", xsem, xtok)
        lasttok = 0
        for tt in range(4):
            bi, pb = banks.next()

            def mmc(e, oc=oc, tt=tt, pb=pb, xb=xb, par=par):
                last = None
                for j in range(4):
                    last = e.matmul(pb, diag_sb[par][:, j * 128:(j + 1) * 128],
                                    xb[:, j + tt * 512: j + tt * 512 + 512],
                                    start=(j == 0), stop=(j == 3))
                return last
            tok = p.t(mmc)
            lasttok = tok
            p.wait("scalar", "mm", tok)
            mk = p.s(lambda e, oc=oc, tt=tt, pb=pb: e.copy(
                out=conv_sb[:, oc * TPC + tt * 512: oc * TPC + (tt + 1) * 512], in_=pb))
            banks.set_guard(bi, "scalar", mk)
        xins.release(xi, "mm", lasttok)
        prev_conv_tok[par] = lasttok
    cmark = p.chain_mark("scalar")
    p.wait_chain("scalar", "tensor", cmark)

    pX = [p.pp[4 + i][0:80, 0:512] for i in range(4)]
    for oc in range(16):
        wi, wb, (wsem, wtok) = ws.load(pw[:, oc, :, :].rearrange("p a b -> p (a b)"))
        for tt in range(4):
            bi, pb = banks.next()
            p.wait("tensor", wsem, wtok)

            def mmp(e, oc=oc, tt=tt, pb=pb, wb=wb):
                last = None
                for ic in range(16):
                    last = e.matmul(pb, wb[:, ic * 128:(ic + 1) * 128],
                                    conv_sb[:, ic * TPC + tt * 512: ic * TPC + (tt + 1) * 512],
                                    start=(ic == 0), stop=(ic == 15))
                return last
            tok = p.t(mmp)
            p.wait("scalar", "mm", tok)
            xcb = xct[(oc * 4 + tt) % 2]
            mk = p.s(lambda e, pb=pb, oc=oc, xcb=xcb: e.activation(
                out=xcb[:], in_=pb, func=AF.Silu, bias=cb_sb[:, oc:oc + 1]))
            banks.set_guard(bi, "scalar", mk)
            xc_out.put(lambda e, buf, xcb=xcb: e.copy(out=buf[:], in_=xcb[:]),
                       flat(o_xc)[:, oc * TPC + tt * 512: oc * TPC + (tt + 1) * 512])
            p.wait_chain("scalar", "tensor", p.chain_mark("scalar"))

            def mmx(e, oc=oc, tt=tt, xcb=xcb):
                return e.matmul(pX[tt], wx_sb[:, oc * 80:(oc + 1) * 80], xcb[:],
                                start=(oc == 0), stop=(oc == 15))
            p.t(mmx)
        ws.release(wi, "mm", p.cnt["mm"])

    tok = p.cnt["mm"]
    p.wait("scalar", "mm", tok)
    for tt in range(4):
        ts = slice(tt * 512, (tt + 1) * 512)
        p.s(lambda e, tt=tt, ts=ts: e.copy(out=xdbl_sb[0:64, ts], in_=pX[tt][0:64, :]))
        mk = p.s(lambda e, tt=tt, ts=ts: e.copy(out=xdbl_sb[64:80, ts], in_=pX[tt][64:80, :]))
    xd_mark = p.chain_mark("scalar")
    p.wait_chain("scalar", "gpsimd", xd_mark)
    bctok = p.dma("gpsimd", o_bc, xdbl_sb[64:80, :], sem_name="dst_bcx")
    p.wait_chain("scalar", "tensor", xd_mark)

    sp2 = [(spa, spb, spc),
           (p.sb("spa1", [128, 2048], F32), p.sb("spb1", [128, 2048], F32),
            p.sb("spc1", [128, 2048], F32))]
    dtok = [0, 0]
    add_mark = [0, 0]
    for oc in range(16):
        par = oc % 2
        pa, pb_, pc = sp2[par]
        p.wait("scalar", f"dst_dlt{par}", dtok[par])
        for tt in range(4):
            bi, pb = banks.next()

            def mmd(e, oc=oc, tt=tt, pb=pb):
                return e.matmul(pb, wdt_sb[:, oc * 128:(oc + 1) * 128],
                                xdbl_sb[0:64, tt * 512:(tt + 1) * 512],
                                start=True, stop=True)
            tok = p.t(mmd)
            p.wait("scalar", "mm", tok)
            m1 = p.s(lambda e, pb=pb, oc=oc, tt=tt, pa=pa: e.activation(
                out=pa[:, tt * 512:(tt + 1) * 512], in_=pb, func=AF.Identity,
                bias=bdt_sb[:, oc:oc + 1]))
            banks.set_guard(bi, "scalar", m1)
        m1 = p.chain_mark("scalar")
        p.wait_chain("scalar", "vector", m1)
        mv = p.v(lambda e, pa=pa, pc=pc: e.tensor_scalar_max(pc[:], pa[:], 0.0))
        mv2 = p.v(lambda e, pa=pa, pb_=pb_, pc=pc: e.scalar_tensor_tensor(
            out=pb_[:], in0=pc[:], scalar=2.0, in1=pa[:],
            op0=ALU.mult, op1=ALU.subtract))
        p.wait_chain("vector", "scalar", mv2)
        p.s(lambda e, pb_=pb_: e.activation(out=pb_[:], in_=pb_[:], func=AF.Exp,
                                            scale=np.float32(-1.0)))
        m2 = p.s(lambda e, pb_=pb_: e.activation(out=pb_[:], in_=pb_[:], func=AF.Ln,
                                                 bias=np.float32(1.0)))
        p.wait_chain("scalar", "vector", m2)
        mv = p.v(lambda e, pa=pa, pb_=pb_, pc=pc: e.tensor_add(pa[:], pb_[:], pc[:]))
        add_mark[par] = mv
        p.wait_chain("vector", "scalar", mv)
        dtok[par] = p.dma("scalar", flat(o_delta)[:, oc * TPC:(oc + 1) * TPC],
                          pa[:], sem_name=f"dst_dlt{par}")

    z_out.flush_wait("gpsimd")
    xc_out.flush_wait("gpsimd")
    p.wait("gpsimd", "dst_dlt0", dtok[0])
    p.wait("gpsimd", "dst_dlt1", dtok[1])
    p.wait("gpsimd", "dst_bcx", bctok)
    return p.finalize()


def build_scan():
    p = P()
    TL = B * L // 2
    QT = 2048
    NQ = TL // QT
    delta_in = p.din("delta_in", [128, 2, 2, TL], F32)
    xc_in = p.din("xc_in", [128, 2, 2, TL], BF16)
    bmat = p.din("bmat", [2, NSTATE, TL], F32)
    cmat = p.din("cmat", [2, NSTATE, TL], F32)
    alog = p.din("alog", [128, 2, NSTATE], F32)

    o_y = p.dout("o_y", [128, 2, 2, TL], BF16)

    dl_sb = p.sb("dl_sb", [128, TL], F32)
    u_sb = p.sb("u_sb", [128, TL], F32)
    y_sb = p.sb("y_sb", [128, TL], F32)
    xc_sb = p.sb("xc_sb", [128, TL], BF16)
    dA_sb = [p.sb(f"dA{i}", [128, QT], F32) for i in range(2)]
    dBu_sb = p.sb("dBu_sb", [128, QT], F32)
    h_sb = p.sb("h_sb", [128, QT], F32)
    hl_sb = p.sb("hl_sb", [128, 1], F32)
    bb_sb = [p.sb(f"bb{i}", [128, QT], F32) for i in range(2)]
    cb_sb = [p.sb(f"cb{i}", [128, QT], F32) for i in range(2)]
    na_sb = p.sb("na_sb", [128, 2 * NSTATE], F32)
    al_sb = p.sb("al_sb", [128, 2 * NSTATE], F32)
    yb_out = OutStream(p, "yb", [128, 2048], BF16, nbuf=2, q="sync")

    ls = p.dma("sync", al_sb[:], flat(alog), sem_name="lds")
    p.wait("scalar", "lds", ls)
    p.s(lambda e: e.activation(out=na_sb[:], in_=al_sb[:], func=AF.Exp))
    nmark = p.chain_mark("scalar")
    p.wait_chain("scalar", "vector", nmark)
    p.v(lambda e: e.tensor_scalar_mul(na_sb[:], na_sb[:], -1.0))
    namark = p.chain_mark("vector")
    p.wait_chain("vector", "scalar", namark)

    iters = [(b, dc, n, qq)
             for b in range(2) for dc in range(2)
             for n in range(NSTATE) for qq in range(NQ)]
    bc_tok = {}
    dA_mark = {}
    bc_guard = [0, 0]
    dA_guard = [0, 0]
    ld_guard = {}

    def issue_bc(i):
        b, dc, n, qq = iters[i]
        sl = slice(qq * QT, (qq + 1) * QT)
        par = i % 2
        p.wait_chain("vector", "sync", bc_guard[par])
        t1 = p.dma("sync", bb_sb[par][:], bmat[b:b + 1, n, sl].broadcast_to((128, QT)),
                   sem_name=f"ld_bb{par}")
        t2 = p.dma("sync", cb_sb[par][:], cmat[b:b + 1, n, sl].broadcast_to((128, QT)),
                   sem_name=f"ld_cb{par}")
        bc_tok[i] = (t1, t2)

    def issue_dA(i):
        b, dc, n, qq = iters[i]
        sl = slice(qq * QT, (qq + 1) * QT)
        par = i % 2
        p.wait_chain("vector", "scalar", dA_guard[par])
        p.wait("scalar", f"ld_dl{qq}", ld_guard[("dl", b, dc, qq)])
        dA_mark[i] = p.s(lambda e, sl=sl, dc=dc, n=n, par=par: e.activation(
            out=dA_sb[par][:], in_=dl_sb[:, sl],
            func=AF.Exp, scale=na_sb[:, dc * NSTATE + n: dc * NSTATE + n + 1]))

    lv_dl = lv_xc = 0
    for b in range(2):
        for dc in range(2):
            vg = p.chain_mark("vector")
            p.wait_chain("vector", "gpsimd", vg)
            sg = p.chain_mark("scalar")
            p.wait_chain("scalar", "gpsimd", sg)
            for q in range(NQ):
                qsl = slice(q * QT, (q + 1) * QT)
                ld_guard[("dl", b, dc, q)] = p.dma(
                    "gpsimd", dl_sb[:, qsl], delta_in[:, dc, b, qsl],
                    sem_name=f"ld_dl{q}")
                ld_guard[("xc", b, dc, q)] = p.dma(
                    "gpsimd", xc_sb[:, qsl], xc_in[:, dc, b, qsl],
                    sem_name=f"ld_xc{q}")
            base = (b * 2 + dc) * NSTATE * NQ
            if base == 0:
                issue_bc(0)
            issue_dA(base)
            p.wait_chain("scalar", "vector", p.chain_mark("scalar"))
            p.v(lambda e: e.memset(y_sb[:], 0.0))
            for q in range(NQ):
                qsl = slice(q * QT, (q + 1) * QT)
                p.wait("vector", f"ld_dl{q}", ld_guard[("dl", b, dc, q)])
                p.wait("vector", f"ld_xc{q}", ld_guard[("xc", b, dc, q)])
                p.v(lambda e, qsl=qsl: e.tensor_mul(u_sb[:, qsl], dl_sb[:, qsl],
                                                    xc_sb[:, qsl]))
            for n in range(NSTATE):
                for qq in range(NQ):
                    i = base + n * NQ + qq
                    par = i % 2
                    sl = slice(qq * QT, (qq + 1) * QT)
                    if i + 1 < len(iters) and iters[i + 1][0] == b and iters[i + 1][1] == dc:
                        issue_bc(i + 1)
                        issue_dA(i + 1)
                    elif i + 1 < len(iters):
                        issue_bc(i + 1)
                    t1, t2 = bc_tok[i]
                    p.wait("vector", f"ld_bb{par}", t1)
                    p.wait_chain("scalar", "vector", dA_mark[i])
                    p.v(lambda e, sl=sl, par=par: e.tensor_mul(
                        dBu_sb[:], u_sb[:, sl], bb_sb[par][:]))
                    if qq == 0:
                        p.v(lambda e, par=par: e.tensor_tensor_scan(
                            out=h_sb[:], data0=dA_sb[par][:], data1=dBu_sb[:],
                            initial=0.0, op0=ALU.mult, op1=ALU.add))
                    else:
                        p.v(lambda e, par=par: e.tensor_tensor_scan(
                            out=h_sb[:], data0=dA_sb[par][:], data1=dBu_sb[:],
                            initial=hl_sb[:, 0:1], op0=ALU.mult, op1=ALU.add))
                    dA_guard[par] = p.chain_mark("vector")
                    if qq < NQ - 1:
                        p.v(lambda e: e.tensor_copy(hl_sb[:, 0:1], h_sb[:, QT - 1:QT]))
                    p.wait("vector", f"ld_cb{par}", t2)
                    p.v(lambda e, par=par: e.tensor_mul(dBu_sb[:], h_sb[:], cb_sb[par][:]))
                    mv = p.v(lambda e, sl=sl: e.tensor_add(y_sb[:, sl], y_sb[:, sl], dBu_sb[:]))
                    bc_guard[par] = mv
            vg = p.chain_mark("vector")
            p.wait_chain("vector", "scalar", vg)
            for q in range(TL // 2048):
                yb_out.put(lambda e, buf, q=q: e.copy(out=buf[:], in_=y_sb[:, q * 2048:(q + 1) * 2048]),
                           o_y[:, dc, b, q * 2048:(q + 1) * 2048])
            nb = (b * 2 + dc) + 1
            if nb < 4:
                pass
    yb_out.flush_wait("gpsimd")
    return p.finalize()


def build_l1back():
    p = P()
    y_in = p.din("y_in", [128, 16, TPC], BF16)
    xc_in = p.din("xc_in", [128, 16, TPC], BF16)
    z_in = p.din("z_in", [128, 16, TPC], BF16)
    dvec = p.din("dvec", [128, 16], F32)
    w_out = p.din("w_out", [128, 8, 16, 128], F32)
    x2 = p.din("x2", [128, 8, TPC], F32)
    ones2 = p.din("ones2", [128, 128], F32)

    o_x3 = p.dout("o_x3", [128, 8, TPC], F32)
    o_x3bf = p.dout("o_x3bf", [128, 8, TPC], BF16)

    y2_sb = p.sb("y2_sb", [128, 16 * TPC], BF16)
    r3_sb = p.sb("r3_sb", [128, 8 * TPC], F32)
    dvec_sb = p.sb("dvec_sb", [128, 16], F32)
    sz_sb = p.sb("sz_sb", [128, 512], F32)
    g1_sb = p.sb("g1_sb", [128, 512], F32)
    ys = InStream(p, "ys", [128, 512], BF16, nbuf=2, q="sync")
    xcs = InStream(p, "xcs", [128, 512], BF16, nbuf=2, q="sync")
    zs = InStream(p, "zs", [128, 512], BF16, nbuf=2, q="sync")
    x2s = InStream(p, "x2s", [128, 512], F32, nbuf=2, q="sync")
    ws = InStream(p, "ws", [128, 16 * 128], BF16, nbuf=2, q="gpsimd")
    ln = dict(
        ones_col_r=p.sb("onescol", [128, 1], F32R),
        ones_row_r=p.sb("onesrow", [1, 128], F32R),
        ones_col_f=p.sb("onescolf", [128, 1], F32),
        sq_r=p.sb("sq_r", [128, 2 * 512], F32R),
        rows=p.sb("rows", [1, 4 * 512], F32),
        rrow=p.sb("rrow", [1, 2 * 512], F32R),
        tmp32=p.sb("tmp32", [128, 512], F32),
    )
    x3_out = OutStream(p, "x3f", [128, 512], F32, nbuf=2)
    x3bf_out = OutStream(p, "x3bf", [128, 512], BF16, nbuf=2)

    lv = p.dma("gpsimd", ln["ones_col_r"][:], ones2[:, 0:1],
               allow_slow_non_contiguous=True)
    lv = p.dma("gpsimd", ln["ones_row_r"][:], ones2[0:1, :])
    ls = p.dma("sync", dvec_sb[:], dvec, sem_name="lds")
    ls = p.dma("sync", ln["ones_col_f"][:], ones2[:, 0:1], sem_name="lds",
               allow_slow_non_contiguous=True)
    for eng in ("tensor", "vector", "scalar"):
        p.wait(eng, "dma_gpsimd", lv)
        p.wait(eng, "lds", ls)

    for oc in range(16):
        for tt in range(4):
            sl = slice(oc * TPC + tt * 512, oc * TPC + (tt + 1) * 512)
            yi, yb, (ysem, ytok) = ys.load(flat(y_in)[:, sl])
            xi, xb, (xsem, xtok) = xcs.load(flat(xc_in)[:, sl])
            zi, zb, (zsem, ztok) = zs.load(flat(z_in)[:, sl])
            p.wait("scalar", zsem, ztok)
            mk = p.s(lambda e, zb=zb: e.activation(out=sz_sb[:], in_=zb[:], func=AF.Silu))
            p.wait_chain("scalar", "vector", mk)
            p.wait("vector", ysem, ytok)
            p.wait("vector", xsem, xtok)
            p.v(lambda e, yb=yb, xb=xb, oc=oc: e.scalar_tensor_tensor(
                out=g1_sb[:], in0=xb[:], scalar=dvec_sb[:, oc:oc + 1], in1=yb[:],
                op0=ALU.mult, op1=ALU.add))
            mv = p.v(lambda e, sl=sl: e.tensor_mul(y2_sb[:, sl], g1_sb[:], sz_sb[:]))
            ys.release(yi, "chain_vector", mv)
            xcs.release(xi, "chain_vector", mv)
            zs.release(zi, "chain_vector", mv)
            p.wait_chain("vector", "scalar", mv)
    y2mark = p.chain_mark("vector")
    p.wait_chain("vector", "tensor", y2mark)

    banks = Banks(p, [pb[:, 0:512] for pb in p.pp[0:2]])
    for oc in range(8):
        wi, wb, (wsem, wtok) = ws.load(w_out[:, oc, :, :].rearrange("p a b -> p (a b)"))
        for tt in range(4):
            sl = slice(oc * TPC + tt * 512, oc * TPC + (tt + 1) * 512)
            xi, xbuf, (xsem, xtok) = x2s.load(flat(x2)[:, sl])
            bi, pb = banks.next()
            p.wait("tensor", wsem, wtok)

            def mm(e, oc=oc, tt=tt, pb=pb, wb=wb):
                last = None
                for ic in range(16):
                    last = e.matmul(pb, wb[:, ic * 128:(ic + 1) * 128],
                                    y2_sb[:, ic * TPC + tt * 512: ic * TPC + (tt + 1) * 512],
                                    start=(ic == 0), stop=(ic == 15))
                return last
            tok = p.t(mm)
            p.wait("scalar", "mm", tok)
            m1 = p.s(lambda e, pb=pb, sl=sl: e.copy(out=r3_sb[:, sl], in_=pb))
            p.wait_chain("scalar", "vector", m1)
            p.wait("vector", xsem, xtok)
            mv = p.v(lambda e, sl=sl, xbuf=xbuf: e.tensor_tensor(
                out=r3_sb[:, sl], in0=r3_sb[:, sl], in1=xbuf[:], op=ALU.add))
            x2s.release(xi, "chain_vector", mv)
            banks.set_guard(bi, "scalar", m1)
        ws.release(wi, "mm", tok)
    vres = p.chain_mark("vector")
    p.wait_chain("vector", "scalar", vres)
    p.wait_chain("vector", "tensor", vres)

    emit_layernorm(p, r3_sb, ln, o_x3, o_x3bf, x3_out, x3bf_out,
                   nchunk=8, tpc=TPC, pbanks=(4, 5, 6, 7))
    x3_out.flush_wait("gpsimd")
    x3bf_out.flush_wait("gpsimd")
    return p.finalize()


def build_qkv():
    p = P()
    x3bf = p.din("x3bf", [128, 8, TPC], BF16)
    wq = p.din("wq", [128, 8, 8, 128], F32)
    wk = p.din("wk", [128, 8, 8, 128], F32)
    wvp = p.din("wvp", [128, 8, D], F32)
    bq = p.din("bq", [128, 8], F32)
    bk = p.din("bk", [128, 8], F32)
    bv_row = p.din("bv_row", [1, D], F32)
    temp_in = p.din("temp_in", [1, NH], F32)
    qe_pcol = p.din("qe_pcol", [128, 8], F32)
    esel = p.din("esel", [NH, 128], F32)
    hmask = p.din("hmask", [NH, 8], F32)
    jsel = p.din("jsel", [2, 128], F32)
    hsel = p.din("hsel", [128, 2], F32)
    ones2 = p.din("ones2", [128, 128], F32)

    o_qt = p.dout("o_qt", [128, 8, TPC], BF16)
    o_kf = p.dout("o_kf", [128, 16, 8, 128], BF16)
    o_vaug = p.dout("o_vaug", [128, 16, NH, DH + 1], BF16)
    o_norms = p.dout("o_norms", [2, 16], F32)

    x3_sb = p.sb("x3_sb", [128, 8 * TPC], BF16)
    qf_sb = p.sb("qf_sb", [128, 8 * TPC], BF16)
    wq_sb = p.sb("wq_sb", [128, 64 * 128], BF16)
    wk_sb = p.sb("wk_sb", [128, 64 * 128], BF16)
    wvp_sb = p.sb("wvp_sb", [128, 8 * D], BF16)
    bq_sb = p.sb("bq_sb", [128, 8], F32)
    bk_sb = p.sb("bk_sb", [128, 8], F32)
    bvr_sb = p.sb("bvr_sb", [1, D], BF16)
    ones_sb = p.sb("ones_sb", [1, 128], BF16)
    temp_sb = p.sb("temp_sb", [1, 3 * NH], F32)
    sptc_sb = p.sb("sptc_sb", [16, 1], F32)
    sptm_sb = p.sb("sptm_sb", [16, 8], F32)
    esel_sb = p.sb("esel_sb", [NH, 128], F32)
    hmask_sb = p.sb("hmask_sb", [NH, 8], F32)
    jsel_sb = p.sb("jsel_sb", [2, 128], F32R)
    hsel_sb = p.sb("hsel_sb", [128, 2], F32R)
    sptp_sb = p.sb("sptp_sb", [128, 8], F32)
    qesp_sb = p.sb("qesp_sb", [128, 8], F32)
    sq_sb = p.sb("sq_sb", [128, 512], F32R)
    rn_sb = p.sb("rn_sb", [2, 512], F32)
    rnr_sb = p.sb("rnr_sb", [2, 512], F32R)
    nmax_sb = p.sb("nmax_sb", [2, 18], F32)
    qt_sb = p.sb("qt_sb", [128, 512], F32)
    vaug_sb = [p.sb(f"vaug{i}", [128, NH * (DH + 1)], BF16) for i in range(2)]
    id1_sb = p.sb("id1_sb", [1, 1], F32)
    qt_out = OutStream(p, "qt", [128, 512], BF16, nbuf=2)
    kf_out = OutStream(p, "kf", [128, 512], BF16, nbuf=2)

    lv = p.dma("gpsimd", x3_sb[:], flat(x3bf))
    lv = p.dma("gpsimd", wq_sb[:], flat(wq))
    lv = p.dma("gpsimd", wk_sb[:], flat(wk))
    lv = p.dma("gpsimd", wvp_sb[:], flat(wvp))
    lv = p.dma("gpsimd", jsel_sb[:], jsel)
    lv = p.dma("gpsimd", hsel_sb[:], hsel)
    lv = p.dma("gpsimd", bvr_sb[:], bv_row)
    ls = p.dma("sync", bq_sb[:], bq, sem_name="lds")
    ls = p.dma("sync", bk_sb[:], bk, sem_name="lds")
    ls = p.dma("sync", temp_sb[:, 0:NH], temp_in, sem_name="lds")
    ls = p.dma("sync", hmask_sb[:], hmask, sem_name="lds")
    ls = p.dma("sync", esel_sb[:], esel, sem_name="lds")
    ls = p.dma("sync", qesp_sb[:], qe_pcol, sem_name="lds")
    ls = p.dma("sync", id1_sb[:], ones2[0:1, 0:1], sem_name="lds")
    for eng in ("tensor", "vector", "scalar"):
        p.wait(eng, "dma_gpsimd", lv)
        p.wait(eng, "lds", ls)
    p.v(lambda e: e.memset(ones_sb[:], 1.0))
    p.v(lambda e: e.memset(nmax_sb[:], 0.0))
    init_mark = p.chain_mark("vector")
    p.wait_chain("vector", "tensor", init_mark)
    p.wait_chain("vector", "scalar", init_mark)

    t0, t1s, t2s = slice(0, NH), slice(NH, 2 * NH), slice(2 * NH, 3 * NH)
    p.s(lambda e: e.activation(out=temp_sb[0:1, t1s], in_=temp_sb[0:1, t0],
                               func=AF.Exp, scale=np.float32(-1.0)))
    mk = p.chain_mark("scalar")
    p.wait_chain("scalar", "vector", mk)
    p.v(lambda e: e.tensor_single_scalar(out=temp_sb[0:1, t2s], in_=temp_sb[0:1, t0],
                                         scalar=0.0, op=ALU.max))
    mk = p.chain_mark("vector")
    p.wait_chain("vector", "scalar", mk)
    p.s(lambda e: e.activation(out=temp_sb[0:1, t1s], in_=temp_sb[0:1, t1s],
                               func=AF.Ln, bias=np.float32(1.0)))
    mk = p.chain_mark("scalar")
    p.wait_chain("scalar", "vector", mk)
    p.v(lambda e: e.tensor_add(temp_sb[0:1, t0], temp_sb[0:1, t1s], temp_sb[0:1, t2s]))
    mk = p.chain_mark("vector")
    p.wait_chain("vector", "tensor", mk)
    tok = p.t(lambda e: e.transpose(p.pp[0][0:NH, 0:1], temp_sb[0:1, t0], id1_sb[:]))
    p.wait("scalar", "mm", tok)
    mk = p.s(lambda e: e.copy(out=sptc_sb[:], in_=p.pp[0][0:NH, 0:1]))
    p.wait_chain("scalar", "vector", mk)
    mv = p.v(lambda e: e.tensor_single_scalar(out=sptm_sb[:], in_=hmask_sb[:],
                                              scalar=sptc_sb[:, 0:1], op=ALU.mult))
    p.wait_chain("vector", "tensor", mv)
    tok = p.t(lambda e: e.matmul(p.pp[1][:, 0:8], esel_sb[:], sptm_sb[:],
                                 start=True, stop=True))
    p.wait("scalar", "mm", tok)
    mk = p.s(lambda e: e.copy(out=sptp_sb[:], in_=p.pp[1][:, 0:8]))
    p.wait_chain("scalar", "vector", mk)
    mv = p.v(lambda e: e.tensor_mul(qesp_sb[:], qesp_sb[:], sptp_sb[:]))
    p.wait_chain("vector", "scalar", mv)
    p.wait_chain("vector", "tensor", mv)

    banks = Banks(p, [pb[:, 0:512] for pb in p.pp[2:4]])
    pN = p.pp[4][0:2, 0:512]
    pRB = p.pp[5][:, 0:512]

    for oc in range(8):
        for tt in range(4):
            bi, pb = banks.next()

            def mm(e, oc=oc, tt=tt, pb=pb):
                last = None
                for ic in range(8):
                    last = e.matmul(pb, wq_sb[:, (oc * 8 + ic) * 128:(oc * 8 + ic + 1) * 128],
                                    x3_sb[:, ic * TPC + tt * 512: ic * TPC + (tt + 1) * 512],
                                    start=(ic == 0), stop=(ic == 7))
                return last
            tok = p.t(mm)
            p.wait("scalar", "mm", tok)
            mk = p.s(lambda e, pb=pb, oc=oc, tt=tt: e.activation(
                out=qf_sb[:, oc * TPC + tt * 512: oc * TPC + (tt + 1) * 512],
                in_=pb, func=AF.Identity, bias=bq_sb[:, oc:oc + 1]))
            banks.set_guard(bi, "scalar", mk)
    qmark = p.chain_mark("scalar")
    p.wait_chain("scalar", "tensor", qmark)

    pNs = [p.pp[4][0:2, 0:512], p.pp[6][0:2, 0:512]]
    pRBs = [p.pp[5][:, 0:512], p.pp[7][:, 0:512]]
    sq2 = [sq_sb, p.sb("sq2_sb", [128, 512], F32R)]
    rn2 = [rn_sb, p.sb("rn2_sb", [2, 512], F32)]
    rnr2 = [rnr_sb, p.sb("rnr2_sb", [2, 512], F32R)]
    qt2 = [qt_sb, p.sb("qt2_sb", [128, 512], F32)]
    sqg = [0, 0]
    ln_done = 0
    rbg = [0, 0]
    qtg = [0, 0]
    red_done = 0
    for fc in range(8):
        for tt in range(4):
            ii = (fc * 4 + tt) % 2
            sl = slice(fc * TPC + tt * 512, fc * TPC + (tt + 1) * 512)
            p.wait("scalar", "mm", sqg[ii])
            mk = p.s(lambda e, sl=sl, ii=ii: e.activation(
                out=sq2[ii][:], in_=qf_sb[:, sl], func=AF.Square))
            p.wait_chain("scalar", "tensor", mk)
            p.wait_chain("scalar", "tensor", ln_done)
            tokq = p.t(lambda e, ii=ii: e.matmul(pNs[0], hsel_sb[:], sq2[ii][:],
                                                 start=True, stop=True))
            p.wait("scalar", "mm", tokq)
            p.s(lambda e, ii=ii: e.activation(out=rn2[ii][:], in_=pNs[0], func=AF.Ln))
            ln_done = lnm = p.chain_mark("scalar")

            def expfn(e, ii=ii):
                with p.nc.allow_low_precision(reason="f32r norm bcast"):
                    return e.activation(out=rnr2[ii][:], in_=rn2[ii][:],
                                        func=AF.Exp, scale=np.float32(-0.5))
            mke = p.s(expfn)
            p.wait_chain("scalar", "tensor", mke)
            p.wait_chain("vector", "tensor", rbg[ii])
            tokb = p.t(lambda e, ii=ii: e.matmul(pRBs[ii], jsel_sb[:], rnr2[ii][:],
                                                 start=True, stop=True))
            p.wait("vector", "mm", tokb)
            p.wait_chain("scalar", "vector", qtg[ii])
            p.v(lambda e, sl=sl, fc=fc, ii=ii: e.scalar_tensor_tensor(
                out=qt2[ii][:], in0=qf_sb[:, sl], scalar=sptp_sb[:, fc:fc + 1],
                in1=pRBs[ii], op0=ALU.mult, op1=ALU.mult))
            mv = p.v(lambda e, fc=fc, ii=ii: e.tensor_single_scalar(
                out=qt2[ii][:], in_=qt2[ii][:], scalar=qesp_sb[:, fc:fc + 1], op=ALU.add))
            rbg[ii] = mv
            p.wait_chain("vector", "scalar", mv)
            qt_out.put(lambda e, buf, ii=ii: e.copy(out=buf[:], in_=qt2[ii][:]),
                       flat(o_qt)[:, sl])
            p.wait("scalar", "mm", tokq)
            mk2 = p.s(lambda e, ii=ii: e.activation(out=sq2[ii][:], in_=qt2[ii][:],
                                                    func=AF.Square))
            qtg[ii] = mk2
            p.wait_chain("scalar", "tensor", mk2)
            p.wait_chain("vector", "tensor", red_done)
            tok2 = p.t(lambda e, ii=ii: e.matmul(pNs[1], hsel_sb[:], sq2[ii][:],
                                                 start=True, stop=True))
            sqg[ii] = tok2
            p.wait("vector", "mm", tok2)
            p.v(lambda e: e.reduce_max(nmax_sb[:, 16:17], pNs[1], axis=AX.X))
            mv = p.v(lambda e, fc=fc: e.tensor_tensor(
                out=nmax_sb[:, fc:fc + 1], in0=nmax_sb[:, fc:fc + 1],
                in1=nmax_sb[:, 16:17], op=ALU.max))
            red_done = mv

    for fc in range(8):
        for tt in range(4):
            ii = (fc * 4 + tt) % 2
            bi, pb = banks.next()

            def mm(e, fc=fc, tt=tt, pb=pb):
                last = None
                for ic in range(8):
                    last = e.matmul(pb, wk_sb[:, (fc * 8 + ic) * 128:(fc * 8 + ic + 1) * 128],
                                    x3_sb[:, ic * TPC + tt * 512: ic * TPC + (tt + 1) * 512],
                                    start=(ic == 0), stop=(ic == 7))
                return last
            tok = p.t(mm)
            p.wait("scalar", "mm", tok)
            kbuf = kf_out.bufs[kf_out.k % kf_out.nbuf]
            kf_out.put(lambda e, buf, pb=pb, fc=fc: e.activation(
                out=buf[:], in_=pb, func=AF.Identity, bias=bk_sb[:, fc:fc + 1]),
                o_kf[:, tt * 4:(tt + 1) * 4, fc, :])
            banks.set_guard(bi, "scalar", p.chain_mark("scalar"))
            p.wait("scalar", "mm", sqg[ii])
            mk2 = p.s(lambda e, ii=ii, kbuf=kbuf: e.activation(
                out=sq2[ii][:], in_=kbuf[:], func=AF.Square))
            p.wait_chain("scalar", "tensor", mk2)
            p.wait_chain("vector", "tensor", red_done)
            tok2 = p.t(lambda e, ii=ii: e.matmul(pNs[1], hsel_sb[:], sq2[ii][:],
                                                 start=True, stop=True))
            sqg[ii] = tok2
            p.wait("vector", "mm", tok2)
            p.v(lambda e: e.reduce_max(nmax_sb[:, 16:17], pNs[1], axis=AX.X))
            mv = p.v(lambda e, fc=fc: e.tensor_tensor(
                out=nmax_sb[:, 8 + fc:9 + fc], in0=nmax_sb[:, 8 + fc:9 + fc],
                in1=nmax_sb[:, 16:17], op=ALU.max))
            red_done = mv
    nm0 = p.chain_mark("vector")
    p.wait_chain("vector", "scalar", nm0)
    nm = p.chain_mark("vector")
    p.wait_chain("vector", "gpsimd", nm)
    ntok = p.dma("gpsimd", o_norms, nmax_sb[0:2, 0:16], sem_name="dst_nm")

    va_guard = [0, 0]
    va_uses = [0, 0]
    for tc in range(16):
        i = tc % 2
        vb = vaug_sb[i]
        p.wait("vector", f"dst_va_{i}", va_uses[i] * 16)
        msk = p.v(lambda e, vb=vb: e.memset(vb[:], 1.0))
        p.wait_chain("vector", "scalar", msk)
        for oh in range(2):
            bi, pb = banks.next()

            def mm(e, tc=tc, oh=oh, pb=pb):
                for ic in range(8):
                    e.matmul(pb, x3_sb[:, ic * TPC + tc * 128: ic * TPC + (tc + 1) * 128],
                             wvp_sb[:, ic * D + oh * 512: ic * D + (oh + 1) * 512],
                             start=(ic == 0), stop=False)
                return e.matmul(pb, ones_sb[:], bvr_sb[:, oh * 512:(oh + 1) * 512],
                                start=False, stop=True)
            tok = p.t(mm)
            p.wait("scalar", "mm", tok)
            mk = p.s(lambda e, pb=pb, oh=oh, vb=vb: e.copy(
                out=vb[:].rearrange("p (h d) -> p h d", h=NH)[:, oh * 8:(oh + 1) * 8, 0:DH],
                in_=pb.rearrange("p (h d) -> p h d", h=8)))
            banks.set_guard(bi, "scalar", mk)
        vamark = p.chain_mark("scalar")
        p.wait_chain("scalar", "gpsimd", vamark)
        p.dma("gpsimd", o_vaug[:, tc, :, :], vb[:], sem_name=f"dst_va_{i}")
        va_uses[i] += 1

    qt_out.flush_wait("gpsimd")
    kf_out.flush_wait("gpsimd")
    p.wait("gpsimd", "dst_nm", ntok)
    p.wait("gpsimd", "dst_va_0", va_uses[0] * 16)
    p.wait("gpsimd", "dst_va_1", va_uses[1] * 16)
    return p.finalize()


def build_flash():
    p = P(npsum=0)
    NRB = 4
    qt = p.din("qt", [128, 8, TPC], BF16)
    kf_all = p.din("kf_all", [NRB, 128, 16, 8, 128], BF16)
    va_all = p.din("va_all", [NRB, 128, 16, NH, DH + 1], BF16)
    norms_all = p.din("norms_all", [NRB, 32], F32)
    x3 = p.din("x3", [128, 8, TPC], F32)
    wrsT = p.din("wrsT", [128, 8, 17], F32)
    brs = p.din("brs", [17, 1], F32)
    wp = p.din("wp", [128, 8, 8, 128], F32)
    bp = p.din("bp", [128, 8], F32)
    id128 = p.din("id128", [128, 128], F32)
    ones2 = p.din("ones2", [128, 128], F32)

    o_out = p.dout("o_out", [128, 8, TPC], F32)

    pscore = [p.ps("pscA", [128, 1024], F32), p.ps("pscB", [128, 1024], F32)]
    pavt = [p.ps("pav0", [128, 512], F32), p.ps("pav1", [128, 512], F32)]
    pmisc = [p.ps("pm0", [128, 512], F32), p.ps("pm1", [128, 512], F32)]

    qt_sb = p.sb("qt_sb", [128, 8 * TPC], BF16)
    h_sb = p.sb("h_sb", [128, 8 * TPC], BF16)
    acc = [p.sb(f"acc{i}", [65, TPC], F32) for i in range(2)]
    expP = [p.sb(f"expP{i}", [128, 1024], BF16) for i in range(4)]
    wp_sb = p.sb("wp_sb", [128, 64 * 128], BF16)
    wrs_sb = p.sb("wrs_sb", [128, 8 * 17], BF16)
    brs_sb = p.sb("brs_sb", [17, 1], F32)
    bp_sb = p.sb("bp_sb", [128, 8], F32)
    id_sb = p.sb("id_sb", [128, 128], F32)
    idb_sb = p.sb("idb_sb", [64, 64], BF16)
    nrm_sb = p.sb("nrm_sb", [1, 6 * 32], F32)
    mneg_sb = p.sb("mneg_sb", [128, NH], F32)
    mrow_r = p.sb("mrow_r", [1, NH], F32R)
    ones512r = p.sb("ones512r", [128, 512], F32R)
    tmp_bf = p.sb("tmp_bf", [64, 512], BF16)
    recr = p.sb("recr", [65, 1024], F32R)
    glog_sb = p.sb("glog_sb", [17, 512], F32)
    gtok_sb = p.sb("gtok_sb", [128, 68], F32)
    gwork_sb = p.sb("gwork_sb", [128, 64], F32)
    gwork2_sb = p.sb("gwork2_sb", [128, 64], F32)
    colw = p.sb("colw", [128, 32], F32)
    grow_r = p.sb("grow_r", [1, 512], F32R)
    gb_sb = p.sb("gb_sb", [128, 512], F32)
    r4_sb = p.sb("r4_sb", [128, 8 * 512], F32)
    pr_sb = p.sb("pr_sb", [128, 512], F32)
    ln = dict(
        ones_col_r=p.sb("onescol", [128, 1], F32R),
        ones_row_r=p.sb("onesrow", [1, 128], F32R),
        ones_col_f=p.sb("onescolf", [128, 1], F32),
        sq_r=p.sb("sq_r", [128, 2 * 512], F32R),
        rows=p.sb("rows", [1, 4 * 512], F32),
        rrow=p.sb("rrow", [1, 2 * 512], F32R),
        tmp32=p.sb("tmp32", [128, 512], F32),
    )
    kvs = InStream(p, "kvs", [128, 16 * 128 + 16 * 2 * 65], BF16, nbuf=3, q="gpsimd")
    x3s = InStream(p, "x3s", [128, 512], F32, nbuf=2, q="sync")
    out_out = OutStream(p, "oo", [128, 512], F32, nbuf=2, q="scalar")
    KOFF = 16 * 128

    lv = p.dma("gpsimd", ln["ones_col_r"][:], ones2[:, 0:1],
               allow_slow_non_contiguous=True, sem_name="ldsm")
    lv = p.dma("gpsimd", ln["ones_row_r"][:], ones2[0:1, :], sem_name="ldsm")
    lv = p.dma("gpsimd", idb_sb[:], id128[0:64, 0:64], sem_name="ldsm")
    for q4 in range(4):
        lv = p.dma("gpsimd", ones512r[:, q4 * 128:(q4 + 1) * 128], ones2,
                   sem_name="ldsm")
    p.dma("gpsimd", qt_sb[:, 0:TPC], flat(qt)[:, 0:TPC], sem_name="qseg0")
    ls = p.dma("sync", brs_sb[:], brs, sem_name="lds")
    ls = p.dma("sync", bp_sb[:], bp, sem_name="lds")
    ls = p.dma("sync", id_sb[:], id128, sem_name="lds")
    ls = p.dma("sync", nrm_sb[0:1, 0:128], norms_all.rearrange("r c -> (r c)").unsqueeze(0),
               sem_name="lds")
    ls = p.dma("sync", ln["ones_col_f"][:], ones2[:, 0:1], sem_name="lds",
               allow_slow_non_contiguous=True)
    for eng in ("tensor", "vector", "scalar"):
        p.wait(eng, "ldsm", lv)
        p.wait(eng, "lds", ls)

    p.v(lambda e: e.tensor_tensor(out=nrm_sb[0:1, 128:144], in0=nrm_sb[0:1, 48:64],
                                  in1=nrm_sb[0:1, 80:96], op=ALU.max))
    p.v(lambda e: e.tensor_tensor(out=nrm_sb[0:1, 128:144], in0=nrm_sb[0:1, 128:144],
                                  in1=nrm_sb[0:1, 112:128], op=ALU.max))
    mk = p.v(lambda e: e.tensor_tensor(out=nrm_sb[0:1, 128:144], in0=nrm_sb[0:1, 128:144],
                                       in1=nrm_sb[0:1, 16:32], op=ALU.max))
    p.wait_chain("vector", "scalar", mk)
    p.s(lambda e: e.activation(out=nrm_sb[0:1, 160:176], in_=nrm_sb[0:1, 0:16],
                               func=AF.Sqrt))
    mk = p.s(lambda e: e.activation(out=nrm_sb[0:1, 128:144], in_=nrm_sb[0:1, 128:144],
                                    func=AF.Sqrt))
    p.wait_chain("scalar", "vector", mk)
    mv = p.v(lambda e: e.tensor_mul(nrm_sb[0:1, 160:176], nrm_sb[0:1, 160:176],
                                    nrm_sb[0:1, 128:144]))
    p.wait_chain("vector", "scalar", mv)
    mk = p.s(lambda e: e.activation(out=nrm_sb[0:1, 160:176], in_=nrm_sb[0:1, 160:176],
                                    func=AF.Identity, scale=np.float32(0.125),
                                    bias=np.float32(2.0)))
    p.wait_chain("scalar", "vector", mk)
    pairs = nrm_sb[0:1, 160:176].rearrange("a (h two) -> a h two", two=2)
    mv = p.v(lambda e: e.tensor_tensor(out=nrm_sb[0:1, 144:152], in0=pairs[:, :, 0],
                                       in1=pairs[:, :, 1], op=ALU.max))
    p.v(lambda e: e.tensor_copy(pairs[:, :, 0], nrm_sb[0:1, 144:152]))
    p.v(lambda e: e.tensor_copy(pairs[:, :, 1], nrm_sb[0:1, 144:152]))
    mv = p.v(lambda e: e.tensor_scalar_mul(nrm_sb[0:1, 160:176], nrm_sb[0:1, 160:176], -1.0))
    p.wait_chain("vector", "scalar", mv)
    mk = p.s(lambda e: e.copy(out=mrow_r[:], in_=nrm_sb[0:1, 160:176]))
    p.wait_chain("scalar", "tensor", mk)
    tok = p.t(lambda e: e.matmul(pmisc[0][:, 0:NH], ln["ones_row_r"][:], mrow_r[:],
                                 start=True, stop=True))
    p.wait("scalar", "mm", tok)
    mk = p.s(lambda e: e.copy(out=mneg_sb[:], in_=pmisc[0][:, 0:NH]))
    p.wait_chain("scalar", "tensor", mk)
    p.wait_chain("scalar", "vector", mk)

    for hl in range(2):
        p.v(lambda e, hl=hl: e.memset(acc[hl][:], 0.0))
    steps = [(fc, rb, qt4, c) for fc in range(8) for rb in range(NRB)
             for qt4 in range(4) for c in range(16)]
    NST = len(steps)
    extra = []
    set_guard = [0, 0]
    exp_mark = {}
    av_tok = {}
    accv_of_run = {}
    kv_state = {}
    norm_state = dict(pm0=("scalar", mk), pm1=None, recr_tok=[0, 0],
                      slot_idx=0, tmpbf_tok=0, hmark=0)
    wtok = {}
    extra.append(lambda: wtok.__setitem__(
        "wrs", p.dma("gpsimd", wrs_sb[:], flat(wrsT), sem_name="ldw")))
    extra.append(lambda: wtok.__setitem__(
        "wp", p.dma("gpsimd", wp_sb[:], flat(wp), sem_name="ldw")))

    def slot_closures(fc, hl, q4):
        st = norm_state
        si = st["slot_idx"]
        st["slot_idx"] += 1
        par = si % 2
        rsl = recr[64:65, par * 512:(par + 1) * 512]
        ts = slice(q4 * 512, (q4 + 1) * 512)
        cols = slice(fc * TPC + q4 * 512, fc * TPC + (q4 + 1) * 512)
        hold = {}

        def A():
            p.wait("vector", "mm", st["recr_tok"][par])

            def f(e):
                with p.nc.allow_low_precision(reason="f32r denom broadcast"):
                    return e.reciprocal(rsl, acc[hl][64:65, ts])
            hold["mv"] = p.v(f)

        def Bc():
            p.wait_chain("vector", "tensor", hold["mv"])
            if st["pm0"]:
                p.wait_chain(st["pm0"][0], "tensor", st["pm0"][1])
            tok = p.t(lambda e: e.matmul(pmisc[0][0:64, 0:512],
                                         ones512r[64:65, 0:64], rsl,
                                         start=True, stop=True))
            st["recr_tok"][par] = tok
            p.wait("vector", "mm", tok)
            if hl == 0:
                mv = p.v(lambda e: e.tensor_tensor(
                    out=h_sb[0:64, cols], in0=acc[0][0:64, ts],
                    in1=pmisc[0][0:64, 0:512], op=ALU.mult))
            else:
                p.wait("vector", "mm", st["tmpbf_tok"])
                mv = p.v(lambda e: e.tensor_tensor(
                    out=tmp_bf[:], in0=acc[1][0:64, ts],
                    in1=pmisc[0][0:64, 0:512], op=ALU.mult))
                hold["mvb"] = mv
            st["pm0"] = ("vector", mv)
            st["hmark"] = mv

        def Cc():
            p.wait_chain("vector", "tensor", hold["mvb"])
            if st["pm1"]:
                p.wait_chain(st["pm1"][0], "tensor", st["pm1"][1])
            tok2 = p.t(lambda e: e.matmul(pmisc[1][64:128, 0:512], idb_sb[:],
                                          tmp_bf[:], start=True, stop=True,
                                          tile_position=(0, 64)))
            st["tmpbf_tok"] = tok2
            p.wait("vector", "mm", tok2)
            mc = p.v(lambda e: e.tensor_copy(h_sb[64:128, cols],
                                             pmisc[1][64:128, 0:512]))
            st["pm1"] = ("vector", mc)
            st["hmark"] = mc
        return [A, Bc] + ([Cc] if hl == 1 else [])

    def norm_closures(fc, final):
        cls = []
        order = ([(hl, q4) for hl in range(2) for q4 in range(4)] if not final
                 else [(hl, q4) for q4 in range(4) for hl in range(2)])
        for (hl, q4) in order:
            cls.extend(slot_closures(fc, hl, q4))
        if not final:
            def zz():
                for hl in range(2):
                    p.v(lambda e, hl=hl: e.memset(acc[hl][:], 0.0))
            cls.append(zz)
        return cls

    def after_run(s):
        fc, rb, qt4, _ = steps[s]
        r = s // 16
        p.wait("vector", "mm", av_tok[s])
        for hl in range(2):
            p.v(lambda e, hl=hl, qt4=qt4: e.tensor_tensor(
                out=acc[hl][:, qt4 * 512:(qt4 + 1) * 512],
                in0=acc[hl][:, qt4 * 512:(qt4 + 1) * 512],
                in1=pavt[hl][:][0:65, 0:512], op=ALU.add))
        accv_of_run[r] = p.chain_mark("vector")
        if qt4 == 3:
            kvs.release(kv_state[(fc, rb)][0], "mm", av_tok[s])
            if rb == NRB - 1 and fc < 7:
                extra.extend(norm_closures(fc, final=False))

    def emit_mmav(s):
        fc, rb, qt4, c = steps[s]
        kvb = kv_state[(fc, rb)][1]
        sp = s % 4
        p.wait_chain("scalar", "tensor", exp_mark[s])
        if c == 0 and s // 16 >= 1:
            p.wait_chain("vector", "tensor", accv_of_run[s // 16 - 1])

        def f(e, kvb=kvb, c=c, sp=sp):
            last = None
            for j in range(2):
                vsl = kvb[:, KOFF + (c * 2 + j) * 65: KOFF + (c * 2 + j + 1) * 65]
                last = e.matmul(pavt[j][0:65, 0:512], vsl,
                                expP[sp][:, j * 512:(j + 1) * 512],
                                start=(c == 0), stop=(c == 15))
            return last
        av_tok[s] = p.t(f)
        if c == 15:
            after_run(s)

    sc_tok = {}

    def emit_mmsc(s):
        fc, rb, qt4, c = steps[s]
        if qt4 == 0 and c == 0:
            ki, kvb, _ = kvs.load(kf_all[rb, :, :, fc, :], part=slice(0, KOFF))
            vsem, vtok = kvs.extra(ki, va_all[rb, :, :, 2 * fc:2 * fc + 2, :],
                                   part=slice(KOFF, KOFF + 16 * 2 * 65))
            kv_state[(fc, rb)] = (ki, kvb, vsem, vtok)
            if rb == 0:
                p.wait("tensor", f"qseg{fc}", 16)
            p.wait("tensor", vsem, vtok)
            if rb == NRB - 1 and fc < 7:
                p.dma("gpsimd", qt_sb[:, (fc + 1) * TPC:(fc + 2) * TPC],
                      flat(qt)[:, (fc + 1) * TPC:(fc + 2) * TPC],
                      sem_name=f"qseg{fc + 1}")
        kvb = kv_state[(fc, rb)][1]
        sidx = s % 2
        p.wait_chain("scalar", "tensor", set_guard[sidx])

        def mmsc(e, fc=fc, c=c, qt4=qt4, kvb=kvb, sidx=sidx):
            last = None
            for j in range(2):
                ksl = kvb[j * 64:j * 64 + 64, c * 128:(c + 1) * 128]
                qs = qt_sb[j * 64:j * 64 + 64,
                           fc * TPC + qt4 * 512: fc * TPC + (qt4 + 1) * 512]
                last = e.matmul(pscore[sidx][:, j * 512:(j + 1) * 512],
                                ksl, qs, start=True, stop=True)
            return last
        sc_tok[s] = p.t(mmsc)

    emit_mmsc(0)
    for s in range(NST):
        fc = steps[s][0]
        sidx = s % 2
        p.wait("scalar", "mm", sc_tok[s])
        expfn = (lambda e, sidx=sidx, sp=s % 4, fc=fc: e.activation(
            out=expP[sp][:], in_=pscore[sidx][:],
            func=AF.Exp, scale=np.float32(0.125),
            bias=mneg_sb[:, 2 * fc:2 * fc + 1]))
        mexp = p.s(expfn) if s < 2 else p.s_nc(expfn)
        set_guard[sidx] = mexp
        exp_mark[s] = mexp
        if s + 1 < NST:
            emit_mmsc(s + 1)
        if s > 0:
            emit_mmav(s - 1)
        if extra:
            extra.pop(0)()
    emit_mmav(NST - 1)
    fin = norm_closures(7, final=True)

    wp_banks = Banks(p, [pscore[0][:, 0:512], pscore[0][:, 512:1024],
                         pscore[1][:, 0:512]])
    for i in range(3):
        wp_banks.set_guard(i, "scalar", exp_mark[NST - 1])
    pGb = pscore[1][:, 512:1024]
    g3 = gtok_sb[:].rearrange("p (a b) -> p a b", b=17)[:, :, 0:15]
    w3 = gwork_sb[:].rearrange("p (a b) -> p a b", b=16)[:, :, 0:15]
    w3b = gwork2_sb[:].rearrange("p (a b) -> p a b", b=16)[:, :, 0:15]
    s3v = gtok_sb[:].rearrange("p (a b) -> p a b", b=17)[:, :, 15:17]

    def cw2(i):
        return colw[:, 4 * i:4 * (i + 1)]

    def cw1(i):
        return colw[:, 4 * i:4 * (i + 1)].unsqueeze(2)

    def cwb(i, n):
        return colw[:, 4 * i:4 * (i + 1)].unsqueeze(2).broadcast_to((128, 4, n))

    def emit_tail(tt4):
        sc0 = p.chain_mark("scalar")
        p.wait_chain("vector", "tensor", norm_state["hmark"])
        p.wait("tensor", "ldw", 32)
        for g_ in (norm_state["pm0"], norm_state["pm1"]):
            if g_:
                p.wait_chain(g_[0], "tensor", g_[1])

        def mmg(e, tt4=tt4):
            last = None
            for ic in range(8):
                last = e.matmul(pmisc[0][0:17, 0:512], wrs_sb[:, ic * 17:(ic + 1) * 17],
                                h_sb[:, ic * TPC + tt4 * 512: ic * TPC + (tt4 + 1) * 512],
                                start=(ic == 0), stop=(ic == 7))
            return last
        tokg = p.t(mmg)
        p.wait("vector", "mm", tokg)
        m2 = p.v(lambda e: e.tensor_single_scalar(out=glog_sb[:], in_=pmisc[0][0:17, 0:512],
                                                  scalar=brs_sb[:, 0:1], op=ALU.add))
        norm_state["pm0"] = ("vector", m2)
        p.wait_chain("vector", "tensor", m2)

        def mtrs(e):
            last = None
            for sub in range(4):
                ss = slice(sub * 128, (sub + 1) * 128)
                last = e.transpose(pavt[0][:][:, sub * 17:sub * 17 + 17], glog_sb[:, ss],
                                   id_sb[0:17, 0:17])
            return last
        tok = p.t(mtrs)
        p.wait("vector", "mm", tok)
        p.v(lambda e: e.tensor_copy(gtok_sb[:], pavt[0][:][:, 0:68]))
        p.v(lambda e: e.tensor_reduce(cw2(0), g3, axis=AX.X, op=ALU.max))
        mv = p.v(lambda e: e.tensor_tensor(out=w3, in0=g3, in1=cwb(0, 15),
                                           op=ALU.subtract))
        p.wait_chain("vector", "scalar", mv)
        mk2 = p.s(lambda e: e.activation(out=w3, in_=w3, func=AF.Exp))
        p.wait_chain("scalar", "vector", mk2)
        p.v(lambda e: e.tensor_reduce(cw2(1), w3, axis=AX.X, op=ALU.add))
        p.v(lambda e: e.reciprocal(cw2(1), cw2(1)))
        p.v(lambda e: e.tensor_tensor(out=g3, in0=w3, in1=cwb(1, 15), op=ALU.mult))
        p.v(lambda e: e.tensor_reduce(cw2(2), g3, axis=AX.X, op=ALU.max))
        p.v(lambda e: e.tensor_tensor(out=w3, in0=g3, in1=cwb(2, 15), op=ALU.is_ge))
        p.v(lambda e: e.scalar_tensor_tensor(out=w3, in0=w3, scalar=-2.0, in1=g3,
                                             op0=ALU.mult, op1=ALU.add))
        p.v(lambda e: e.tensor_reduce(cw2(2), w3, axis=AX.X, op=ALU.max))
        p.v(lambda e: e.tensor_tensor(out=w3b, in0=w3, in1=cwb(2, 15), op=ALU.is_ge))
        p.v(lambda e: e.scalar_tensor_tensor(out=w3, in0=w3b, scalar=-2.0, in1=w3,
                                             op0=ALU.mult, op1=ALU.add))
        p.v(lambda e: e.tensor_reduce(cw2(2), w3, axis=AX.X, op=ALU.max))
        p.v(lambda e: e.tensor_tensor(out=w3, in0=g3, in1=cwb(2, 15), op=ALU.is_ge))
        p.v(lambda e: e.tensor_mul(w3, w3, g3))
        p.v(lambda e: e.tensor_reduce(cw2(3), w3, axis=AX.X, op=ALU.add))
        p.v(lambda e: e.tensor_reduce(cw2(4), s3v, axis=AX.X, op=ALU.max))
        mv = p.v(lambda e: e.tensor_tensor(out=s3v, in0=s3v, in1=cwb(4, 2),
                                           op=ALU.subtract))
        p.wait_chain("vector", "scalar", mv)
        mk2 = p.s(lambda e: e.activation(out=s3v, in_=s3v, func=AF.Exp))
        p.wait_chain("scalar", "vector", mk2)
        p.v(lambda e: e.tensor_reduce(cw2(5), s3v, axis=AX.X, op=ALU.add))
        p.v(lambda e: e.reciprocal(cw2(5), cw2(5)))
        p.v(lambda e: e.tensor_tensor(out=cw1(6), in0=s3v[:, :, 1:2], in1=cw1(3),
                                      op=ALU.mult))
        p.v(lambda e: e.tensor_scalar_mul(cw2(6), cw2(6), 6.0))
        p.v(lambda e: e.scalar_tensor_tensor(out=cw1(6), in0=s3v[:, :, 0:1],
                                             scalar=2.0, in1=cw1(6),
                                             op0=ALU.mult, op1=ALU.add))
        mv = p.v(lambda e: e.tensor_tensor(out=cw1(6), in0=cw1(6), in1=cw1(5),
                                           op=ALU.mult))
        p.wait_chain("vector", "tensor", mv)

        def mtr2(e):
            last = None
            for sub in range(4):
                last = e.transpose(pavt[1][:][0:1, sub * 128:(sub + 1) * 128],
                                   colw[:, 24 + sub:25 + sub], id_sb[:])
            return last
        tok = p.t(mtr2)
        p.wait("scalar", "mm", tok)
        mk3 = p.s(lambda e: e.copy(out=grow_r[:], in_=pavt[1][:][0:1, 0:512]))
        p.wait_chain("scalar", "tensor", mk3)
        gb_tok = p.t(lambda e: e.matmul(pGb, ln["ones_row_r"][:], grow_r[:],
                                        start=True, stop=True))
        p.wait("vector", "mm", gb_tok)
        p.v(lambda e: e.tensor_copy(gb_sb[:], pGb))
        p.wait_chain("scalar", "vector", sc0)
        for oc in range(8):
            bi, pb = wp_banks.next()

            def mmp(e, oc=oc, tt4=tt4, pb=pb):
                last = None
                for ic in range(8):
                    last = e.matmul(pb, wp_sb[:, (oc * 8 + ic) * 128:(oc * 8 + ic + 1) * 128],
                                    h_sb[:, ic * TPC + tt4 * 512: ic * TPC + (tt4 + 1) * 512],
                                    start=(ic == 0), stop=(ic == 7))
                return last
            tok = p.t(mmp)
            sl34 = slice(oc * TPC + tt4 * 512, oc * TPC + (tt4 + 1) * 512)
            xi, xbuf, (xsem, xtok) = x3s.load(flat(x3)[:, sl34])
            p.wait("vector", "mm", tok)
            mv1 = p.v(lambda e, pb=pb: e.tensor_tensor(out=pr_sb[:], in0=pb,
                                                       in1=gb_sb[:], op=ALU.mult))
            wp_banks.set_guard(bi, "vector", mv1)
            p.wait("vector", xsem, xtok)
            mv2 = p.v(lambda e, oc=oc, xbuf=xbuf: e.scalar_tensor_tensor(
                out=r4_sb[:, oc * 512:(oc + 1) * 512], in0=pr_sb[:],
                scalar=bp_sb[:, oc:oc + 1], in1=xbuf[:], op0=ALU.add, op1=ALU.add))
            x3s.release(xi, "chain_vector", mv2)
        vm = p.chain_mark("vector")
        p.wait_chain("vector", "tensor", vm)
        p.wait_chain("vector", "scalar", vm)
        emit_layernorm(p, r4_sb, ln, None, None, None, out_out,
                       nchunk=8, tpc=512,
                       ptiles=(pmisc[0][:], pmisc[1][:], pavt[0][:], pavt[1][:]),
                       out_fn=lambda c, tt, tt4=tt4: (None, flat(o_out)[
                           :, c * TPC + tt4 * 512: c * TPC + (tt4 + 1) * 512]))
        vm2 = p.chain_mark("vector")
        norm_state["pm0"] = ("vector", vm2)
        norm_state["pm1"] = ("vector", vm2)

    idx = 0
    for tt4 in range(4):
        for _ in range(5):
            fin[idx]()
            idx += 1
        emit_tail(tt4)
    out_out.flush_wait("gpsimd")
    return p.finalize()


_NC_CACHE = {}


def _get(name, builder):
    if name not in _NC_CACHE:
        _NC_CACHE[name] = builder()
    return _NC_CACHE[name]


def _ones2():
    return np.ones((128, 128), dtype=np.float32)


def run_l0_attn(inputs, xpm):
    nc1 = _get("l1", build_l0attn_pre)
    im1 = [dict(x_in=xpm[c],
                wq=_wblk(inputs["L0_attn_wq_w"]), wk=_wblk(inputs["L0_attn_wk_w"]),
                wkp=_wplain(inputs["L0_attn_wk_w"]), wvp=_wplain(inputs["L0_attn_wv_w"]),
                bq=_pcol(inputs["L0_attn_wq_b"]), bk=_pcol(inputs["L0_attn_wk_b"]),
                bk_row=_prow(inputs["L0_attn_wk_b"]), bv_row=_prow(inputs["L0_attn_wv_b"]))
           for c in range(NCORE)]
    r1 = run_bass_kernel_spmd(nc1, im1, core_ids=list(range(NCORE))).results

    nc2 = _get("l2", build_l0attn_main)
    st = [np.asarray(r1[c]["o_st"]) for c in range(NCORE)]
    zero_st = np.zeros_like(st[0])
    tri = np.triu(np.ones((128, 128), dtype=np.float32))
    ident = np.eye(64, dtype=np.float32)
    woT = _wblk(inputs["L0_attn_wo_w"])
    bo = _pcol(inputs["L0_attn_wo_b"])
    im2 = []
    for c in range(NCORE):
        b, k = divmod(c, 4)
        preds = [st[b * 4 + kk] for kk in range(k)]
        while len(preds) < 3:
            preds.append(zero_st)
        sp = np.stack(preds, axis=1)
        s_pred = np.ascontiguousarray(np.concatenate([sp, sp], axis=0))
        im2.append(dict(x_in=xpm[c], qf=r1[c]["o_qf"], kf=r1[c]["o_kf"],
                        kt=r1[c]["o_kt"], vt=r1[c]["o_vt"],
                        s_pred=s_pred, wo=woT, bo=bo,
                        trimask=tri, ident=ident, ones2=_ones2()))
    r2 = run_bass_kernel_spmd(nc2, im2, core_ids=list(range(NCORE))).results
    return r2


def kernel(**inputs):
    x = np.asarray(inputs["L0_attn_x"], dtype=np.float32)
    xpm = []
    for c in range(NCORE):
        b, k = divmod(c, 4)
        xpm.append(_pm_feat(x[b, k * TPC:(k + 1) * TPC, :]))

    r2 = run_l0_attn(inputs, xpm)

    nc3 = _get("l3", build_l0ffn)
    im3 = [dict(x1=np.asarray(r2[c]["o_x1"]), x1bf=np.asarray(r2[c]["o_x1bf"]),
                w1=_wblk(inputs["L0_ffn_w1_w"]), w2=_wblk(inputs["L0_ffn_w2_w"]),
                h1=_wblk(inputs["L0_ffn_h1_w"]), h2=_wblk(inputs["L0_ffn_h2_w"]),
                w3=_wblk(inputs["L0_ffn_w3_w"]),
                s1=_pcol(inputs["L0_ffn_w1_s"]), s2=_pcol(inputs["L0_ffn_w2_s"]),
                h1s=_pcol(inputs["L0_ffn_h1_s"]), h2s=_pcol(inputs["L0_ffn_h2_s"]),
                s3=_pcol(inputs["L0_ffn_w3_s"]), ones2=_ones2())
           for c in range(NCORE)]
    r3 = run_bass_kernel_spmd(nc3, im3, core_ids=list(range(NCORE))).results

    nc4 = _get("l4", build_l1front)
    dw = inputs["L1_attn_dw_kernel"]
    dwk_pm = np.ascontiguousarray(np.stack([_pcol(dw[j]) for j in range(4)], axis=1))
    zero3 = np.zeros((128, 8, 3), dtype=BFD)
    im4 = []
    for c in range(NCORE):
        b, k = divmod(c, 4)
        if k == 0:
            xprev = zero3
        else:
            xprev = np.ascontiguousarray(np.asarray(r3[b * 4 + k - 1]["o_x2bf"])[:, :, TPC - 3:TPC])
        im4.append(dict(x2bf=np.asarray(r3[c]["o_x2bf"]), xprev=xprev,
                        w_in=_wblkT(inputs["L1_attn_W_in"]), dwk=dwk_pm,
                        pw=_wblkT(inputs["L1_attn_pw_kernel"]),
                        cbias=_pcol(inputs["L1_attn_conv_bias"]),
                        w_x=_wplainT(inputs["L1_attn_W_x"]),
                        w_dt=np.ascontiguousarray(inputs["L1_attn_W_dt"], dtype=np.float32),
                        bdt=_pcol(inputs["L1_attn_b_dt"]),
                        ident=np.eye(128, dtype=np.float32)))
    r4 = run_bass_kernel_spmd(nc4, im4, core_ids=list(range(NCORE))).results

    nc5 = _get("l5", build_scan)
    alog = np.asarray(inputs["L1_attn_A_log"], dtype=np.float32)
    im5 = []
    for s in range(NCORE):
        dd = np.empty((128, 2, 2, L), dtype=np.float32)
        xx = np.empty((128, 2, 2, L), dtype=BFD)
        for dci in range(2):
            oc = 2 * s + dci
            for b in range(2):
                for k in range(4):
                    dd[:, dci, b, k * TPC:(k + 1) * TPC] = np.asarray(r4[b * 4 + k]["o_delta"])[:, oc, :]
                    xx[:, dci, b, k * TPC:(k + 1) * TPC] = np.asarray(r4[b * 4 + k]["o_xc"])[:, oc, :]
        bm = np.empty((2, NSTATE, L), dtype=np.float32)
        cm = np.empty((2, NSTATE, L), dtype=np.float32)
        for b in range(2):
            for k in range(4):
                bc = np.asarray(r4[b * 4 + k]["o_bc"])
                bm[b, :, k * TPC:(k + 1) * TPC] = bc[0:8]
                cm[b, :, k * TPC:(k + 1) * TPC] = bc[8:16]
        al = np.ascontiguousarray(
            alog[s * 256:(s + 1) * 256].reshape(2, 128, NSTATE).transpose(1, 0, 2))
        im5.append(dict(delta_in=np.ascontiguousarray(dd), xc_in=np.ascontiguousarray(xx),
                        bmat=bm, cmat=cm, alog=al))
    r5 = run_bass_kernel_spmd(nc5, im5, core_ids=list(range(NCORE))).results

    nc6 = _get("l6a", build_l1back)
    im6 = []
    for c in range(NCORE):
        b, k = divmod(c, 4)
        yy = np.empty((128, 16, TPC), dtype=BFD)
        for oc in range(16):
            yy[:, oc, :] = np.asarray(r5[oc // 2]["o_y"])[:, oc % 2, b, k * TPC:(k + 1) * TPC]
        im6.append(dict(y_in=np.ascontiguousarray(yy), xc_in=np.asarray(r4[c]["o_xc"]),
                        z_in=np.asarray(r4[c]["o_z"]), dvec=_pcol(inputs["L1_attn_D"]),
                        w_out=_wblkT(inputs["L1_attn_W_out"]),
                        x2=np.asarray(r3[c]["o_x2"]), ones2=_ones2()))
    r6 = run_bass_kernel_spmd(nc6, im6, core_ids=list(range(NCORE))).results

    nc6b = _get("l6b", build_qkv)
    esel = np.zeros((16, 128), dtype=np.float32)
    for h in range(16):
        esel[h, (h % 2) * 64:(h % 2) * 64 + 64] = 1.0
    hmask = np.zeros((16, 8), dtype=np.float32)
    for h in range(16):
        hmask[h, h // 2] = 1.0
    jsel = np.zeros((2, 128), dtype=np.float32)
    jsel[0, 0:64] = 1.0
    jsel[1, 64:128] = 1.0
    temp = np.ascontiguousarray(np.asarray(inputs["L1_ffn_temperature"],
                                           dtype=np.float32).reshape(1, 16))
    qe_pcol = _pcol(np.asarray(inputs["L1_ffn_query_embedding"], dtype=np.float32).reshape(1024))
    im6b = [dict(x3bf=np.asarray(r6[c]["o_x3bf"]),
                 wq=_wblk(inputs["L1_ffn_Wq"]), wk=_wblk(inputs["L1_ffn_Wk"]),
                 wvp=_wplain(inputs["L1_ffn_Wv"]),
                 bq=_pcol(inputs["L1_ffn_bq"]), bk=_pcol(inputs["L1_ffn_bk"]),
                 bv_row=_prow(inputs["L1_ffn_bv"]), temp_in=temp, qe_pcol=qe_pcol,
                 esel=esel, hmask=hmask, jsel=jsel,
                 hsel=np.ascontiguousarray(jsel.T), ones2=_ones2())
            for c in range(NCORE)]
    r6b = run_bass_kernel_spmd(nc6b, im6b, core_ids=list(range(NCORE))).results

    nc7 = _get("l7", build_flash)
    def headorder(nm):
        out = np.empty(32, dtype=np.float32)
        for h in range(16):
            out[h] = nm[h % 2, h // 2]
            out[16 + h] = nm[h % 2, 8 + h // 2]
        return out
    norms = [headorder(np.asarray(r6b[c]["o_norms"])) for c in range(NCORE)]
    im7 = []
    for c in range(NCORE):
        b, k = divmod(c, 4)
        grp = [b * 4 + kk for kk in range(4)]
        kf_all = np.ascontiguousarray(np.stack([np.asarray(r6b[g]["o_kf"]) for g in grp]))
        va_all = np.ascontiguousarray(np.stack([np.asarray(r6b[g]["o_vaug"]) for g in grp]))
        order = [c] + [g for g in grp if g != c]
        norms_all = np.ascontiguousarray(np.stack([norms[g] for g in order]))
        im7.append(dict(qt=np.asarray(r6b[c]["o_qt"]), kf_all=kf_all, va_all=va_all,
                        norms_all=norms_all, x3=np.asarray(r6[c]["o_x3"]),
                        wrsT=np.ascontiguousarray(np.concatenate([
                            _wplainT(np.ascontiguousarray(
                                np.asarray(inputs["L1_ffn_Wr"], dtype=np.float32).T)),
                            _wplainT(np.ascontiguousarray(
                                np.asarray(inputs["L1_ffn_Ws"], dtype=np.float32).T))],
                            axis=2)),
                        brs=np.ascontiguousarray(np.concatenate([
                            np.asarray(inputs["L1_ffn_br"], dtype=np.float32).reshape(15, 1),
                            np.asarray(inputs["L1_ffn_bs"], dtype=np.float32).reshape(2, 1)])),
                        wp=_wblk(inputs["L1_ffn_Wp"]), bp=_pcol(inputs["L1_ffn_bp"]),
                        id128=np.eye(128, dtype=np.float32), ones2=_ones2()))
    r7 = run_bass_kernel_spmd(nc7, im7, core_ids=list(range(NCORE))).results

    out = np.empty((B, L, D), dtype=np.float32)
    for c in range(NCORE):
        b, k = divmod(c, 4)
        oo = np.asarray(r7[c]["o_out"])
        out[b, k * TPC:(k + 1) * TPC, :] = oo.transpose(1, 0, 2).reshape(D, TPC).T
    return out


# version 19
# speedup vs baseline: 1.1912x; 1.1912x over previous
import numpy as np
import ml_dtypes
from contextlib import ExitStack

import concourse.bass as bass
import concourse.mybir as mybir
from concourse.bass_utils import run_bass_kernel_spmd

F32 = mybir.dt.float32
F32R = mybir.dt.float32r
BF16 = mybir.dt.bfloat16
AF = mybir.ActivationFunctionType
ALU = mybir.AluOpType
AX = mybir.AxisListType
BFD = ml_dtypes.bfloat16

NCORE = 8
B, L, D = 2, 8192, 1024
TPC = 2048
NH, DH = 16, 64
DFF = 4096
DIN = 2048
NSTATE = 8
DTR = 64
EPS = 1e-5

ENGS = ("sync", "gpsimd", "scalar", "vector", "tensor")


class P:

    def __init__(self, npsum=8):
        self.nc = bass.Bass("TRN2", target_bir_lowering=False)
        self.ctx = ExitStack()
        self.ops = {e: [] for e in ENGS}
        self.sems = {}
        self.cnt = {}
        self.chain_cnt = {e: 0 for e in ENGS}
        self.pp = [self.ps(f"pp{i}", [128, 512], F32) for i in range(npsum)]

    def sem(self, name):
        if name not in self.sems:
            self.sems[name] = self.ctx.enter_context(self.nc.semaphore(name))
            self.cnt[name] = 0
        return self.sems[name]

    def sb(self, name, shape, dt):
        return self.ctx.enter_context(self.nc.sbuf_tensor(name, shape, dt))

    def ps(self, name, shape, dt=F32):
        return self.ctx.enter_context(self.nc.psum_tensor(name, shape, dt))

    def din(self, name, shape, dt):
        return self.nc.dram_tensor(name, shape, dt, kind="ExternalInput").ap()

    def dout(self, name, shape, dt):
        return self.nc.dram_tensor(name, shape, dt, kind="ExternalOutput").ap()

    def on(self, eng, fn):
        self.ops[eng].append(fn)

    def wait(self, eng, name, val):
        if val <= 0:
            return
        s = self.sem(name)
        self.ops[eng].append(lambda e: e.wait_ge(s, val))

    def _emit(self, eng, build):
        cs = self.sem("chain_" + eng)
        prior = self.chain_cnt[eng]
        self.chain_cnt[eng] += 1

        def fn(e, prior=prior):
            e.wait_ge(cs, prior)
            build(e).then_inc(cs, 1)
        self.ops[eng].append(fn)
        return self.chain_cnt[eng]

    def v(self, build):
        return self._emit("vector", build)

    def s(self, build):
        return self._emit("scalar", build)

    def s_nc(self, build):
        cs = self.sem("chain_scalar")
        self.chain_cnt["scalar"] += 1

        def fn(e):
            build(e).then_inc(cs, 1)
        self.ops["scalar"].append(fn)
        return self.chain_cnt["scalar"]

    def g(self, build):
        return self._emit("gpsimd", build)

    def chain_mark(self, eng):
        return self.chain_cnt[eng]

    def wait_chain(self, eng, waiter_eng, val):
        if val <= 0:
            return
        cs = self.sem("chain_" + eng)
        self.ops[waiter_eng].append(lambda e: e.wait_ge(cs, val))

    def t(self, build):
        s = self.sem("mm")
        self.cnt["mm"] += 1
        val = self.cnt["mm"]

        def fn(e):
            last = build(e)
            (last if last is not None else e.nop()).then_inc(s, 1)
        self.ops["tensor"].append(fn)
        return val

    def dma(self, q, out, in_, sem_name=None, **kw):
        name = sem_name or ("dma_" + q)
        s = self.sem(name)
        self.cnt[name] += 16
        val = self.cnt[name]

        def fn(e):
            e.dma_start(out=out, in_=in_, **kw).then_inc(s, 16)
        self.ops[q].append(fn)
        return val

    def dma_mark(self, q, sem_name=None):
        name = sem_name or ("dma_" + q)
        self.sem(name)
        return self.cnt[name]

    def finalize(self):
        blk = self.ctx.enter_context(self.nc.Block())
        for eng in ENGS:
            fns = self.ops[eng]
            getattr(blk, eng)(lambda e, fns=fns: [f(e) for f in fns])
        self.ctx.close()
        return self.nc


class Banks:

    def __init__(self, p, tiles):
        self.p = p
        self.tiles = tiles
        self.guard = [None] * len(tiles)
        self.i = 0

    def next(self):
        i = self.i % len(self.tiles)
        self.i += 1
        if self.guard[i]:
            eng, val = self.guard[i]
            self.p.wait_chain(eng, "tensor", val)
        return i, self.tiles[i]

    def set_guard(self, i, eng, val):
        self.guard[i] = (eng, val)


class OutStream:

    def __init__(self, p, name, tile_shape, dt, nbuf=4, q="gpsimd"):
        self.p = p
        self.q = q
        self.nbuf = nbuf
        self.name = name
        self.bufs = [p.sb(f"{name}_st{i}", tile_shape, dt) for i in range(nbuf)]
        self.uses = [0] * nbuf
        self.k = 0

    def put(self, evac_build, dram_ap, part=None):
        p = self.p
        i = self.k % self.nbuf
        buf = self.bufs[i]
        sname = f"dst_{self.name}_{i}"
        p.wait("scalar", sname, self.uses[i] * 16)
        mark = p.s(lambda e, buf=buf: evac_build(e, buf))
        p.wait_chain("scalar", self.q, mark)
        src = buf[:] if part is None else buf[:, part]
        p.dma(self.q, dram_ap, src, sem_name=sname)
        self.uses[i] += 1
        self.k += 1

    def flush_wait(self, eng):
        for i in range(self.nbuf):
            self.p.wait(eng, f"dst_{self.name}_{i}", self.uses[i] * 16)


class InStream:

    def __init__(self, p, name, tile_shape, dt, nbuf=2, q="sync"):
        self.p = p
        self.q = q
        self.nbuf = nbuf
        self.name = name
        self.bufs = [p.sb(f"{name}_ld{i}", tile_shape, dt) for i in range(nbuf)]
        self.k = 0
        self.guard = [None] * nbuf

    def _sem(self, i):
        return f"lds_{self.name}_{i}"

    def load(self, src_ap, part=None):
        p = self.p
        i = self.k % self.nbuf
        if self.guard[i]:
            sname, val = self.guard[i]
            p.wait(self.q, sname, val)
        dst = self.bufs[i][:] if part is None else self.bufs[i][:, part]
        tok = p.dma(self.q, dst, src_ap, sem_name=self._sem(i))
        self.k += 1
        return i, self.bufs[i], (self._sem(i), tok)

    def extra(self, i, src_ap, part):
        tok = self.p.dma(self.q, self.bufs[i][:, part], src_ap,
                         sem_name=self._sem(i))
        return (self._sem(i), tok)

    def release(self, i, sem_name, val):
        self.guard[i] = (sem_name, val)


def _pm_feat(sl):
    T, Dp = sl.shape
    return np.ascontiguousarray(
        np.asarray(sl, dtype=np.float32).T.reshape(Dp // 128, 128, T).transpose(1, 0, 2))


def _wblkT(wT):
    I, O = wT.shape
    wT = np.asarray(wT, dtype=np.float32)
    return np.ascontiguousarray(wT.reshape(I // 128, 128, O // 128, 128).transpose(1, 2, 0, 3))


def _wblk(w):
    return _wblkT(np.asarray(w, dtype=np.float32).T)


def _wplainT(wT):
    I, O = wT.shape
    wT = np.asarray(wT, dtype=np.float32)
    return np.ascontiguousarray(wT.reshape(I // 128, 128, O).transpose(1, 0, 2))


def _wplain(w):
    return _wplainT(np.asarray(w, dtype=np.float32).T)


def _pcol(vec):
    v = np.asarray(vec, dtype=np.float32)
    C = v.shape[0] // 128
    return np.ascontiguousarray(v.reshape(C, 128).T)


def _prow(vec):
    return np.ascontiguousarray(np.asarray(vec, dtype=np.float32).reshape(1, -1))


def flat(ap):
    if ap.ndim == 2:
        return ap
    if ap.ndim == 3:
        return ap.rearrange("p a b -> p (a b)")
    if ap.ndim == 4:
        return ap.rearrange("p a b c -> p (a b c)")
    raise ValueError(ap.shape)


def build_l0attn_pre():
    p = P()
    x_in = p.din("x_in", [128, 8, TPC], F32)
    wq = p.din("wq", [128, 8, 8, 128], F32)
    wk = p.din("wk", [128, 8, 8, 128], F32)
    wkp = p.din("wkp", [128, 8, D], F32)
    wvp = p.din("wvp", [128, 8, D], F32)
    bq = p.din("bq", [128, 8], F32)
    bk = p.din("bk", [128, 8], F32)
    bk_row = p.din("bk_row", [1, D], F32)
    bv_row = p.din("bv_row", [1, D], F32)

    o_qf = p.dout("o_qf", [128, 16, 8, 128], BF16)
    o_kf = p.dout("o_kf", [128, 16, 8, 128], BF16)
    o_kt = p.dout("o_kt", [128, 16, D], BF16)
    o_vt = p.dout("o_vt", [128, 16, D], BF16)
    o_st = p.dout("o_st", [64, NH, DH], F32)

    x_bf = p.sb("x_bf", [128, 8 * TPC], BF16)
    wq_sb = p.sb("wq_sb", [128, 64 * 128], BF16)
    wk_sb = p.sb("wk_sb", [128, 64 * 128], BF16)
    wkp_sb = p.sb("wkp_sb", [128, 8 * D], BF16)
    wvp_sb = p.sb("wvp_sb", [128, 8 * D], BF16)
    bq_sb = p.sb("bq_sb", [128, 8], F32)
    bk_sb = p.sb("bk_sb", [128, 8], F32)
    bkr_sb = p.sb("bkr_sb", [1, D], BF16)
    bvr_sb = p.sb("bvr_sb", [1, D], BF16)
    ones_sb = p.sb("ones_sb", [1, 128], BF16)
    kt_sb = p.sb("kt_sb", [128, 16 * D], BF16)
    vt_sb = p.sb("vt_sb", [128, 16 * D], BF16)
    st_sb = p.sb("st_sb", [64, NH * DH], F32)

    lv = 0
    for dst, src in ((x_bf[:], x_in), (wq_sb[:], wq), (wk_sb[:], wk),
                     (wkp_sb[:], wkp), (wvp_sb[:], wvp),
                     (bkr_sb[:], bk_row), (bvr_sb[:], bv_row)):
        lv = p.dma("gpsimd", dst, flat(src))
    ls = p.dma("sync", bq_sb[:], bq, sem_name="lds")
    ls = p.dma("sync", bk_sb[:], bk, sem_name="lds")
    p.v(lambda e: e.memset(ones_sb[:], 1.0))
    ones_mark = p.chain_mark("vector")

    p.wait("tensor", "dma_gpsimd", lv)
    p.wait("tensor", "lds", ls)
    p.wait_chain("vector", "tensor", ones_mark)
    p.wait("scalar", "lds", ls)

    banks = Banks(p, [pb[:, 0:512] for pb in p.pp[0:4]])
    qf_out = OutStream(p, "qf", [128, 512], BF16)
    kf_out = OutStream(p, "kf", [128, 512], BF16)

    for w_sb, bias_col, ostream, oname in ((wq_sb, bq_sb, qf_out, o_qf),
                                           (wk_sb, bk_sb, kf_out, o_kf)):
        for oc in range(8):
            for tt in range(4):
                bi, pb = banks.next()

                def mm(e, oc=oc, tt=tt, w_sb=w_sb, pb=pb):
                    last = None
                    for ic in range(8):
                        last = e.matmul(pb, w_sb[:, (oc * 8 + ic) * 128:(oc * 8 + ic + 1) * 128],
                                 x_bf[:, ic * TPC + tt * 512: ic * TPC + (tt + 1) * 512],
                                 start=(ic == 0), stop=(ic == 7))
                    return last
                tok = p.t(mm)
                p.wait("scalar", "mm", tok)

                def ev(e, buf, pb=pb, bias_col=bias_col, oc=oc):
                    return e.activation(out=buf[:], in_=pb, func=AF.Identity,
                                        bias=bias_col[:, oc:oc + 1])
                ostream.put(ev, oname[:, tt * 4:(tt + 1) * 4, oc, :])
                banks.set_guard(bi, "scalar", p.chain_mark("scalar"))

    for w_sb, out_sb, brow, oname in ((wkp_sb, kt_sb, bkr_sb, o_kt),
                                      (wvp_sb, vt_sb, bvr_sb, o_vt)):
        for tc in range(16):
            for oh in range(2):
                bi, pb = banks.next()

                def mm(e, tc=tc, oh=oh, w_sb=w_sb, brow=brow, pb=pb):
                    for ic in range(8):
                        e.matmul(pb, x_bf[:, ic * TPC + tc * 128: ic * TPC + (tc + 1) * 128],
                                 w_sb[:, ic * D + oh * 512: ic * D + (oh + 1) * 512],
                                 start=(ic == 0), stop=False)
                    return e.matmul(pb, ones_sb[:], brow[:, oh * 512:(oh + 1) * 512],
                             start=False, stop=True)
                tok = p.t(mm)
                p.wait("scalar", "mm", tok)
                mark = p.s(lambda e, tc=tc, oh=oh, out_sb=out_sb, pb=pb:
                           e.copy(out=out_sb[:, tc * D + oh * 512: tc * D + (oh + 1) * 512],
                                  in_=pb))
                banks.set_guard(bi, "scalar", mark)
        mark = p.chain_mark("scalar")
        p.wait_chain("scalar", "gpsimd", mark)
        p.dma("gpsimd", flat(oname), out_sb[:], sem_name="dst_kv")

    kv_mark = p.chain_mark("scalar")
    p.wait_chain("scalar", "tensor", kv_mark)
    pS = [p.pp[4][0:64, 0:512], p.pp[5][0:64, 0:512]]

    def smm(e):
        last = None
        for h in range(NH):
            pb = pS[h // 8]
            off = (h % 8) * DH
            for c in range(16):
                last = e.matmul(pb[:, off:off + DH],
                         kt_sb[:, c * D + h * DH: c * D + (h + 1) * DH],
                         vt_sb[:, c * D + h * DH: c * D + (h + 1) * DH],
                         start=(c == 0), stop=(c == 15))
        return last
    tok = p.t(smm)
    p.wait("scalar", "mm", tok)
    p.s(lambda e: e.copy(out=st_sb[:, 0:512], in_=pS[0]))
    p.s(lambda e: e.copy(out=st_sb[:, 512:1024], in_=pS[1]))
    mark = p.chain_mark("scalar")
    p.wait_chain("scalar", "gpsimd", mark)
    last = p.dma("gpsimd", flat(o_st), st_sb[:], sem_name="dst_kv")
    qf_out.flush_wait("gpsimd")
    kf_out.flush_wait("gpsimd")
    p.wait("gpsimd", "dst_kv", last)
    return p.finalize()


def build_l0attn_main():
    p = P()
    x_in = p.din("x_in", [128, 8, TPC], F32)
    qf = p.din("qf", [128, 16, 8, 128], BF16)
    kf = p.din("kf", [128, 16, 8, 128], BF16)
    kt = p.din("kt", [128, 16, D], BF16)
    vt = p.din("vt", [128, 16, D], BF16)
    s_pred = p.din("s_pred", [128, 3, NH, DH], F32)
    wo = p.din("wo", [128, 8, 8, 128], F32)
    bo = p.din("bo", [128, 8], F32)
    trimask = p.din("trimask", [128, 128], F32)
    ident = p.din("ident", [64, 64], F32)
    ones2 = p.din("ones2", [128, 128], F32)

    o_x1 = p.dout("o_x1", [128, 8, TPC], F32)
    o_x1bf = p.dout("o_x1bf", [128, 8, TPC], BF16)

    spred_sb = p.sb("spred_sb", [128, 3 * NH * DH], F32)
    wo_sb = p.sb("wo_sb", [128, 64 * 128], BF16)
    bo_sb = p.sb("bo_sb", [128, 8], F32)
    tri_sb = p.sb("tri_sb", [128, 128], F32)
    id_sb = p.sb("id_sb", [64, 64], BF16)
    sacc = [p.sb(f"sacc{i}", [128, NH * DH], F32) for i in range(2)]
    scur_sb = p.sb("scur_sb", [128, NH * DH], BF16)
    masked_sb = p.sb("masked_sb", [128, 2 * 128], BF16)
    attn_sb = p.sb("attn_sb", [128, 8 * TPC], BF16)
    r1_sb = p.sb("r1_sb", [128, 8 * TPC], F32)
    xs = InStream(p, "xs", [128, 512], F32, nbuf=2, q="sync")
    qks = InStream(p, "qks", [128, 2 * 8 * 128], BF16, nbuf=2, q="sync")
    kvs = InStream(p, "kvs", [128, 2 * D], BF16, nbuf=2, q="gpsimd")

    ln = dict(
        ones_col_r=p.sb("onescol", [128, 1], F32R),
        ones_row_r=p.sb("onesrow", [1, 128], F32R),
        ones_col_f=p.sb("onescolf", [128, 1], F32),
        sq_r=p.sb("sq_r", [128, 2 * 512], F32R),
        rows=p.sb("rows", [1, 4 * 512], F32),
        rrow=p.sb("rrow", [1, 2 * 512], F32R),
        tmp32=p.sb("tmp32", [128, 512], F32),
    )
    x1bf_out = OutStream(p, "x1bf", [128, 512], BF16)
    x1_out = OutStream(p, "x1f", [128, 512], F32)

    lv = 0
    for dst, src in ((spred_sb[:], s_pred), (wo_sb[:], wo)):
        lv = p.dma("gpsimd", dst, flat(src))
    lv = p.dma("gpsimd", ln["ones_col_r"][:], ones2[:, 0:1],
               allow_slow_non_contiguous=True)
    lv = p.dma("gpsimd", ln["ones_row_r"][:], ones2[0:1, :])
    lv = p.dma("gpsimd", id_sb[:], ident)
    ls = p.dma("sync", bo_sb[:], bo, sem_name="lds")
    ls = p.dma("sync", tri_sb[:], trimask, sem_name="lds")
    ls = p.dma("sync", ln["ones_col_f"][:], ones2[:, 0:1], sem_name="lds",
               allow_slow_non_contiguous=True)
    for eng in ("tensor", "vector", "scalar"):
        p.wait(eng, "dma_gpsimd", lv)
        p.wait(eng, "lds", ls)

    p.v(lambda e: e.tensor_add(sacc[0][:], spred_sb[:, 0:NH * DH],
                               spred_sb[:, NH * DH:2 * NH * DH]))
    p.v(lambda e: e.tensor_add(sacc[1][:], sacc[0][:],
                               spred_sb[:, 2 * NH * DH:3 * NH * DH]))
    vmark = p.chain_mark("vector")
    p.wait_chain("vector", "scalar", vmark)
    p.s(lambda e: e.copy(out=scur_sb[:], in_=sacc[1][:]))
    smark = p.chain_mark("scalar")
    p.wait_chain("scalar", "tensor", smark)

    pSc = [p.pp[0][:, 0:128], p.pp[1][:, 0:128]]
    pOut = [p.pp[2][:, 0:128], p.pp[3][:, 0:128]]
    pS = [p.pp[4][:, 0:512], p.pp[5][:, 0:512]]
    S_sb = sacc[1]

    sc_guard = [0, 0]
    out_guard = [0, 0]
    for c in range(16):
        qi, qkbuf, _ = qks.load(
            qf[:, c:c + 1, :, :].rearrange("p a f t -> p (a f t)"),
            part=slice(0, 8 * 128))
        qsem, qktok = qks.extra(qi, kf[:, c:c + 1, :, :].rearrange("p a f t -> p (a f t)"),
                                part=slice(8 * 128, 2 * 8 * 128))
        ki, kvbuf, _ = kvs.load(kt[:, c:c + 1, :].rearrange("p a f -> p (a f)"),
                                part=slice(0, D))
        ksem, kvtok = kvs.extra(ki, vt[:, c:c + 1, :].rearrange("p a f -> p (a f)"),
                                part=slice(D, 2 * D))
        p.wait("tensor", qsem, qktok)
        p.wait("tensor", ksem, kvtok)

        for hp in range(8):
            fc = hp
            for j in range(2):
                p.wait_chain("vector", "tensor", sc_guard[j])

            def mm_pair(e, c=c, fc=fc, qkbuf=qkbuf):
                ksl = qkbuf[:, 8 * 128 + fc * 128:8 * 128 + (fc + 1) * 128]
                qsl = qkbuf[:, fc * 128:(fc + 1) * 128]
                last = None
                for j in range(2):
                    last = e.matmul(pSc[j], ksl[j * 64:j * 64 + 64, :], qsl[j * 64:j * 64 + 64, :],
                             start=True, stop=True)
                return last
            tok = p.t(mm_pair)
            p.wait("vector", "mm", tok)
            for j in range(2):
                p.v(lambda e, j=j: e.tensor_tensor(
                    out=masked_sb[:, j * 128:(j + 1) * 128], in0=pSc[j],
                    in1=tri_sb[:], op=ALU.mult))
            vmark = p.chain_mark("vector")
            sc_guard = [vmark, vmark]
            p.wait_chain("vector", "tensor", vmark)
            p.wait_chain("scalar", "tensor", out_guard[c % 2])

            def mm_out(e, c=c, hp=hp, fc=fc, qkbuf=qkbuf, kvbuf=kvbuf):
                pb = pOut[c % 2]
                qsl = qkbuf[:, fc * 128:(fc + 1) * 128]
                last = None
                for j in range(2):
                    h = 2 * hp + j
                    e.matmul(pb[j * 64:j * 64 + 64, :],
                             kvbuf[:, D + h * DH:D + (h + 1) * DH],
                             masked_sb[:, j * 128:(j + 1) * 128],
                             start=True, stop=False, tile_position=(0, j * 64))
                    last = e.matmul(pb[j * 64:j * 64 + 64, :],
                             scur_sb[j * 64:j * 64 + 64, h * DH:(h + 1) * DH],
                             qsl[j * 64:j * 64 + 64, :],
                             start=False, stop=True, tile_position=(j * 64, j * 64))
                return last
            tok = p.t(mm_out)
            p.wait("scalar", "mm", tok)
            mark = p.s(lambda e, c=c, fc=fc: e.copy(
                out=attn_sb[:, fc * TPC + c * 128: fc * TPC + (c + 1) * 128],
                in_=pOut[c % 2]))
            out_guard[c % 2] = mark

        if c < 15:
            def mm_supd(e, c=c, kvbuf=kvbuf):
                last = None
                for h in range(NH):
                    pb = pS[h // 8]
                    off = (h % 8) * DH
                    e.matmul(pb[0:64, off:off + DH],
                             kvbuf[:, h * DH:(h + 1) * DH],
                             kvbuf[:, D + h * DH:D + (h + 1) * DH],
                             start=True, stop=True, tile_position=(0, 0))
                    last = e.matmul(pb[64:128, off:off + DH],
                             kvbuf[:, h * DH:(h + 1) * DH],
                             kvbuf[:, D + h * DH:D + (h + 1) * DH],
                             start=True, stop=True, tile_position=(0, 64))
                return last
            tok = p.t(mm_supd)
            qks.release(qi, "mm", tok)
            kvs.release(ki, "mm", tok)
            p.wait("vector", "mm", tok)
            p.v(lambda e: e.tensor_add(S_sb[:, 0:512], S_sb[:, 0:512], pS[0]))
            mv = p.v(lambda e: e.tensor_add(S_sb[:, 512:1024], S_sb[:, 512:1024], pS[1]))
            p.wait_chain("vector", "scalar", mv)
            p.wait_chain("vector", "tensor", mv)
            mark = p.s(lambda e: e.copy(out=scur_sb[:], in_=S_sb[:]))
            p.wait_chain("scalar", "tensor", mark)
        else:
            mmlast = p.cnt["mm"]
            qks.release(qi, "mm", mmlast)
            kvs.release(ki, "mm", mmlast)

    attn_mark = p.chain_mark("scalar")
    p.wait_chain("scalar", "tensor", attn_mark)
    banks = Banks(p, [pb[:, 0:512] for pb in p.pp[6:8]])

    for oc in range(8):
        for tt in range(4):
            sl = slice(oc * TPC + tt * 512, oc * TPC + (tt + 1) * 512)
            xi, xbuf, (xsem, xtok) = xs.load(flat(x_in)[:, sl])
            bi, pb = banks.next()

            def mm(e, oc=oc, tt=tt, pb=pb):
                last = None
                for ic in range(8):
                    last = e.matmul(pb, wo_sb[:, (oc * 8 + ic) * 128:(oc * 8 + ic + 1) * 128],
                             attn_sb[:, ic * TPC + tt * 512: ic * TPC + (tt + 1) * 512],
                             start=(ic == 0), stop=(ic == 7))
                return last
            tok = p.t(mm)
            p.wait("scalar", "mm", tok)
            m1 = p.s(lambda e, pb=pb, oc=oc, sl=sl: e.activation(
                out=r1_sb[:, sl], in_=pb, func=AF.Identity, bias=bo_sb[:, oc:oc + 1]))
            p.wait_chain("scalar", "vector", m1)
            p.wait("vector", xsem, xtok)
            mv = p.v(lambda e, sl=sl, xbuf=xbuf: e.tensor_tensor(
                out=r1_sb[:, sl], in0=r1_sb[:, sl], in1=xbuf[:], op=ALU.add))
            xs.release(xi, "chain_vector", mv)
            banks.set_guard(bi, "scalar", m1)

    vres = p.chain_mark("vector")
    p.wait_chain("vector", "scalar", vres)
    p.wait_chain("vector", "tensor", vres)

    emit_layernorm(p, r1_sb, ln, o_x1, o_x1bf, x1_out, x1bf_out)
    x1_out.flush_wait("gpsimd")
    x1bf_out.flush_wait("gpsimd")
    return p.finalize()


def emit_layernorm(p, r_sb, ln, o_f32, o_bf16, f32_out, bf16_out, nchunk=8,
                   tpc=TPC, pbanks=(0, 1, 2, 3), out_fn=None, ptiles=None,
                   tts=None):
    ntt = tpc // 512
    nfeat = nchunk * 128
    if ptiles is None:
        ptiles = [p.pp[i][:] for i in pbanks]
    pSum = ptiles[0][0:1, 0:512]
    pSq = ptiles[1][0:1, 0:512]
    pA = ptiles[2][:, 0:512]
    pB = ptiles[3][:, 0:512]
    rows, rrow, sq_r, tmp32 = ln["rows"], ln["rrow"], ln["sq_r"], ln["tmp32"]

    for tt in (range(ntt) if tts is None else tts):
        ts = slice(tt * 512, (tt + 1) * 512)
        sq_tok = [0, 0]
        for c in range(nchunk):
            sl = slice(c * tpc + tt * 512, c * tpc + (tt + 1) * 512)
            jb = c % 2
            p.wait("scalar", "mm", sq_tok[jb])
            smark = p.s(lambda e, c=c, sl=sl, jb=jb: e.activation(
                out=sq_r[:, jb * 512:(jb + 1) * 512], in_=r_sb[:, sl], func=AF.Square))
            p.wait_chain("scalar", "tensor", smark)
            sq_tok[jb] = p.t(lambda e, c=c, jb=jb: e.matmul(
                pSq, ln["ones_col_r"][:], sq_r[:, jb * 512:(jb + 1) * 512],
                start=(c == 0), stop=(c == nchunk - 1)))

        def statmm(e, tt=tt):
            last = None
            for c in range(nchunk):
                sl = slice(c * tpc + tt * 512, c * tpc + (tt + 1) * 512)
                last = e.matmul(pSum, ln["ones_col_f"][:], r_sb[:, sl],
                         start=(c == 0), stop=(c == nchunk - 1))
            return last
        tok = p.t(statmm)
        p.wait("vector", "mm", tok)
        r0 = slice(0, 512)
        r1s = slice(512, 1024)
        r2s = slice(1024, 1536)
        a0 = slice(0, 512)
        b0 = slice(512, 1024)
        p.v(lambda e: e.tensor_scalar_mul(rows[0:1, r0], pSum, 1.0 / nfeat))
        p.v(lambda e: e.tensor_scalar_mul(rows[0:1, r1s], pSq, 1.0 / nfeat))
        p.v(lambda e: e.tensor_tensor(out=rows[0:1, r2s], in0=rows[0:1, r0],
                                      in1=rows[0:1, r0], op=ALU.mult))
        p.v(lambda e: e.tensor_tensor(out=rows[0:1, r1s], in0=rows[0:1, r1s],
                                      in1=rows[0:1, r2s], op=ALU.subtract))
        vmark = p.chain_mark("vector")
        p.wait_chain("vector", "scalar", vmark)
        p.s(lambda e: e.activation(out=rows[0:1, r2s], in_=rows[0:1, r1s],
                                   func=AF.Ln, bias=np.float32(EPS)))
        p.s(lambda e: e.activation(out=rows[0:1, r1s], in_=rows[0:1, r2s],
                                   func=AF.Exp, scale=np.float32(-0.5)))
        smark = p.chain_mark("scalar")
        p.wait_chain("scalar", "vector", smark)
        p.v(lambda e: e.tensor_tensor(out=rows[0:1, r0], in0=rows[0:1, r0],
                                      in1=rows[0:1, r1s], op=ALU.mult))
        p.v(lambda e: e.tensor_scalar_mul(rows[0:1, r0], rows[0:1, r0], -1.0))
        vmark = p.chain_mark("vector")
        p.wait_chain("vector", "scalar", vmark)
        p.s(lambda e: e.copy(out=rrow[0:1, a0], in_=rows[0:1, r1s]))
        mark = p.s(lambda e: e.copy(out=rrow[0:1, b0], in_=rows[0:1, r0]))
        p.wait_chain("scalar", "tensor", mark)

        def bc(e, a0=a0, b0=b0):
            e.matmul(pA, ln["ones_row_r"][:], rrow[0:1, a0], start=True, stop=True)
            return e.matmul(pB, ln["ones_row_r"][:], rrow[0:1, b0], start=True, stop=True)
        tok = p.t(bc)
        p.wait("vector", "mm", tok)
        for c in range(nchunk):
            sl = slice(c * tpc + tt * 512, c * tpc + (tt + 1) * 512)
            p.v(lambda e, sl=sl: e.tensor_tensor(out=tmp32[:], in0=r_sb[:, sl],
                                                 in1=pA, op=ALU.mult))
            mv = p.v(lambda e, sl=sl: e.tensor_tensor(out=r_sb[:, sl], in0=tmp32[:],
                                                      in1=pB, op=ALU.add))
            p.wait_chain("vector", "scalar", mv)
            if out_fn is None:
                f32_ap, bf16_ap = flat(o_f32)[:, sl], flat(o_bf16)[:, sl]
            else:
                f32_ap, bf16_ap = out_fn(c, tt)
            if f32_out is not None:
                f32_out.put(lambda e, buf, sl=sl: e.copy(out=buf[:], in_=r_sb[:, sl]),
                            f32_ap)
            bf16_out.put(lambda e, buf, sl=sl: e.copy(out=buf[:], in_=r_sb[:, sl]),
                         bf16_ap)
        p.wait_chain("vector", "tensor", p.chain_mark("vector"))


def build_l0ffn():
    p = P()
    x1 = p.din("x1", [128, 8, TPC], F32)
    x1bf = p.din("x1bf", [128, 8, TPC], BF16)
    w1 = p.din("w1", [128, 32, 8, 128], F32)
    w2 = p.din("w2", [128, 32, 8, 128], F32)
    h1 = p.din("h1", [128, 32, 32, 128], F32)
    h2 = p.din("h2", [128, 32, 32, 128], F32)
    w3 = p.din("w3", [128, 8, 32, 128], F32)
    s1 = p.din("s1", [128, 32], F32)
    s2 = p.din("s2", [128, 32], F32)
    h1s = p.din("h1s", [128, 32], F32)
    h2s = p.din("h2s", [128, 32], F32)
    s3 = p.din("s3", [128, 8], F32)
    ones2 = p.din("ones2", [128, 128], F32)

    o_x2 = p.dout("o_x2", [128, 8, TPC], F32)
    o_x2bf = p.dout("o_x2bf", [128, 8, TPC], BF16)

    HTOK = 1024

    xbf_sb = p.sb("xbf_sb", [128, 8 * HTOK], BF16)
    h_sb = p.sb("h_sb", [128, 32 * HTOK], BF16)
    g_sb = p.sb("g_sb", [128, 32 * HTOK], BF16)
    r2_sb = p.sb("r2_sb", [128, 8 * 512], F32)
    tmp1 = p.sb("tmp1", [128, 512], BF16)
    tmp2 = p.sb("tmp2", [128, 512], BF16)
    sc_sb = p.sb("sc_sb", [128, 32 * 4 + 8], F32)
    ws = InStream(p, "ws", [128, 32 * 128], BF16, nbuf=2, q="gpsimd")
    x1s = InStream(p, "x1s", [128, 512], F32, nbuf=2, q="sync")
    ln = dict(
        ones_col_r=p.sb("onescol", [128, 1], F32R),
        ones_row_r=p.sb("onesrow", [1, 128], F32R),
        ones_col_f=p.sb("onescolf", [128, 1], F32),
        sq_r=p.sb("sq_r", [128, 2 * 512], F32R),
        rows=p.sb("rows", [1, 4 * 512], F32),
        rrow=p.sb("rrow", [1, 2 * 512], F32R),
        tmp32=p.sb("tmp32", [128, 512], F32),
    )
    x2_out = OutStream(p, "x2f", [128, 512], F32, nbuf=2, q="scalar")
    x2bf_out = OutStream(p, "x2bf", [128, 512], BF16, nbuf=2, q="scalar")

    lv = p.dma("gpsimd", ln["ones_col_r"][:], ones2[:, 0:1],
               allow_slow_non_contiguous=True)
    lv = p.dma("gpsimd", ln["ones_row_r"][:], ones2[0:1, :])
    ls = p.dma("sync", sc_sb[:, 0:32], s1, sem_name="lds")
    ls = p.dma("sync", sc_sb[:, 32:64], s2, sem_name="lds")
    ls = p.dma("sync", sc_sb[:, 64:96], h1s, sem_name="lds")
    ls = p.dma("sync", sc_sb[:, 96:128], h2s, sem_name="lds")
    ls = p.dma("sync", sc_sb[:, 128:136], s3, sem_name="lds")
    ls = p.dma("sync", ln["ones_col_f"][:], ones2[:, 0:1], sem_name="lds",
               allow_slow_non_contiguous=True)
    for eng in ("tensor", "vector", "scalar"):
        p.wait(eng, "dma_gpsimd", lv)
        p.wait(eng, "lds", ls)

    banks = Banks(p, [pb[:, 0:512] for pb in p.pp[0:4]])
    xbf_uses = 0
    tokA_prev = 0
    for H in range(2):
        if H > 0:
            p.wait("gpsimd", "mm", tokA_prev)
        htok = slice(H * HTOK, (H + 1) * HTOK)
        lvh = p.dma("gpsimd", xbf_sb[:].rearrange("p (c t) -> p c t", c=8),
                    x1bf[:, :, htok], sem_name="ldxh")
        p.wait("tensor", "ldxh", lvh)

        for oc in range(32):
            wi1, wb1, _ = ws.load(
                w1[:, oc, :, :].rearrange("p a b -> p (a b)"), part=slice(0, 1024))
            wsem1, wtok1 = ws.extra(
                wi1, w2[:, oc, :, :].rearrange("p a b -> p (a b)"),
                part=slice(1024, 2048))
            for tt in range(2):
                bi1, pb1 = banks.next()
                p.wait("tensor", wsem1, wtok1)

                def mm1(e, oc=oc, tt=tt, pb=pb1, wb=wb1):
                    last = None
                    for ic in range(8):
                        last = e.matmul(pb, wb[:, ic * 128:(ic + 1) * 128],
                                        xbf_sb[:, ic * HTOK + tt * 512: ic * HTOK + (tt + 1) * 512],
                                        start=(ic == 0), stop=(ic == 7))
                    return last
                tok1 = p.t(mm1)
                bi2, pb2 = banks.next()

                def mm2(e, oc=oc, tt=tt, pb=pb2, wb=wb1):
                    last = None
                    for ic in range(8):
                        last = e.matmul(pb, wb[:, 1024 + ic * 128:1024 + (ic + 1) * 128],
                                        xbf_sb[:, ic * HTOK + tt * 512: ic * HTOK + (tt + 1) * 512],
                                        start=(ic == 0), stop=(ic == 7))
                    return last
                tok2 = p.t(mm2)
                p.wait("scalar", "mm", tok2)
                m1 = p.s(lambda e, pb=pb1, oc=oc: e.activation(
                    out=tmp1[:], in_=pb, func=AF.Silu, scale=sc_sb[:, oc:oc + 1]))
                m2 = p.s(lambda e, pb=pb2, oc=oc: e.activation(
                    out=tmp2[:], in_=pb, func=AF.Identity, scale=sc_sb[:, 32 + oc:33 + oc]))
                p.wait_chain("scalar", "vector", m2)
                mv = p.v(lambda e, oc=oc, tt=tt: e.tensor_mul(
                    h_sb[:, oc * HTOK + tt * 512: oc * HTOK + (tt + 1) * 512],
                    tmp1[:], tmp2[:]))
                p.wait_chain("vector", "scalar", mv)
                banks.set_guard(bi1, "scalar", m1)
                banks.set_guard(bi2, "scalar", m2)
            ws.release(wi1, "mm", tok2)
        tokA_prev = p.cnt["mm"]
        hmark = p.chain_mark("vector")
        p.wait_chain("vector", "tensor", hmark)

        for phase, (wt, ssoff, src, dst) in enumerate(
                ((h1, 64, h_sb, g_sb), (h2, 96, g_sb, h_sb))):
            for oc in range(32):
                wi, wb, (wsem, wtok) = ws.load(wt[:, oc, :, :].rearrange("p a b -> p (a b)"))
                for tt in range(2):
                    bi, pb = banks.next()
                    p.wait("tensor", wsem, wtok)

                    def mmh(e, oc=oc, tt=tt, pb=pb, wb=wb, src=src):
                        last = None
                        for ic in range(32):
                            last = e.matmul(pb, wb[:, ic * 128:(ic + 1) * 128],
                                            src[:, ic * HTOK + tt * 512: ic * HTOK + (tt + 1) * 512],
                                            start=(ic == 0), stop=(ic == 31))
                        return last
                    tok = p.t(mmh)
                    p.wait("scalar", "mm", tok)
                    mk = p.s(lambda e, pb=pb, oc=oc, tt=tt, dst=dst, ssoff=ssoff: e.activation(
                        out=dst[:, oc * HTOK + tt * 512: oc * HTOK + (tt + 1) * 512],
                        in_=pb, func=AF.Gelu, scale=sc_sb[:, ssoff + oc:ssoff + oc + 1]))
                    banks.set_guard(bi, "scalar", mk)
                ws.release(wi, "mm", tok)
            fmark = p.chain_mark("scalar")
            p.wait_chain("scalar", "tensor", fmark)

        for tt in range(2):
            gtok = H * HTOK + tt * 512
            for oc in range(8):
                wi3, wb3, (wsem3, wtok3) = ws.load(
                    w3[:, oc, :, :].rearrange("p a b -> p (a b)"))
                bi, pb = banks.next()
                p.wait("tensor", wsem3, wtok3)

                def mmo(e, oc=oc, tt=tt, pb=pb, wb=wb3):
                    last = None
                    for ic in range(32):
                        last = e.matmul(pb, wb[:, ic * 128:(ic + 1) * 128],
                                        h_sb[:, ic * HTOK + tt * 512: ic * HTOK + (tt + 1) * 512],
                                        start=(ic == 0), stop=(ic == 31))
                    return last
                tok = p.t(mmo)
                ws.release(wi3, "mm", tok)
                sl = slice(oc * 512, (oc + 1) * 512)
                xi, xbuf, (xsem, xtok) = x1s.load(
                    flat(x1)[:, oc * TPC + gtok: oc * TPC + gtok + 512])
                p.wait("scalar", "mm", tok)
                mk = p.s(lambda e, pb=pb, oc=oc, sl=sl: e.activation(
                    out=r2_sb[:, sl], in_=pb, func=AF.Identity,
                    scale=sc_sb[:, 128 + oc:129 + oc]))
                p.wait_chain("scalar", "vector", mk)
                p.wait("vector", xsem, xtok)
                mv = p.v(lambda e, sl=sl, xbuf=xbuf: e.tensor_tensor(
                    out=r2_sb[:, sl], in0=r2_sb[:, sl], in1=xbuf[:], op=ALU.add))
                x1s.release(xi, "chain_vector", mv)
                banks.set_guard(bi, "scalar", mk)
            vmk = p.chain_mark("vector")
            p.wait_chain("vector", "scalar", vmk)
            p.wait_chain("vector", "tensor", vmk)

            def out_fn(c, _tt, gtok=gtok):
                ap = flat(o_x2)[:, c * TPC + gtok: c * TPC + gtok + 512]
                apb = flat(o_x2bf)[:, c * TPC + gtok: c * TPC + gtok + 512]
                return ap, apb
            emit_layernorm(p, r2_sb, ln, None, None, x2_out, x2bf_out,
                           nchunk=8, tpc=512, pbanks=(4, 5, 6, 7), out_fn=out_fn)
    x2_out.flush_wait("gpsimd")
    x2bf_out.flush_wait("gpsimd")
    return p.finalize()


def build_l1front():
    p = P()
    x2bf = p.din("x2bf", [128, 8, TPC], BF16)
    xprev = p.din("xprev", [128, 8, 3], BF16)
    w_in = p.din("w_in", [128, 32, 8, 128], F32)
    dwk = p.din("dwk", [128, 4, 16], F32)
    pw = p.din("pw", [128, 16, 16, 128], F32)
    cbias = p.din("cbias", [128, 16], F32)
    w_x = p.din("w_x", [128, 16, 80], F32)
    w_dt = p.din("w_dt", [64, DIN], F32)
    bdt = p.din("bdt", [128, 16], F32)
    ident = p.din("ident", [128, 128], F32)

    o_z = p.dout("o_z", [128, 16, TPC], BF16)
    o_xc = p.dout("o_xc", [128, 16, TPC], BF16)
    o_delta = p.dout("o_delta", [128, 16, TPC], F32)
    o_bc = p.dout("o_bc", [16, TPC], F32)
    xin_scr = p.nc.dram_tensor("xin_scr", [128, 16, 3 + TPC], BF16, kind="Internal").ap()

    x2_sb = p.sb("x2_sb", [128, 8 * TPC], BF16)
    xp_sb = p.sb("xp_sb", [128, 8 * 3], BF16)
    conv_sb = p.sb("conv_sb", [128, 16 * TPC], BF16)
    xdbl_sb = p.sb("xdbl_sb", [80, TPC], BF16)
    wx_sb = p.sb("wx_sb", [128, 16 * 80], BF16)
    wdt_sb = p.sb("wdt_sb", [64, DIN], BF16)
    dwk_sb = p.sb("dwk_sb", [128, 4 * 16], F32)
    cb_sb = p.sb("cb_sb", [128, 16], F32)
    bdt_sb = p.sb("bdt_sb", [128, 16], F32)
    id_sb = p.sb("id_sb", [128, 128], F32)
    diag_sb = [p.sb(f"diag_sb{i}", [128, 4 * 128], BF16) for i in range(2)]
    xct = [p.sb(f"xct{i}", [128, 512], BF16) for i in range(2)]
    spa = p.sb("spa", [128, 2048], F32)
    spb = p.sb("spb", [128, 2048], F32)
    spc = p.sb("spc", [128, 2048], F32)
    ws = InStream(p, "ws", [128, 16 * 128], BF16, nbuf=2, q="gpsimd")
    xins = InStream(p, "xins", [128, 3 + TPC], BF16, nbuf=2, q="sync")
    z_out = OutStream(p, "z", [128, 512], BF16, nbuf=2, q="scalar")
    xin_out = OutStream(p, "xin", [128, 512], BF16, nbuf=2, q="scalar")
    xc_out = OutStream(p, "xc", [128, 512], BF16, nbuf=2, q="scalar")

    lv = p.dma("gpsimd", x2_sb[:], flat(x2bf))
    lv = p.dma("gpsimd", xp_sb[:], flat(xprev))
    lv = p.dma("gpsimd", wx_sb[:], flat(w_x))
    lv = p.dma("gpsimd", wdt_sb[:], w_dt)
    ls = p.dma("sync", dwk_sb[:], flat(dwk), sem_name="lds")
    ls = p.dma("sync", cb_sb[:], cbias, sem_name="lds")
    ls = p.dma("sync", bdt_sb[:], bdt, sem_name="lds")
    ls = p.dma("sync", id_sb[:], ident, sem_name="lds")
    for eng in ("tensor", "vector", "scalar"):
        p.wait(eng, "dma_gpsimd", lv)
        p.wait(eng, "lds", ls)

    banks = Banks(p, [pb[:, 0:512] for pb in p.pp[0:2]])

    for oc in range(16):
        wi, wb, (wsem, wtok) = ws.load(w_in[:, oc, :, :].rearrange("p a b -> p (a b)"),
                                       part=slice(0, 8 * 128))
        bi, pb = banks.next()
        p.wait("tensor", wsem, wtok)

        def mmh(e, oc=oc, pb=pb, wb=wb):
            last = None
            for ic in range(8):
                last = e.matmul(pb[:, 0:3], wb[:, ic * 128:(ic + 1) * 128],
                                xp_sb[:, ic * 3:(ic + 1) * 3],
                                start=(ic == 0), stop=(ic == 7))
            return last
        tok = p.t(mmh)
        p.wait("scalar", "mm", tok)

        def evh(e, buf, pb=pb):
            return e.copy(out=buf[:, 0:3], in_=pb[:, 0:3])
        xin_out.put(evh, xin_scr[:, oc, 0:3], part=slice(0, 3))
        banks.set_guard(bi, "scalar", p.chain_mark("scalar"))
        for tt in range(4):
            bi, pb = banks.next()

            def mm(e, oc=oc, tt=tt, pb=pb, wb=wb):
                last = None
                for ic in range(8):
                    last = e.matmul(pb, wb[:, ic * 128:(ic + 1) * 128],
                                    x2_sb[:, ic * TPC + tt * 512: ic * TPC + (tt + 1) * 512],
                                    start=(ic == 0), stop=(ic == 7))
                return last
            tok = p.t(mm)
            p.wait("scalar", "mm", tok)
            xin_out.put(lambda e, buf, pb=pb: e.copy(out=buf[:], in_=pb),
                        xin_scr[:, oc, 3 + tt * 512: 3 + (tt + 1) * 512])
            banks.set_guard(bi, "scalar", p.chain_mark("scalar"))
        ws.release(wi, "mm", tok)
    for oc in range(16):
        wi, wb, (wsem, wtok) = ws.load(w_in[:, 16 + oc, :, :].rearrange("p a b -> p (a b)"),
                                       part=slice(0, 8 * 128))
        for tt in range(4):
            bi, pb = banks.next()
            p.wait("tensor", wsem, wtok)

            def mm(e, oc=oc, tt=tt, pb=pb, wb=wb):
                last = None
                for ic in range(8):
                    last = e.matmul(pb, wb[:, ic * 128:(ic + 1) * 128],
                                    x2_sb[:, ic * TPC + tt * 512: ic * TPC + (tt + 1) * 512],
                                    start=(ic == 0), stop=(ic == 7))
                return last
            tok = p.t(mm)
            p.wait("scalar", "mm", tok)
            z_out.put(lambda e, buf, pb=pb: e.copy(out=buf[:], in_=pb),
                      flat(o_z)[:, oc * TPC + tt * 512: oc * TPC + (tt + 1) * 512])
            banks.set_guard(bi, "scalar", p.chain_mark("scalar"))
        ws.release(wi, "mm", tok)

    xin_out.flush_wait("sync")
    prev_conv_tok = [0, 0]
    for oc in range(16):
        par = oc % 2
        xi, xb, (xsem, xtok) = xins.load(xin_scr[:, oc, :])
        p.wait("vector", "mm", prev_conv_tok[par])
        for j in range(4):
            p.v(lambda e, j=j, oc=oc, par=par: e.tensor_single_scalar(
                out=diag_sb[par][:, j * 128:(j + 1) * 128], in_=id_sb[:],
                scalar=dwk_sb[:, j * 16 + oc: j * 16 + oc + 1], op=ALU.mult))
        dmark = p.chain_mark("vector")
        p.wait_chain("vector", "tensor", dmark)
        p.wait("tensor", xsem, xtok)
        lasttok = 0
        for tt in range(4):
            bi, pb = banks.next()

            def mmc(e, oc=oc, tt=tt, pb=pb, xb=xb, par=par):
                last = None
                for j in range(4):
                    last = e.matmul(pb, diag_sb[par][:, j * 128:(j + 1) * 128],
                                    xb[:, j + tt * 512: j + tt * 512 + 512],
                                    start=(j == 0), stop=(j == 3))
                return last
            tok = p.t(mmc)
            lasttok = tok
            p.wait("scalar", "mm", tok)
            mk = p.s(lambda e, oc=oc, tt=tt, pb=pb: e.copy(
                out=conv_sb[:, oc * TPC + tt * 512: oc * TPC + (tt + 1) * 512], in_=pb))
            banks.set_guard(bi, "scalar", mk)
        xins.release(xi, "mm", lasttok)
        prev_conv_tok[par] = lasttok
    cmark = p.chain_mark("scalar")
    p.wait_chain("scalar", "tensor", cmark)

    pX = [p.pp[4 + i][0:80, 0:512] for i in range(4)]
    for oc in range(16):
        wi, wb, (wsem, wtok) = ws.load(pw[:, oc, :, :].rearrange("p a b -> p (a b)"))
        for tt in range(4):
            bi, pb = banks.next()
            p.wait("tensor", wsem, wtok)

            def mmp(e, oc=oc, tt=tt, pb=pb, wb=wb):
                last = None
                for ic in range(16):
                    last = e.matmul(pb, wb[:, ic * 128:(ic + 1) * 128],
                                    conv_sb[:, ic * TPC + tt * 512: ic * TPC + (tt + 1) * 512],
                                    start=(ic == 0), stop=(ic == 15))
                return last
            tok = p.t(mmp)
            p.wait("scalar", "mm", tok)
            xcb = xct[(oc * 4 + tt) % 2]
            mk = p.s(lambda e, pb=pb, oc=oc, xcb=xcb: e.activation(
                out=xcb[:], in_=pb, func=AF.Silu, bias=cb_sb[:, oc:oc + 1]))
            banks.set_guard(bi, "scalar", mk)
            xc_out.put(lambda e, buf, xcb=xcb: e.copy(out=buf[:], in_=xcb[:]),
                       flat(o_xc)[:, oc * TPC + tt * 512: oc * TPC + (tt + 1) * 512])
            p.wait_chain("scalar", "tensor", p.chain_mark("scalar"))

            def mmx(e, oc=oc, tt=tt, xcb=xcb):
                return e.matmul(pX[tt], wx_sb[:, oc * 80:(oc + 1) * 80], xcb[:],
                                start=(oc == 0), stop=(oc == 15))
            p.t(mmx)
        ws.release(wi, "mm", p.cnt["mm"])

    tok = p.cnt["mm"]
    p.wait("scalar", "mm", tok)
    for tt in range(4):
        ts = slice(tt * 512, (tt + 1) * 512)
        p.s(lambda e, tt=tt, ts=ts: e.copy(out=xdbl_sb[0:64, ts], in_=pX[tt][0:64, :]))
        mk = p.s(lambda e, tt=tt, ts=ts: e.copy(out=xdbl_sb[64:80, ts], in_=pX[tt][64:80, :]))
    xd_mark = p.chain_mark("scalar")
    p.wait_chain("scalar", "gpsimd", xd_mark)
    bctok = p.dma("gpsimd", o_bc, xdbl_sb[64:80, :], sem_name="dst_bcx")
    p.wait_chain("scalar", "tensor", xd_mark)

    sp2 = [(spa, spb, spc),
           (p.sb("spa1", [128, 2048], F32), p.sb("spb1", [128, 2048], F32),
            p.sb("spc1", [128, 2048], F32))]
    dtok = [0, 0]
    add_mark = [0, 0]
    for oc in range(16):
        par = oc % 2
        pa, pb_, pc = sp2[par]
        p.wait("scalar", f"dst_dlt{par}", dtok[par])
        for tt in range(4):
            bi, pb = banks.next()

            def mmd(e, oc=oc, tt=tt, pb=pb):
                return e.matmul(pb, wdt_sb[:, oc * 128:(oc + 1) * 128],
                                xdbl_sb[0:64, tt * 512:(tt + 1) * 512],
                                start=True, stop=True)
            tok = p.t(mmd)
            p.wait("scalar", "mm", tok)
            m1 = p.s(lambda e, pb=pb, oc=oc, tt=tt, pa=pa: e.activation(
                out=pa[:, tt * 512:(tt + 1) * 512], in_=pb, func=AF.Identity,
                bias=bdt_sb[:, oc:oc + 1]))
            banks.set_guard(bi, "scalar", m1)
        m1 = p.chain_mark("scalar")
        p.wait_chain("scalar", "vector", m1)
        mv = p.v(lambda e, pa=pa, pc=pc: e.tensor_scalar_max(pc[:], pa[:], 0.0))
        mv2 = p.v(lambda e, pa=pa, pb_=pb_, pc=pc: e.scalar_tensor_tensor(
            out=pb_[:], in0=pc[:], scalar=2.0, in1=pa[:],
            op0=ALU.mult, op1=ALU.subtract))
        p.wait_chain("vector", "scalar", mv2)
        p.s(lambda e, pb_=pb_: e.activation(out=pb_[:], in_=pb_[:], func=AF.Exp,
                                            scale=np.float32(-1.0)))
        m2 = p.s(lambda e, pb_=pb_: e.activation(out=pb_[:], in_=pb_[:], func=AF.Ln,
                                                 bias=np.float32(1.0)))
        p.wait_chain("scalar", "vector", m2)
        mv = p.v(lambda e, pa=pa, pb_=pb_, pc=pc: e.tensor_add(pa[:], pb_[:], pc[:]))
        add_mark[par] = mv
        p.wait_chain("vector", "scalar", mv)
        dtok[par] = p.dma("scalar", flat(o_delta)[:, oc * TPC:(oc + 1) * TPC],
                          pa[:], sem_name=f"dst_dlt{par}")

    z_out.flush_wait("gpsimd")
    xc_out.flush_wait("gpsimd")
    p.wait("gpsimd", "dst_dlt0", dtok[0])
    p.wait("gpsimd", "dst_dlt1", dtok[1])
    p.wait("gpsimd", "dst_bcx", bctok)
    return p.finalize()


def build_scan():
    p = P()
    TL = B * L // 2
    QT = 2048
    NQ = TL // QT
    delta_in = p.din("delta_in", [128, 2, 2, TL], F32)
    xc_in = p.din("xc_in", [128, 2, 2, TL], BF16)
    bmat = p.din("bmat", [2, NSTATE, TL], F32)
    cmat = p.din("cmat", [2, NSTATE, TL], F32)
    alog = p.din("alog", [128, 2, NSTATE], F32)

    o_y = p.dout("o_y", [128, 2, 2, TL], BF16)

    dl_sb = p.sb("dl_sb", [128, TL], F32)
    u_sb = p.sb("u_sb", [128, TL], F32)
    y_sb = p.sb("y_sb", [128, TL], F32)
    xc_sb = p.sb("xc_sb", [128, TL], BF16)
    dA_sb = [p.sb(f"dA{i}", [128, QT], F32) for i in range(2)]
    dBu_sb = p.sb("dBu_sb", [128, QT], F32)
    h_sb = p.sb("h_sb", [128, QT], F32)
    hl_sb = p.sb("hl_sb", [128, 1], F32)
    bb_sb = [p.sb(f"bb{i}", [128, QT], F32) for i in range(2)]
    cb_sb = [p.sb(f"cb{i}", [128, QT], F32) for i in range(2)]
    na_sb = p.sb("na_sb", [128, 2 * NSTATE], F32)
    al_sb = p.sb("al_sb", [128, 2 * NSTATE], F32)
    yb_out = OutStream(p, "yb", [128, 2048], BF16, nbuf=2, q="sync")

    ls = p.dma("sync", al_sb[:], flat(alog), sem_name="lds")
    p.wait("scalar", "lds", ls)
    p.s(lambda e: e.activation(out=na_sb[:], in_=al_sb[:], func=AF.Exp))
    nmark = p.chain_mark("scalar")
    p.wait_chain("scalar", "vector", nmark)
    p.v(lambda e: e.tensor_scalar_mul(na_sb[:], na_sb[:], -1.0))
    namark = p.chain_mark("vector")
    p.wait_chain("vector", "scalar", namark)

    iters = [(b, dc, n, qq)
             for b in range(2) for dc in range(2)
             for n in range(NSTATE) for qq in range(NQ)]
    bc_tok = {}
    dA_mark = {}
    bc_guard = [0, 0]
    dA_guard = [0, 0]
    ld_guard = {}

    def issue_bc(i):
        b, dc, n, qq = iters[i]
        sl = slice(qq * QT, (qq + 1) * QT)
        par = i % 2
        p.wait_chain("vector", "sync", bc_guard[par])
        t1 = p.dma("sync", bb_sb[par][:], bmat[b:b + 1, n, sl].broadcast_to((128, QT)),
                   sem_name=f"ld_bb{par}")
        t2 = p.dma("sync", cb_sb[par][:], cmat[b:b + 1, n, sl].broadcast_to((128, QT)),
                   sem_name=f"ld_cb{par}")
        bc_tok[i] = (t1, t2)

    def issue_dA(i):
        b, dc, n, qq = iters[i]
        sl = slice(qq * QT, (qq + 1) * QT)
        par = i % 2
        p.wait_chain("vector", "scalar", dA_guard[par])
        p.wait("scalar", f"ld_dl{qq}", ld_guard[("dl", b, dc, qq)])
        dA_mark[i] = p.s(lambda e, sl=sl, dc=dc, n=n, par=par: e.activation(
            out=dA_sb[par][:], in_=dl_sb[:, sl],
            func=AF.Exp, scale=na_sb[:, dc * NSTATE + n: dc * NSTATE + n + 1]))

    lv_dl = lv_xc = 0
    for b in range(2):
        for dc in range(2):
            vg = p.chain_mark("vector")
            p.wait_chain("vector", "gpsimd", vg)
            sg = p.chain_mark("scalar")
            p.wait_chain("scalar", "gpsimd", sg)
            for q in range(NQ):
                qsl = slice(q * QT, (q + 1) * QT)
                ld_guard[("dl", b, dc, q)] = p.dma(
                    "gpsimd", dl_sb[:, qsl], delta_in[:, dc, b, qsl],
                    sem_name=f"ld_dl{q}")
                ld_guard[("xc", b, dc, q)] = p.dma(
                    "gpsimd", xc_sb[:, qsl], xc_in[:, dc, b, qsl],
                    sem_name=f"ld_xc{q}")
            base = (b * 2 + dc) * NSTATE * NQ
            if base == 0:
                issue_bc(0)
            issue_dA(base)
            p.wait_chain("scalar", "vector", p.chain_mark("scalar"))
            p.v(lambda e: e.memset(y_sb[:], 0.0))
            for q in range(NQ):
                qsl = slice(q * QT, (q + 1) * QT)
                p.wait("vector", f"ld_dl{q}", ld_guard[("dl", b, dc, q)])
                p.wait("vector", f"ld_xc{q}", ld_guard[("xc", b, dc, q)])
                p.v(lambda e, qsl=qsl: e.tensor_mul(u_sb[:, qsl], dl_sb[:, qsl],
                                                    xc_sb[:, qsl]))
            for n in range(NSTATE):
                for qq in range(NQ):
                    i = base + n * NQ + qq
                    par = i % 2
                    sl = slice(qq * QT, (qq + 1) * QT)
                    if i + 1 < len(iters) and iters[i + 1][0] == b and iters[i + 1][1] == dc:
                        issue_bc(i + 1)
                        issue_dA(i + 1)
                    elif i + 1 < len(iters):
                        issue_bc(i + 1)
                    t1, t2 = bc_tok[i]
                    p.wait("vector", f"ld_bb{par}", t1)
                    p.wait_chain("scalar", "vector", dA_mark[i])
                    p.v(lambda e, sl=sl, par=par: e.tensor_mul(
                        dBu_sb[:], u_sb[:, sl], bb_sb[par][:]))
                    if qq == 0:
                        p.v(lambda e, par=par: e.tensor_tensor_scan(
                            out=h_sb[:], data0=dA_sb[par][:], data1=dBu_sb[:],
                            initial=0.0, op0=ALU.mult, op1=ALU.add))
                    else:
                        p.v(lambda e, par=par: e.tensor_tensor_scan(
                            out=h_sb[:], data0=dA_sb[par][:], data1=dBu_sb[:],
                            initial=hl_sb[:, 0:1], op0=ALU.mult, op1=ALU.add))
                    dA_guard[par] = p.chain_mark("vector")
                    if qq < NQ - 1:
                        p.v(lambda e: e.tensor_copy(hl_sb[:, 0:1], h_sb[:, QT - 1:QT]))
                    p.wait("vector", f"ld_cb{par}", t2)
                    p.v(lambda e, par=par: e.tensor_mul(dBu_sb[:], h_sb[:], cb_sb[par][:]))
                    mv = p.v(lambda e, sl=sl: e.tensor_add(y_sb[:, sl], y_sb[:, sl], dBu_sb[:]))
                    bc_guard[par] = mv
            vg = p.chain_mark("vector")
            p.wait_chain("vector", "scalar", vg)
            for q in range(TL // 2048):
                yb_out.put(lambda e, buf, q=q: e.copy(out=buf[:], in_=y_sb[:, q * 2048:(q + 1) * 2048]),
                           o_y[:, dc, b, q * 2048:(q + 1) * 2048])
            nb = (b * 2 + dc) + 1
            if nb < 4:
                pass
    yb_out.flush_wait("gpsimd")
    return p.finalize()


def build_l1back():
    p = P()
    y_in = p.din("y_in", [128, 16, TPC], BF16)
    xc_in = p.din("xc_in", [128, 16, TPC], BF16)
    z_in = p.din("z_in", [128, 16, TPC], BF16)
    dvec = p.din("dvec", [128, 16], F32)
    w_out = p.din("w_out", [128, 8, 16, 128], F32)
    x2 = p.din("x2", [128, 8, TPC], F32)
    ones2 = p.din("ones2", [128, 128], F32)

    o_x3 = p.dout("o_x3", [128, 8, TPC], F32)
    o_x3bf = p.dout("o_x3bf", [128, 8, TPC], BF16)

    y2_sb = p.sb("y2_sb", [128, 16 * TPC], BF16)
    r3_sb = p.sb("r3_sb", [128, 8 * TPC], F32)
    dvec_sb = p.sb("dvec_sb", [128, 16], F32)
    sz_sb = p.sb("sz_sb", [128, 512], F32)
    g1_sb = p.sb("g1_sb", [128, 512], F32)
    ys = InStream(p, "ys", [128, 512], BF16, nbuf=2, q="sync")
    xcs = InStream(p, "xcs", [128, 512], BF16, nbuf=2, q="sync")
    zs = InStream(p, "zs", [128, 512], BF16, nbuf=2, q="sync")
    x2s = InStream(p, "x2s", [128, 512], F32, nbuf=2, q="sync")
    ws = InStream(p, "ws", [128, 16 * 128], BF16, nbuf=2, q="gpsimd")
    ln = dict(
        ones_col_r=p.sb("onescol", [128, 1], F32R),
        ones_row_r=p.sb("onesrow", [1, 128], F32R),
        ones_col_f=p.sb("onescolf", [128, 1], F32),
        sq_r=p.sb("sq_r", [128, 2 * 512], F32R),
        rows=p.sb("rows", [1, 4 * 512], F32),
        rrow=p.sb("rrow", [1, 2 * 512], F32R),
        tmp32=p.sb("tmp32", [128, 512], F32),
    )
    x3_out = OutStream(p, "x3f", [128, 512], F32, nbuf=2)
    x3bf_out = OutStream(p, "x3bf", [128, 512], BF16, nbuf=2)

    lv = p.dma("gpsimd", ln["ones_col_r"][:], ones2[:, 0:1],
               allow_slow_non_contiguous=True)
    lv = p.dma("gpsimd", ln["ones_row_r"][:], ones2[0:1, :])
    ls = p.dma("sync", dvec_sb[:], dvec, sem_name="lds")
    ls = p.dma("sync", ln["ones_col_f"][:], ones2[:, 0:1], sem_name="lds",
               allow_slow_non_contiguous=True)
    for eng in ("tensor", "vector", "scalar"):
        p.wait(eng, "dma_gpsimd", lv)
        p.wait(eng, "lds", ls)

    for oc in range(16):
        for tt in range(4):
            sl = slice(oc * TPC + tt * 512, oc * TPC + (tt + 1) * 512)
            yi, yb, (ysem, ytok) = ys.load(flat(y_in)[:, sl])
            xi, xb, (xsem, xtok) = xcs.load(flat(xc_in)[:, sl])
            zi, zb, (zsem, ztok) = zs.load(flat(z_in)[:, sl])
            p.wait("scalar", zsem, ztok)
            mk = p.s(lambda e, zb=zb: e.activation(out=sz_sb[:], in_=zb[:], func=AF.Silu))
            p.wait_chain("scalar", "vector", mk)
            p.wait("vector", ysem, ytok)
            p.wait("vector", xsem, xtok)
            p.v(lambda e, yb=yb, xb=xb, oc=oc: e.scalar_tensor_tensor(
                out=g1_sb[:], in0=xb[:], scalar=dvec_sb[:, oc:oc + 1], in1=yb[:],
                op0=ALU.mult, op1=ALU.add))
            mv = p.v(lambda e, sl=sl: e.tensor_mul(y2_sb[:, sl], g1_sb[:], sz_sb[:]))
            ys.release(yi, "chain_vector", mv)
            xcs.release(xi, "chain_vector", mv)
            zs.release(zi, "chain_vector", mv)
            p.wait_chain("vector", "scalar", mv)
    y2mark = p.chain_mark("vector")
    p.wait_chain("vector", "tensor", y2mark)

    banks = Banks(p, [pb[:, 0:512] for pb in p.pp[0:2]])
    for oc in range(8):
        wi, wb, (wsem, wtok) = ws.load(w_out[:, oc, :, :].rearrange("p a b -> p (a b)"))
        for tt in range(4):
            sl = slice(oc * TPC + tt * 512, oc * TPC + (tt + 1) * 512)
            xi, xbuf, (xsem, xtok) = x2s.load(flat(x2)[:, sl])
            bi, pb = banks.next()
            p.wait("tensor", wsem, wtok)

            def mm(e, oc=oc, tt=tt, pb=pb, wb=wb):
                last = None
                for ic in range(16):
                    last = e.matmul(pb, wb[:, ic * 128:(ic + 1) * 128],
                                    y2_sb[:, ic * TPC + tt * 512: ic * TPC + (tt + 1) * 512],
                                    start=(ic == 0), stop=(ic == 15))
                return last
            tok = p.t(mm)
            p.wait("scalar", "mm", tok)
            m1 = p.s(lambda e, pb=pb, sl=sl: e.copy(out=r3_sb[:, sl], in_=pb))
            p.wait_chain("scalar", "vector", m1)
            p.wait("vector", xsem, xtok)
            mv = p.v(lambda e, sl=sl, xbuf=xbuf: e.tensor_tensor(
                out=r3_sb[:, sl], in0=r3_sb[:, sl], in1=xbuf[:], op=ALU.add))
            x2s.release(xi, "chain_vector", mv)
            banks.set_guard(bi, "scalar", m1)
        ws.release(wi, "mm", tok)
    vres = p.chain_mark("vector")
    p.wait_chain("vector", "scalar", vres)
    p.wait_chain("vector", "tensor", vres)

    emit_layernorm(p, r3_sb, ln, o_x3, o_x3bf, x3_out, x3bf_out,
                   nchunk=8, tpc=TPC, pbanks=(4, 5, 6, 7))
    x3_out.flush_wait("gpsimd")
    x3bf_out.flush_wait("gpsimd")
    return p.finalize()


def build_qkv():
    p = P()
    x3bf = p.din("x3bf", [128, 8, TPC], BF16)
    wq = p.din("wq", [128, 8, 8, 128], F32)
    wk = p.din("wk", [128, 8, 8, 128], F32)
    wvp = p.din("wvp", [128, 8, D], F32)
    bq = p.din("bq", [128, 8], F32)
    bk = p.din("bk", [128, 8], F32)
    bv_row = p.din("bv_row", [1, D], F32)
    temp_in = p.din("temp_in", [1, NH], F32)
    qe_pcol = p.din("qe_pcol", [128, 8], F32)
    esel = p.din("esel", [NH, 128], F32)
    hmask = p.din("hmask", [NH, 8], F32)
    jsel = p.din("jsel", [2, 128], F32)
    hsel = p.din("hsel", [128, 2], F32)
    ones2 = p.din("ones2", [128, 128], F32)

    o_qt = p.dout("o_qt", [128, 8, TPC], BF16)
    o_kf = p.dout("o_kf", [128, 16, 8, 128], BF16)
    o_vaug = p.dout("o_vaug", [128, 16, NH, DH + 1], BF16)
    o_norms = p.dout("o_norms", [2, 16], F32)

    x3_sb = p.sb("x3_sb", [128, 8 * TPC], BF16)
    qf_sb = p.sb("qf_sb", [128, 8 * TPC], BF16)
    wq_sb = p.sb("wq_sb", [128, 64 * 128], BF16)
    wk_sb = p.sb("wk_sb", [128, 64 * 128], BF16)
    wvp_sb = p.sb("wvp_sb", [128, 8 * D], BF16)
    bq_sb = p.sb("bq_sb", [128, 8], F32)
    bk_sb = p.sb("bk_sb", [128, 8], F32)
    bvr_sb = p.sb("bvr_sb", [1, D], BF16)
    ones_sb = p.sb("ones_sb", [1, 128], BF16)
    temp_sb = p.sb("temp_sb", [1, 3 * NH], F32)
    sptc_sb = p.sb("sptc_sb", [16, 1], F32)
    sptm_sb = p.sb("sptm_sb", [16, 8], F32)
    esel_sb = p.sb("esel_sb", [NH, 128], F32)
    hmask_sb = p.sb("hmask_sb", [NH, 8], F32)
    jsel_sb = p.sb("jsel_sb", [2, 128], F32R)
    hsel_sb = p.sb("hsel_sb", [128, 2], F32R)
    sptp_sb = p.sb("sptp_sb", [128, 8], F32)
    qesp_sb = p.sb("qesp_sb", [128, 8], F32)
    sq_sb = p.sb("sq_sb", [128, 512], F32R)
    rn_sb = p.sb("rn_sb", [2, 512], F32)
    rnr_sb = p.sb("rnr_sb", [2, 512], F32R)
    nmax_sb = p.sb("nmax_sb", [2, 18], F32)
    qt_sb = p.sb("qt_sb", [128, 512], F32)
    vaug_sb = [p.sb(f"vaug{i}", [128, NH * (DH + 1)], BF16) for i in range(2)]
    id1_sb = p.sb("id1_sb", [1, 1], F32)
    qt_out = OutStream(p, "qt", [128, 512], BF16, nbuf=2)
    kf_out = OutStream(p, "kf", [128, 512], BF16, nbuf=2)

    lv = p.dma("gpsimd", x3_sb[:], flat(x3bf))
    lv = p.dma("gpsimd", wq_sb[:], flat(wq))
    lv = p.dma("gpsimd", wk_sb[:], flat(wk))
    lv = p.dma("gpsimd", wvp_sb[:], flat(wvp))
    lv = p.dma("gpsimd", jsel_sb[:], jsel)
    lv = p.dma("gpsimd", hsel_sb[:], hsel)
    lv = p.dma("gpsimd", bvr_sb[:], bv_row)
    ls = p.dma("sync", bq_sb[:], bq, sem_name="lds")
    ls = p.dma("sync", bk_sb[:], bk, sem_name="lds")
    ls = p.dma("sync", temp_sb[:, 0:NH], temp_in, sem_name="lds")
    ls = p.dma("sync", hmask_sb[:], hmask, sem_name="lds")
    ls = p.dma("sync", esel_sb[:], esel, sem_name="lds")
    ls = p.dma("sync", qesp_sb[:], qe_pcol, sem_name="lds")
    ls = p.dma("sync", id1_sb[:], ones2[0:1, 0:1], sem_name="lds")
    for eng in ("tensor", "vector", "scalar"):
        p.wait(eng, "dma_gpsimd", lv)
        p.wait(eng, "lds", ls)
    p.v(lambda e: e.memset(ones_sb[:], 1.0))
    p.v(lambda e: e.memset(nmax_sb[:], 0.0))
    init_mark = p.chain_mark("vector")
    p.wait_chain("vector", "tensor", init_mark)
    p.wait_chain("vector", "scalar", init_mark)

    t0, t1s, t2s = slice(0, NH), slice(NH, 2 * NH), slice(2 * NH, 3 * NH)
    p.s(lambda e: e.activation(out=temp_sb[0:1, t1s], in_=temp_sb[0:1, t0],
                               func=AF.Exp, scale=np.float32(-1.0)))
    mk = p.chain_mark("scalar")
    p.wait_chain("scalar", "vector", mk)
    p.v(lambda e: e.tensor_single_scalar(out=temp_sb[0:1, t2s], in_=temp_sb[0:1, t0],
                                         scalar=0.0, op=ALU.max))
    mk = p.chain_mark("vector")
    p.wait_chain("vector", "scalar", mk)
    p.s(lambda e: e.activation(out=temp_sb[0:1, t1s], in_=temp_sb[0:1, t1s],
                               func=AF.Ln, bias=np.float32(1.0)))
    mk = p.chain_mark("scalar")
    p.wait_chain("scalar", "vector", mk)
    p.v(lambda e: e.tensor_add(temp_sb[0:1, t0], temp_sb[0:1, t1s], temp_sb[0:1, t2s]))
    mk = p.chain_mark("vector")
    p.wait_chain("vector", "tensor", mk)
    tok = p.t(lambda e: e.transpose(p.pp[0][0:NH, 0:1], temp_sb[0:1, t0], id1_sb[:]))
    p.wait("scalar", "mm", tok)
    mk = p.s(lambda e: e.copy(out=sptc_sb[:], in_=p.pp[0][0:NH, 0:1]))
    p.wait_chain("scalar", "vector", mk)
    mv = p.v(lambda e: e.tensor_single_scalar(out=sptm_sb[:], in_=hmask_sb[:],
                                              scalar=sptc_sb[:, 0:1], op=ALU.mult))
    p.wait_chain("vector", "tensor", mv)
    tok = p.t(lambda e: e.matmul(p.pp[1][:, 0:8], esel_sb[:], sptm_sb[:],
                                 start=True, stop=True))
    p.wait("scalar", "mm", tok)
    mk = p.s(lambda e: e.copy(out=sptp_sb[:], in_=p.pp[1][:, 0:8]))
    p.wait_chain("scalar", "vector", mk)
    mv = p.v(lambda e: e.tensor_mul(qesp_sb[:], qesp_sb[:], sptp_sb[:]))
    p.wait_chain("vector", "scalar", mv)
    p.wait_chain("vector", "tensor", mv)

    banks = Banks(p, [pb[:, 0:512] for pb in p.pp[2:4]])
    pN = p.pp[4][0:2, 0:512]
    pRB = p.pp[5][:, 0:512]

    for oc in range(8):
        for tt in range(4):
            bi, pb = banks.next()

            def mm(e, oc=oc, tt=tt, pb=pb):
                last = None
                for ic in range(8):
                    last = e.matmul(pb, wq_sb[:, (oc * 8 + ic) * 128:(oc * 8 + ic + 1) * 128],
                                    x3_sb[:, ic * TPC + tt * 512: ic * TPC + (tt + 1) * 512],
                                    start=(ic == 0), stop=(ic == 7))
                return last
            tok = p.t(mm)
            p.wait("scalar", "mm", tok)
            mk = p.s(lambda e, pb=pb, oc=oc, tt=tt: e.activation(
                out=qf_sb[:, oc * TPC + tt * 512: oc * TPC + (tt + 1) * 512],
                in_=pb, func=AF.Identity, bias=bq_sb[:, oc:oc + 1]))
            banks.set_guard(bi, "scalar", mk)
    qmark = p.chain_mark("scalar")
    p.wait_chain("scalar", "tensor", qmark)

    pNs = [p.pp[4][0:2, 0:512], p.pp[6][0:2, 0:512]]
    pRBs = [p.pp[5][:, 0:512], p.pp[7][:, 0:512]]
    sq2 = [sq_sb, p.sb("sq2_sb", [128, 512], F32R)]
    rn2 = [rn_sb, p.sb("rn2_sb", [2, 512], F32)]
    rnr2 = [rnr_sb, p.sb("rnr2_sb", [2, 512], F32R)]
    qt2 = [qt_sb, p.sb("qt2_sb", [128, 512], F32)]
    sqg = [0, 0]
    ln_done = 0
    rbg = [0, 0]
    qtg = [0, 0]
    red_done = 0
    for fc in range(8):
        for tt in range(4):
            ii = (fc * 4 + tt) % 2
            sl = slice(fc * TPC + tt * 512, fc * TPC + (tt + 1) * 512)
            p.wait("scalar", "mm", sqg[ii])
            mk = p.s(lambda e, sl=sl, ii=ii: e.activation(
                out=sq2[ii][:], in_=qf_sb[:, sl], func=AF.Square))
            p.wait_chain("scalar", "tensor", mk)
            p.wait_chain("scalar", "tensor", ln_done)
            tokq = p.t(lambda e, ii=ii: e.matmul(pNs[0], hsel_sb[:], sq2[ii][:],
                                                 start=True, stop=True))
            p.wait("scalar", "mm", tokq)
            p.s(lambda e, ii=ii: e.activation(out=rn2[ii][:], in_=pNs[0], func=AF.Ln))
            ln_done = lnm = p.chain_mark("scalar")

            def expfn(e, ii=ii):
                with p.nc.allow_low_precision(reason="f32r norm bcast"):
                    return e.activation(out=rnr2[ii][:], in_=rn2[ii][:],
                                        func=AF.Exp, scale=np.float32(-0.5))
            mke = p.s(expfn)
            p.wait_chain("scalar", "tensor", mke)
            p.wait_chain("vector", "tensor", rbg[ii])
            tokb = p.t(lambda e, ii=ii: e.matmul(pRBs[ii], jsel_sb[:], rnr2[ii][:],
                                                 start=True, stop=True))
            p.wait("vector", "mm", tokb)
            p.wait_chain("scalar", "vector", qtg[ii])
            p.v(lambda e, sl=sl, fc=fc, ii=ii: e.scalar_tensor_tensor(
                out=qt2[ii][:], in0=qf_sb[:, sl], scalar=sptp_sb[:, fc:fc + 1],
                in1=pRBs[ii], op0=ALU.mult, op1=ALU.mult))
            mv = p.v(lambda e, fc=fc, ii=ii: e.tensor_single_scalar(
                out=qt2[ii][:], in_=qt2[ii][:], scalar=qesp_sb[:, fc:fc + 1], op=ALU.add))
            rbg[ii] = mv
            p.wait_chain("vector", "scalar", mv)
            qt_out.put(lambda e, buf, ii=ii: e.copy(out=buf[:], in_=qt2[ii][:]),
                       flat(o_qt)[:, sl])
            p.wait("scalar", "mm", tokq)
            mk2 = p.s(lambda e, ii=ii: e.activation(out=sq2[ii][:], in_=qt2[ii][:],
                                                    func=AF.Square))
            qtg[ii] = mk2
            p.wait_chain("scalar", "tensor", mk2)
            p.wait_chain("vector", "tensor", red_done)
            tok2 = p.t(lambda e, ii=ii: e.matmul(pNs[1], hsel_sb[:], sq2[ii][:],
                                                 start=True, stop=True))
            sqg[ii] = tok2
            p.wait("vector", "mm", tok2)
            p.v(lambda e: e.reduce_max(nmax_sb[:, 16:17], pNs[1], axis=AX.X))
            mv = p.v(lambda e, fc=fc: e.tensor_tensor(
                out=nmax_sb[:, fc:fc + 1], in0=nmax_sb[:, fc:fc + 1],
                in1=nmax_sb[:, 16:17], op=ALU.max))
            red_done = mv

    for fc in range(8):
        for tt in range(4):
            ii = (fc * 4 + tt) % 2
            bi, pb = banks.next()

            def mm(e, fc=fc, tt=tt, pb=pb):
                last = None
                for ic in range(8):
                    last = e.matmul(pb, wk_sb[:, (fc * 8 + ic) * 128:(fc * 8 + ic + 1) * 128],
                                    x3_sb[:, ic * TPC + tt * 512: ic * TPC + (tt + 1) * 512],
                                    start=(ic == 0), stop=(ic == 7))
                return last
            tok = p.t(mm)
            p.wait("scalar", "mm", tok)
            kbuf = kf_out.bufs[kf_out.k % kf_out.nbuf]
            kf_out.put(lambda e, buf, pb=pb, fc=fc: e.activation(
                out=buf[:], in_=pb, func=AF.Identity, bias=bk_sb[:, fc:fc + 1]),
                o_kf[:, tt * 4:(tt + 1) * 4, fc, :])
            banks.set_guard(bi, "scalar", p.chain_mark("scalar"))
            p.wait("scalar", "mm", sqg[ii])
            mk2 = p.s(lambda e, ii=ii, kbuf=kbuf: e.activation(
                out=sq2[ii][:], in_=kbuf[:], func=AF.Square))
            p.wait_chain("scalar", "tensor", mk2)
            p.wait_chain("vector", "tensor", red_done)
            tok2 = p.t(lambda e, ii=ii: e.matmul(pNs[1], hsel_sb[:], sq2[ii][:],
                                                 start=True, stop=True))
            sqg[ii] = tok2
            p.wait("vector", "mm", tok2)
            p.v(lambda e: e.reduce_max(nmax_sb[:, 16:17], pNs[1], axis=AX.X))
            mv = p.v(lambda e, fc=fc: e.tensor_tensor(
                out=nmax_sb[:, 8 + fc:9 + fc], in0=nmax_sb[:, 8 + fc:9 + fc],
                in1=nmax_sb[:, 16:17], op=ALU.max))
            red_done = mv
    nm0 = p.chain_mark("vector")
    p.wait_chain("vector", "scalar", nm0)
    nm = p.chain_mark("vector")
    p.wait_chain("vector", "gpsimd", nm)
    ntok = p.dma("gpsimd", o_norms, nmax_sb[0:2, 0:16], sem_name="dst_nm")

    va_guard = [0, 0]
    va_uses = [0, 0]
    for tc in range(16):
        i = tc % 2
        vb = vaug_sb[i]
        p.wait("vector", f"dst_va_{i}", va_uses[i] * 16)
        msk = p.v(lambda e, vb=vb: e.memset(vb[:], 1.0))
        p.wait_chain("vector", "scalar", msk)
        for oh in range(2):
            bi, pb = banks.next()

            def mm(e, tc=tc, oh=oh, pb=pb):
                for ic in range(8):
                    e.matmul(pb, x3_sb[:, ic * TPC + tc * 128: ic * TPC + (tc + 1) * 128],
                             wvp_sb[:, ic * D + oh * 512: ic * D + (oh + 1) * 512],
                             start=(ic == 0), stop=False)
                return e.matmul(pb, ones_sb[:], bvr_sb[:, oh * 512:(oh + 1) * 512],
                                start=False, stop=True)
            tok = p.t(mm)
            p.wait("scalar", "mm", tok)
            mk = p.s(lambda e, pb=pb, oh=oh, vb=vb: e.copy(
                out=vb[:].rearrange("p (h d) -> p h d", h=NH)[:, oh * 8:(oh + 1) * 8, 0:DH],
                in_=pb.rearrange("p (h d) -> p h d", h=8)))
            banks.set_guard(bi, "scalar", mk)
        vamark = p.chain_mark("scalar")
        p.wait_chain("scalar", "gpsimd", vamark)
        p.dma("gpsimd", o_vaug[:, tc, :, :], vb[:], sem_name=f"dst_va_{i}")
        va_uses[i] += 1

    qt_out.flush_wait("gpsimd")
    kf_out.flush_wait("gpsimd")
    p.wait("gpsimd", "dst_nm", ntok)
    p.wait("gpsimd", "dst_va_0", va_uses[0] * 16)
    p.wait("gpsimd", "dst_va_1", va_uses[1] * 16)
    return p.finalize()


def build_flash():
    p = P(npsum=0)
    NRB = 4
    qt = p.din("qt", [128, 8, TPC], BF16)
    kf_all = p.din("kf_all", [NRB, 128, 16, 8, 128], BF16)
    va_all = p.din("va_all", [NRB, 128, 16, NH, DH + 1], BF16)
    norms_all = p.din("norms_all", [NRB, 32], F32)
    x3 = p.din("x3", [128, 8, TPC], F32)
    wrsT = p.din("wrsT", [128, 8, 17], F32)
    brs = p.din("brs", [17, 1], F32)
    wp = p.din("wp", [128, 8, 8, 128], F32)
    bp = p.din("bp", [128, 8], F32)
    id128 = p.din("id128", [128, 128], F32)
    ones2 = p.din("ones2", [128, 128], F32)

    o_out = p.dout("o_out", [128, 8, TPC], F32)

    pscore = [p.ps("pscA", [128, 1024], F32), p.ps("pscB", [128, 1024], F32)]
    pavt = [p.ps("pav0", [128, 512], F32), p.ps("pav1", [128, 512], F32)]
    pmisc = [p.ps("pm0", [128, 512], F32), p.ps("pm1", [128, 512], F32)]

    qt_sb = p.sb("qt_sb", [128, 8 * TPC], BF16)
    h_sb = p.sb("h_sb", [128, 8 * TPC], BF16)
    acc = [p.sb(f"acc{i}", [65, TPC], F32) for i in range(2)]
    expP = [p.sb(f"expP{i}", [128, 1024], BF16) for i in range(4)]
    wp_sb = p.sb("wp_sb", [128, 64 * 128], BF16)
    wrs_sb = p.sb("wrs_sb", [128, 8 * 17], BF16)
    brs_sb = p.sb("brs_sb", [17, 1], F32)
    bp_sb = p.sb("bp_sb", [128, 8], F32)
    id_sb = p.sb("id_sb", [128, 128], F32)
    idb_sb = p.sb("idb_sb", [64, 64], BF16)
    nrm_sb = p.sb("nrm_sb", [1, 6 * 32], F32)
    mneg_sb = p.sb("mneg_sb", [128, NH], F32)
    mrow_r = p.sb("mrow_r", [1, NH], F32R)
    ones512r = p.sb("ones512r", [128, 512], F32R)
    tmp_bf = p.sb("tmp_bf", [64, 512], BF16)
    recr = p.sb("recr", [65, 1024], F32R)
    glog_sb = p.sb("glog_sb", [17, 512], F32)
    gtok_sb = p.sb("gtok_sb", [128, 68], F32)
    gwork_sb = p.sb("gwork_sb", [128, 64], F32)
    gwork2_sb = p.sb("gwork2_sb", [128, 64], F32)
    colw = p.sb("colw", [128, 32], F32)
    grow_r = p.sb("grow_r", [1, 512], F32R)
    gb_sb = p.sb("gb_sb", [128, 512], F32)
    r4_sb = p.sb("r4_sb", [128, 8 * 512], F32)
    pr_sb = p.sb("pr_sb", [128, 512], F32)
    ln = dict(
        ones_col_r=p.sb("onescol", [128, 1], F32R),
        ones_row_r=p.sb("onesrow", [1, 128], F32R),
        ones_col_f=p.sb("onescolf", [128, 1], F32),
        sq_r=p.sb("sq_r", [128, 2 * 512], F32R),
        rows=p.sb("rows", [1, 4 * 512], F32),
        rrow=p.sb("rrow", [1, 2 * 512], F32R),
        tmp32=p.sb("tmp32", [128, 512], F32),
    )
    kvs = InStream(p, "kvs", [128, 16 * 128 + 16 * 2 * 65], BF16, nbuf=3, q="gpsimd")
    x3s = InStream(p, "x3s", [128, 512], F32, nbuf=2, q="sync")
    out_out = OutStream(p, "oo", [128, 512], F32, nbuf=2, q="scalar")
    KOFF = 16 * 128

    lv = p.dma("gpsimd", ln["ones_col_r"][:], ones2[:, 0:1],
               allow_slow_non_contiguous=True, sem_name="ldsm")
    lv = p.dma("gpsimd", ln["ones_row_r"][:], ones2[0:1, :], sem_name="ldsm")
    lv = p.dma("gpsimd", idb_sb[:], id128[0:64, 0:64], sem_name="ldsm")
    for q4 in range(4):
        lv = p.dma("gpsimd", ones512r[:, q4 * 128:(q4 + 1) * 128], ones2,
                   sem_name="ldsm")
    p.dma("gpsimd", qt_sb[:, 0:TPC], flat(qt)[:, 0:TPC], sem_name="qseg0")
    ls = p.dma("sync", brs_sb[:], brs, sem_name="lds")
    ls = p.dma("sync", bp_sb[:], bp, sem_name="lds")
    ls = p.dma("sync", id_sb[:], id128, sem_name="lds")
    ls = p.dma("sync", nrm_sb[0:1, 0:128], norms_all.rearrange("r c -> (r c)").unsqueeze(0),
               sem_name="lds")
    ls = p.dma("sync", ln["ones_col_f"][:], ones2[:, 0:1], sem_name="lds",
               allow_slow_non_contiguous=True)
    for eng in ("tensor", "vector", "scalar"):
        p.wait(eng, "ldsm", lv)
        p.wait(eng, "lds", ls)

    p.v(lambda e: e.tensor_tensor(out=nrm_sb[0:1, 128:144], in0=nrm_sb[0:1, 48:64],
                                  in1=nrm_sb[0:1, 80:96], op=ALU.max))
    p.v(lambda e: e.tensor_tensor(out=nrm_sb[0:1, 128:144], in0=nrm_sb[0:1, 128:144],
                                  in1=nrm_sb[0:1, 112:128], op=ALU.max))
    mk = p.v(lambda e: e.tensor_tensor(out=nrm_sb[0:1, 128:144], in0=nrm_sb[0:1, 128:144],
                                       in1=nrm_sb[0:1, 16:32], op=ALU.max))
    p.wait_chain("vector", "scalar", mk)
    p.s(lambda e: e.activation(out=nrm_sb[0:1, 160:176], in_=nrm_sb[0:1, 0:16],
                               func=AF.Sqrt))
    mk = p.s(lambda e: e.activation(out=nrm_sb[0:1, 128:144], in_=nrm_sb[0:1, 128:144],
                                    func=AF.Sqrt))
    p.wait_chain("scalar", "vector", mk)
    mv = p.v(lambda e: e.tensor_mul(nrm_sb[0:1, 160:176], nrm_sb[0:1, 160:176],
                                    nrm_sb[0:1, 128:144]))
    p.wait_chain("vector", "scalar", mv)
    mk = p.s(lambda e: e.activation(out=nrm_sb[0:1, 160:176], in_=nrm_sb[0:1, 160:176],
                                    func=AF.Identity, scale=np.float32(0.125),
                                    bias=np.float32(2.0)))
    p.wait_chain("scalar", "vector", mk)
    pairs = nrm_sb[0:1, 160:176].rearrange("a (h two) -> a h two", two=2)
    mv = p.v(lambda e: e.tensor_tensor(out=nrm_sb[0:1, 144:152], in0=pairs[:, :, 0],
                                       in1=pairs[:, :, 1], op=ALU.max))
    p.v(lambda e: e.tensor_copy(pairs[:, :, 0], nrm_sb[0:1, 144:152]))
    p.v(lambda e: e.tensor_copy(pairs[:, :, 1], nrm_sb[0:1, 144:152]))
    mv = p.v(lambda e: e.tensor_scalar_mul(nrm_sb[0:1, 160:176], nrm_sb[0:1, 160:176], -1.0))
    p.wait_chain("vector", "scalar", mv)
    mk = p.s(lambda e: e.copy(out=mrow_r[:], in_=nrm_sb[0:1, 160:176]))
    p.wait_chain("scalar", "tensor", mk)
    tok = p.t(lambda e: e.matmul(pmisc[0][:, 0:NH], ln["ones_row_r"][:], mrow_r[:],
                                 start=True, stop=True))
    p.wait("scalar", "mm", tok)
    mk = p.s(lambda e: e.copy(out=mneg_sb[:], in_=pmisc[0][:, 0:NH]))
    p.wait_chain("scalar", "tensor", mk)
    p.wait_chain("scalar", "vector", mk)

    for hl in range(2):
        p.v(lambda e, hl=hl: e.memset(acc[hl][:], 0.0))
    steps = [(fc, rb, qt4, c) for fc in range(8) for rb in range(NRB)
             for qt4 in range(4) for c in range(16)]
    NST = len(steps)
    extra = []
    set_guard = [0, 0]
    exp_mark = {}
    av_tok = {}
    accv_of_run = {}
    kv_state = {}
    norm_state = dict(pm0=("scalar", mk), pm1=None, recr_tok=[0, 0],
                      slot_idx=0, tmpbf_tok=0, hmark=0)
    wtok = {}
    extra.append(lambda: wtok.__setitem__(
        "wrs", p.dma("gpsimd", wrs_sb[:], flat(wrsT), sem_name="ldw")))
    extra.append(lambda: wtok.__setitem__(
        "wp", p.dma("gpsimd", wp_sb[:], flat(wp), sem_name="ldw")))

    def slot_closures(fc, hl, q4):
        st = norm_state
        si = st["slot_idx"]
        st["slot_idx"] += 1
        par = si % 2
        rsl = recr[64:65, par * 512:(par + 1) * 512]
        ts = slice(q4 * 512, (q4 + 1) * 512)
        cols = slice(fc * TPC + q4 * 512, fc * TPC + (q4 + 1) * 512)
        hold = {}

        def A():
            p.wait("vector", "mm", st["recr_tok"][par])

            def f(e):
                with p.nc.allow_low_precision(reason="f32r denom broadcast"):
                    return e.reciprocal(rsl, acc[hl][64:65, ts])
            hold["mv"] = p.v(f)

        def Bc():
            p.wait_chain("vector", "tensor", hold["mv"])
            if st["pm0"]:
                p.wait_chain(st["pm0"][0], "tensor", st["pm0"][1])
            tok = p.t(lambda e: e.matmul(pmisc[0][0:64, 0:512],
                                         ones512r[64:65, 0:64], rsl,
                                         start=True, stop=True))
            st["recr_tok"][par] = tok
            p.wait("vector", "mm", tok)
            if hl == 0:
                mv = p.v(lambda e: e.tensor_tensor(
                    out=h_sb[0:64, cols], in0=acc[0][0:64, ts],
                    in1=pmisc[0][0:64, 0:512], op=ALU.mult))
            else:
                p.wait("vector", "mm", st["tmpbf_tok"])
                mv = p.v(lambda e: e.tensor_tensor(
                    out=tmp_bf[:], in0=acc[1][0:64, ts],
                    in1=pmisc[0][0:64, 0:512], op=ALU.mult))
                hold["mvb"] = mv
            st["pm0"] = ("vector", mv)
            st["hmark"] = mv

        def Cc():
            p.wait_chain("vector", "tensor", hold["mvb"])
            if st["pm1"]:
                p.wait_chain(st["pm1"][0], "tensor", st["pm1"][1])
            tok2 = p.t(lambda e: e.matmul(pmisc[1][64:128, 0:512], idb_sb[:],
                                          tmp_bf[:], start=True, stop=True,
                                          tile_position=(0, 64)))
            st["tmpbf_tok"] = tok2
            p.wait("vector", "mm", tok2)
            mc = p.v(lambda e: e.tensor_copy(h_sb[64:128, cols],
                                             pmisc[1][64:128, 0:512]))
            st["pm1"] = ("vector", mc)
            st["hmark"] = mc
        return [A, Bc] + ([Cc] if hl == 1 else [])

    def norm_closures(fc, final):
        cls = []
        order = ([(hl, q4) for hl in range(2) for q4 in range(4)] if not final
                 else [(hl, q4) for q4 in range(4) for hl in range(2)])
        for (hl, q4) in order:
            cls.extend(slot_closures(fc, hl, q4))
        if not final:
            def zz():
                for hl in range(2):
                    p.v(lambda e, hl=hl: e.memset(acc[hl][:], 0.0))
            cls.append(zz)
        return cls

    def after_run(s):
        fc, rb, qt4, _ = steps[s]
        r = s // 16
        for hl in range(2):
            p.wait("vector", "mm", av_tok[s] - 1 + hl)
            p.v(lambda e, hl=hl, qt4=qt4: e.tensor_tensor(
                out=acc[hl][:, qt4 * 512:(qt4 + 1) * 512],
                in0=acc[hl][:, qt4 * 512:(qt4 + 1) * 512],
                in1=pavt[hl][:][0:65, 0:512], op=ALU.add))
        accv_of_run[r] = p.chain_mark("vector")
        if qt4 == 3:
            kvs.release(kv_state[(fc, rb)][0], "mm", av_tok[s])
            if rb == NRB - 1 and fc < 7:
                extra.extend(norm_closures(fc, final=False))

    def emit_mmav(s):
        fc, rb, qt4, c = steps[s]
        kvb = kv_state[(fc, rb)][1]
        sp = s % 4
        p.wait_chain("scalar", "tensor", exp_mark[s])
        if c == 0 and s // 16 >= 1:
            p.wait_chain("vector", "tensor", accv_of_run[s // 16 - 1])

        for j in range(2):
            def f(e, kvb=kvb, c=c, sp=sp, j=j):
                vsl = kvb[:, KOFF + (c * 2 + j) * 65: KOFF + (c * 2 + j + 1) * 65]
                return e.matmul(pavt[j][0:65, 0:512], vsl,
                                expP[sp][:, j * 512:(j + 1) * 512],
                                start=(c == 0), stop=(c == 15))
            av_tok[s] = p.t(f)
        if c == 15:
            after_run(s)

    sc_tok = {}

    def emit_mmsc(s):
        fc, rb, qt4, c = steps[s]
        if qt4 == 0 and c == 0:
            ki, kvb, _ = kvs.load(kf_all[rb, :, :, fc, :], part=slice(0, KOFF))
            vsem, vtok = kvs.extra(ki, va_all[rb, :, :, 2 * fc:2 * fc + 2, :],
                                   part=slice(KOFF, KOFF + 16 * 2 * 65))
            kv_state[(fc, rb)] = (ki, kvb, vsem, vtok)
            if rb == 0:
                p.wait("tensor", f"qseg{fc}", 16)
            p.wait("tensor", vsem, vtok)
            if rb == NRB - 1 and fc < 7:
                p.dma("gpsimd", qt_sb[:, (fc + 1) * TPC:(fc + 2) * TPC],
                      flat(qt)[:, (fc + 1) * TPC:(fc + 2) * TPC],
                      sem_name=f"qseg{fc + 1}")
        kvb = kv_state[(fc, rb)][1]
        sidx = s % 2
        p.wait_chain("scalar", "tensor", set_guard[sidx])

        def mmsc(e, fc=fc, c=c, qt4=qt4, kvb=kvb, sidx=sidx):
            last = None
            for j in range(2):
                ksl = kvb[j * 64:j * 64 + 64, c * 128:(c + 1) * 128]
                qs = qt_sb[j * 64:j * 64 + 64,
                           fc * TPC + qt4 * 512: fc * TPC + (qt4 + 1) * 512]
                last = e.matmul(pscore[sidx][:, j * 512:(j + 1) * 512],
                                ksl, qs, start=True, stop=True)
            return last
        sc_tok[s] = p.t(mmsc)

    emit_mmsc(0)
    for s in range(NST):
        fc = steps[s][0]
        sidx = s % 2
        p.wait("scalar", "mm", sc_tok[s])
        expfn = (lambda e, sidx=sidx, sp=s % 4, fc=fc: e.activation(
            out=expP[sp][:], in_=pscore[sidx][:],
            func=AF.Exp, scale=np.float32(0.125),
            bias=mneg_sb[:, 2 * fc:2 * fc + 1]))
        mexp = p.s(expfn) if s < 2 else p.s_nc(expfn)
        set_guard[sidx] = mexp
        exp_mark[s] = mexp
        if s + 1 < NST:
            emit_mmsc(s + 1)
        if s > 0:
            emit_mmav(s - 1)
        if extra:
            extra.pop(0)()
    emit_mmav(NST - 1)
    fin = norm_closures(7, final=True)

    wp_banks = Banks(p, [pscore[0][:, 0:512], pscore[0][:, 512:1024],
                         pscore[1][:, 0:512]])
    for i in range(3):
        wp_banks.set_guard(i, "scalar", exp_mark[NST - 1])
    pGb = pscore[1][:, 512:1024]
    g3 = gtok_sb[:].rearrange("p (a b) -> p a b", b=17)[:, :, 0:15]
    w3 = gwork_sb[:].rearrange("p (a b) -> p a b", b=16)[:, :, 0:15]
    w3b = gwork2_sb[:].rearrange("p (a b) -> p a b", b=16)[:, :, 0:15]
    s3v = gtok_sb[:].rearrange("p (a b) -> p a b", b=17)[:, :, 15:17]

    def cw2(i):
        return colw[:, 4 * i:4 * (i + 1)]

    def cw1(i):
        return colw[:, 4 * i:4 * (i + 1)].unsqueeze(2)

    def cwb(i, n):
        return colw[:, 4 * i:4 * (i + 1)].unsqueeze(2).broadcast_to((128, 4, n))

    def emit_tail(tt4):
        sc0 = p.chain_mark("scalar")
        p.wait_chain("vector", "tensor", norm_state["hmark"])
        p.wait("tensor", "ldw", 32)
        for g_ in (norm_state["pm0"], norm_state["pm1"]):
            if g_:
                p.wait_chain(g_[0], "tensor", g_[1])

        def mmg(e, tt4=tt4):
            last = None
            for ic in range(8):
                last = e.matmul(pmisc[0][0:17, 0:512], wrs_sb[:, ic * 17:(ic + 1) * 17],
                                h_sb[:, ic * TPC + tt4 * 512: ic * TPC + (tt4 + 1) * 512],
                                start=(ic == 0), stop=(ic == 7))
            return last
        tokg = p.t(mmg)
        p.wait("vector", "mm", tokg)
        m2 = p.v(lambda e: e.tensor_single_scalar(out=glog_sb[:], in_=pmisc[0][0:17, 0:512],
                                                  scalar=brs_sb[:, 0:1], op=ALU.add))
        norm_state["pm0"] = ("vector", m2)
        p.wait_chain("vector", "tensor", m2)

        def mtrs(e):
            last = None
            for sub in range(4):
                ss = slice(sub * 128, (sub + 1) * 128)
                last = e.transpose(pavt[0][:][:, sub * 17:sub * 17 + 17], glog_sb[:, ss],
                                   id_sb[0:17, 0:17])
            return last
        tok = p.t(mtrs)
        p.wait("vector", "mm", tok)
        p.v(lambda e: e.tensor_copy(gtok_sb[:], pavt[0][:][:, 0:68]))
        p.v(lambda e: e.tensor_reduce(cw2(0), g3, axis=AX.X, op=ALU.max))
        mv = p.v(lambda e: e.tensor_tensor(out=w3, in0=g3, in1=cwb(0, 15),
                                           op=ALU.subtract))
        p.wait_chain("vector", "scalar", mv)
        mk2 = p.s(lambda e: e.activation(out=w3, in_=w3, func=AF.Exp))
        p.wait_chain("scalar", "vector", mk2)
        p.v(lambda e: e.tensor_reduce(cw2(1), w3, axis=AX.X, op=ALU.add))
        p.v(lambda e: e.reciprocal(cw2(1), cw2(1)))
        p.v(lambda e: e.tensor_tensor(out=g3, in0=w3, in1=cwb(1, 15), op=ALU.mult))
        p.v(lambda e: e.tensor_reduce(cw2(2), g3, axis=AX.X, op=ALU.max))
        p.v(lambda e: e.tensor_tensor(out=w3, in0=g3, in1=cwb(2, 15), op=ALU.is_ge))
        p.v(lambda e: e.scalar_tensor_tensor(out=w3, in0=w3, scalar=-2.0, in1=g3,
                                             op0=ALU.mult, op1=ALU.add))
        p.v(lambda e: e.tensor_reduce(cw2(2), w3, axis=AX.X, op=ALU.max))
        p.v(lambda e: e.tensor_tensor(out=w3b, in0=w3, in1=cwb(2, 15), op=ALU.is_ge))
        p.v(lambda e: e.scalar_tensor_tensor(out=w3, in0=w3b, scalar=-2.0, in1=w3,
                                             op0=ALU.mult, op1=ALU.add))
        p.v(lambda e: e.tensor_reduce(cw2(2), w3, axis=AX.X, op=ALU.max))
        p.v(lambda e: e.tensor_tensor(out=w3, in0=g3, in1=cwb(2, 15), op=ALU.is_ge))
        p.v(lambda e: e.tensor_mul(w3, w3, g3))
        p.v(lambda e: e.tensor_reduce(cw2(3), w3, axis=AX.X, op=ALU.add))
        p.v(lambda e: e.tensor_reduce(cw2(4), s3v, axis=AX.X, op=ALU.max))
        mv = p.v(lambda e: e.tensor_tensor(out=s3v, in0=s3v, in1=cwb(4, 2),
                                           op=ALU.subtract))
        p.wait_chain("vector", "scalar", mv)
        mk2 = p.s(lambda e: e.activation(out=s3v, in_=s3v, func=AF.Exp))
        p.wait_chain("scalar", "vector", mk2)
        p.v(lambda e: e.tensor_reduce(cw2(5), s3v, axis=AX.X, op=ALU.add))
        p.v(lambda e: e.reciprocal(cw2(5), cw2(5)))
        p.v(lambda e: e.tensor_tensor(out=cw1(6), in0=s3v[:, :, 1:2], in1=cw1(3),
                                      op=ALU.mult))
        p.v(lambda e: e.tensor_scalar_mul(cw2(6), cw2(6), 6.0))
        p.v(lambda e: e.scalar_tensor_tensor(out=cw1(6), in0=s3v[:, :, 0:1],
                                             scalar=2.0, in1=cw1(6),
                                             op0=ALU.mult, op1=ALU.add))
        mv = p.v(lambda e: e.tensor_tensor(out=cw1(6), in0=cw1(6), in1=cw1(5),
                                           op=ALU.mult))
        p.wait_chain("vector", "tensor", mv)

        def mtr2(e):
            last = None
            for sub in range(4):
                last = e.transpose(pavt[1][:][0:1, sub * 128:(sub + 1) * 128],
                                   colw[:, 24 + sub:25 + sub], id_sb[:])
            return last
        tok = p.t(mtr2)
        p.wait("scalar", "mm", tok)
        mk3 = p.s(lambda e: e.copy(out=grow_r[:], in_=pavt[1][:][0:1, 0:512]))
        p.wait_chain("scalar", "tensor", mk3)
        gb_tok = p.t(lambda e: e.matmul(pGb, ln["ones_row_r"][:], grow_r[:],
                                        start=True, stop=True))
        p.wait("vector", "mm", gb_tok)
        p.v(lambda e: e.tensor_copy(gb_sb[:], pGb))
        p.wait_chain("scalar", "vector", sc0)
        for oc in range(8):
            bi, pb = wp_banks.next()

            def mmp(e, oc=oc, tt4=tt4, pb=pb):
                last = None
                for ic in range(8):
                    last = e.matmul(pb, wp_sb[:, (oc * 8 + ic) * 128:(oc * 8 + ic + 1) * 128],
                                    h_sb[:, ic * TPC + tt4 * 512: ic * TPC + (tt4 + 1) * 512],
                                    start=(ic == 0), stop=(ic == 7))
                return last
            tok = p.t(mmp)
            sl34 = slice(oc * TPC + tt4 * 512, oc * TPC + (tt4 + 1) * 512)
            xi, xbuf, (xsem, xtok) = x3s.load(flat(x3)[:, sl34])
            p.wait("vector", "mm", tok)
            mv1 = p.v(lambda e, pb=pb: e.tensor_tensor(out=pr_sb[:], in0=pb,
                                                       in1=gb_sb[:], op=ALU.mult))
            wp_banks.set_guard(bi, "vector", mv1)
            p.wait("vector", xsem, xtok)
            mv2 = p.v(lambda e, oc=oc, xbuf=xbuf: e.scalar_tensor_tensor(
                out=r4_sb[:, oc * 512:(oc + 1) * 512], in0=pr_sb[:],
                scalar=bp_sb[:, oc:oc + 1], in1=xbuf[:], op0=ALU.add, op1=ALU.add))
            x3s.release(xi, "chain_vector", mv2)
        vm = p.chain_mark("vector")
        p.wait_chain("vector", "tensor", vm)
        p.wait_chain("vector", "scalar", vm)
        emit_layernorm(p, r4_sb, ln, None, None, None, out_out,
                       nchunk=8, tpc=512,
                       ptiles=(pmisc[0][:], pmisc[1][:], pavt[0][:], pavt[1][:]),
                       out_fn=lambda c, tt, tt4=tt4: (None, flat(o_out)[
                           :, c * TPC + tt4 * 512: c * TPC + (tt4 + 1) * 512]))
        vm2 = p.chain_mark("vector")
        norm_state["pm0"] = ("vector", vm2)
        norm_state["pm1"] = ("vector", vm2)

    idx = 0
    for tt4 in range(4):
        for _ in range(5):
            fin[idx]()
            idx += 1
        emit_tail(tt4)
    out_out.flush_wait("gpsimd")
    return p.finalize()


_NC_CACHE = {}


def _get(name, builder):
    if name not in _NC_CACHE:
        _NC_CACHE[name] = builder()
    return _NC_CACHE[name]


def _ones2():
    return np.ones((128, 128), dtype=np.float32)


def run_l0_attn(inputs, xpm):
    nc1 = _get("l1", build_l0attn_pre)
    im1 = [dict(x_in=xpm[c],
                wq=_wblk(inputs["L0_attn_wq_w"]), wk=_wblk(inputs["L0_attn_wk_w"]),
                wkp=_wplain(inputs["L0_attn_wk_w"]), wvp=_wplain(inputs["L0_attn_wv_w"]),
                bq=_pcol(inputs["L0_attn_wq_b"]), bk=_pcol(inputs["L0_attn_wk_b"]),
                bk_row=_prow(inputs["L0_attn_wk_b"]), bv_row=_prow(inputs["L0_attn_wv_b"]))
           for c in range(NCORE)]
    r1 = run_bass_kernel_spmd(nc1, im1, core_ids=list(range(NCORE))).results

    nc2 = _get("l2", build_l0attn_main)
    st = [np.asarray(r1[c]["o_st"]) for c in range(NCORE)]
    zero_st = np.zeros_like(st[0])
    tri = np.triu(np.ones((128, 128), dtype=np.float32))
    ident = np.eye(64, dtype=np.float32)
    woT = _wblk(inputs["L0_attn_wo_w"])
    bo = _pcol(inputs["L0_attn_wo_b"])
    im2 = []
    for c in range(NCORE):
        b, k = divmod(c, 4)
        preds = [st[b * 4 + kk] for kk in range(k)]
        while len(preds) < 3:
            preds.append(zero_st)
        sp = np.stack(preds, axis=1)
        s_pred = np.ascontiguousarray(np.concatenate([sp, sp], axis=0))
        im2.append(dict(x_in=xpm[c], qf=r1[c]["o_qf"], kf=r1[c]["o_kf"],
                        kt=r1[c]["o_kt"], vt=r1[c]["o_vt"],
                        s_pred=s_pred, wo=woT, bo=bo,
                        trimask=tri, ident=ident, ones2=_ones2()))
    r2 = run_bass_kernel_spmd(nc2, im2, core_ids=list(range(NCORE))).results
    return r2


def kernel(**inputs):
    x = np.asarray(inputs["L0_attn_x"], dtype=np.float32)
    xpm = []
    for c in range(NCORE):
        b, k = divmod(c, 4)
        xpm.append(_pm_feat(x[b, k * TPC:(k + 1) * TPC, :]))

    r2 = run_l0_attn(inputs, xpm)

    nc3 = _get("l3", build_l0ffn)
    im3 = [dict(x1=np.asarray(r2[c]["o_x1"]), x1bf=np.asarray(r2[c]["o_x1bf"]),
                w1=_wblk(inputs["L0_ffn_w1_w"]), w2=_wblk(inputs["L0_ffn_w2_w"]),
                h1=_wblk(inputs["L0_ffn_h1_w"]), h2=_wblk(inputs["L0_ffn_h2_w"]),
                w3=_wblk(inputs["L0_ffn_w3_w"]),
                s1=_pcol(inputs["L0_ffn_w1_s"]), s2=_pcol(inputs["L0_ffn_w2_s"]),
                h1s=_pcol(inputs["L0_ffn_h1_s"]), h2s=_pcol(inputs["L0_ffn_h2_s"]),
                s3=_pcol(inputs["L0_ffn_w3_s"]), ones2=_ones2())
           for c in range(NCORE)]
    r3 = run_bass_kernel_spmd(nc3, im3, core_ids=list(range(NCORE))).results

    nc4 = _get("l4", build_l1front)
    dw = inputs["L1_attn_dw_kernel"]
    dwk_pm = np.ascontiguousarray(np.stack([_pcol(dw[j]) for j in range(4)], axis=1))
    zero3 = np.zeros((128, 8, 3), dtype=BFD)
    im4 = []
    for c in range(NCORE):
        b, k = divmod(c, 4)
        if k == 0:
            xprev = zero3
        else:
            xprev = np.ascontiguousarray(np.asarray(r3[b * 4 + k - 1]["o_x2bf"])[:, :, TPC - 3:TPC])
        im4.append(dict(x2bf=np.asarray(r3[c]["o_x2bf"]), xprev=xprev,
                        w_in=_wblkT(inputs["L1_attn_W_in"]), dwk=dwk_pm,
                        pw=_wblkT(inputs["L1_attn_pw_kernel"]),
                        cbias=_pcol(inputs["L1_attn_conv_bias"]),
                        w_x=_wplainT(inputs["L1_attn_W_x"]),
                        w_dt=np.ascontiguousarray(inputs["L1_attn_W_dt"], dtype=np.float32),
                        bdt=_pcol(inputs["L1_attn_b_dt"]),
                        ident=np.eye(128, dtype=np.float32)))
    r4 = run_bass_kernel_spmd(nc4, im4, core_ids=list(range(NCORE))).results

    nc5 = _get("l5", build_scan)
    alog = np.asarray(inputs["L1_attn_A_log"], dtype=np.float32)
    im5 = []
    for s in range(NCORE):
        dd = np.empty((128, 2, 2, L), dtype=np.float32)
        xx = np.empty((128, 2, 2, L), dtype=BFD)
        for dci in range(2):
            oc = 2 * s + dci
            for b in range(2):
                for k in range(4):
                    dd[:, dci, b, k * TPC:(k + 1) * TPC] = np.asarray(r4[b * 4 + k]["o_delta"])[:, oc, :]
                    xx[:, dci, b, k * TPC:(k + 1) * TPC] = np.asarray(r4[b * 4 + k]["o_xc"])[:, oc, :]
        bm = np.empty((2, NSTATE, L), dtype=np.float32)
        cm = np.empty((2, NSTATE, L), dtype=np.float32)
        for b in range(2):
            for k in range(4):
                bc = np.asarray(r4[b * 4 + k]["o_bc"])
                bm[b, :, k * TPC:(k + 1) * TPC] = bc[0:8]
                cm[b, :, k * TPC:(k + 1) * TPC] = bc[8:16]
        al = np.ascontiguousarray(
            alog[s * 256:(s + 1) * 256].reshape(2, 128, NSTATE).transpose(1, 0, 2))
        im5.append(dict(delta_in=np.ascontiguousarray(dd), xc_in=np.ascontiguousarray(xx),
                        bmat=bm, cmat=cm, alog=al))
    r5 = run_bass_kernel_spmd(nc5, im5, core_ids=list(range(NCORE))).results

    nc6 = _get("l6a", build_l1back)
    im6 = []
    for c in range(NCORE):
        b, k = divmod(c, 4)
        yy = np.empty((128, 16, TPC), dtype=BFD)
        for oc in range(16):
            yy[:, oc, :] = np.asarray(r5[oc // 2]["o_y"])[:, oc % 2, b, k * TPC:(k + 1) * TPC]
        im6.append(dict(y_in=np.ascontiguousarray(yy), xc_in=np.asarray(r4[c]["o_xc"]),
                        z_in=np.asarray(r4[c]["o_z"]), dvec=_pcol(inputs["L1_attn_D"]),
                        w_out=_wblkT(inputs["L1_attn_W_out"]),
                        x2=np.asarray(r3[c]["o_x2"]), ones2=_ones2()))
    r6 = run_bass_kernel_spmd(nc6, im6, core_ids=list(range(NCORE))).results

    nc6b = _get("l6b", build_qkv)
    esel = np.zeros((16, 128), dtype=np.float32)
    for h in range(16):
        esel[h, (h % 2) * 64:(h % 2) * 64 + 64] = 1.0
    hmask = np.zeros((16, 8), dtype=np.float32)
    for h in range(16):
        hmask[h, h // 2] = 1.0
    jsel = np.zeros((2, 128), dtype=np.float32)
    jsel[0, 0:64] = 1.0
    jsel[1, 64:128] = 1.0
    temp = np.ascontiguousarray(np.asarray(inputs["L1_ffn_temperature"],
                                           dtype=np.float32).reshape(1, 16))
    qe_pcol = _pcol(np.asarray(inputs["L1_ffn_query_embedding"], dtype=np.float32).reshape(1024))
    im6b = [dict(x3bf=np.asarray(r6[c]["o_x3bf"]),
                 wq=_wblk(inputs["L1_ffn_Wq"]), wk=_wblk(inputs["L1_ffn_Wk"]),
                 wvp=_wplain(inputs["L1_ffn_Wv"]),
                 bq=_pcol(inputs["L1_ffn_bq"]), bk=_pcol(inputs["L1_ffn_bk"]),
                 bv_row=_prow(inputs["L1_ffn_bv"]), temp_in=temp, qe_pcol=qe_pcol,
                 esel=esel, hmask=hmask, jsel=jsel,
                 hsel=np.ascontiguousarray(jsel.T), ones2=_ones2())
            for c in range(NCORE)]
    r6b = run_bass_kernel_spmd(nc6b, im6b, core_ids=list(range(NCORE))).results

    nc7 = _get("l7", build_flash)
    def headorder(nm):
        out = np.empty(32, dtype=np.float32)
        for h in range(16):
            out[h] = nm[h % 2, h // 2]
            out[16 + h] = nm[h % 2, 8 + h // 2]
        return out
    norms = [headorder(np.asarray(r6b[c]["o_norms"])) for c in range(NCORE)]
    im7 = []
    for c in range(NCORE):
        b, k = divmod(c, 4)
        grp = [b * 4 + kk for kk in range(4)]
        kf_all = np.ascontiguousarray(np.stack([np.asarray(r6b[g]["o_kf"]) for g in grp]))
        va_all = np.ascontiguousarray(np.stack([np.asarray(r6b[g]["o_vaug"]) for g in grp]))
        order = [c] + [g for g in grp if g != c]
        norms_all = np.ascontiguousarray(np.stack([norms[g] for g in order]))
        im7.append(dict(qt=np.asarray(r6b[c]["o_qt"]), kf_all=kf_all, va_all=va_all,
                        norms_all=norms_all, x3=np.asarray(r6[c]["o_x3"]),
                        wrsT=np.ascontiguousarray(np.concatenate([
                            _wplainT(np.ascontiguousarray(
                                np.asarray(inputs["L1_ffn_Wr"], dtype=np.float32).T)),
                            _wplainT(np.ascontiguousarray(
                                np.asarray(inputs["L1_ffn_Ws"], dtype=np.float32).T))],
                            axis=2)),
                        brs=np.ascontiguousarray(np.concatenate([
                            np.asarray(inputs["L1_ffn_br"], dtype=np.float32).reshape(15, 1),
                            np.asarray(inputs["L1_ffn_bs"], dtype=np.float32).reshape(2, 1)])),
                        wp=_wblk(inputs["L1_ffn_Wp"]), bp=_pcol(inputs["L1_ffn_bp"]),
                        id128=np.eye(128, dtype=np.float32), ones2=_ones2()))
    r7 = run_bass_kernel_spmd(nc7, im7, core_ids=list(range(NCORE))).results

    out = np.empty((B, L, D), dtype=np.float32)
    for c in range(NCORE):
        b, k = divmod(c, 4)
        oo = np.asarray(r7[c]["o_out"])
        out[b, k * TPC:(k + 1) * TPC, :] = oo.transpose(1, 0, 2).reshape(D, TPC).T
    return out
